# version 13
# baseline (speedup 1.0000x reference)
import contextlib
import math
import numpy as np
import ml_dtypes
import concourse.bass as bass
import concourse.mybir as mybir
from concourse.bass_utils import run_bass_kernel_spmd
from concourse.alu_op_type import AluOpType

F32 = mybir.dt.float32
BF16 = mybir.dt.bfloat16
AF = mybir.ActivationFunctionType
AX = mybir.AxisListType.X
ALU = AluOpType
BF = ml_dtypes.bfloat16

B, S, DM, H, HD = 4, 2048, 2048, 16, 128
NCH = DM // 128
LCH = 16
OWN = 8
SCALE = 1.0 / math.sqrt(2048.0)
EPS = 1e-5
DFF = 1408
FCH = DFF // 128
NEXP = 18
NRT = 16
NEG = -1.0e30
F_TILES = [(0, 256), (256, 512), (512, 768), (768, 1024), (1024, 1280), (1280, 1408)]
SEMS = ("pe", "ac", "ve", "po", "dxa", "dxb", "dcc", "dwk", "dwq", "dwv", "dwp", "dw1a", "dw1b", "dw2a", "dw2b", "ds", "dta", "dtb", "dsa", "dsb")


def wplan():
    plan = []
    for h in range(H):
        if h % 2 == 0:
            plan.append((("wv", h), NCH * 256))
        plan.append((("wk", h), NCH * 128))
        plan.append((("wq", h), NCH * 128))
    for hc in range(H):
        plan.append((("wp", hc), DM))
    for e in range(NEXP):
        for ft, (f0, f1) in enumerate(F_TILES):
            plan.append((("w1", e, ft), NCH * (f1 - f0)))
        for dt in range(4):
            plan.append((("w2", e, dt), FCH * 512))
    return plan


WPLAN = wplan()
WOFFS = {}
_cur = 0
for _key, _n in WPLAN:
    WOFFS[_key] = (_cur, _n)
    _cur += 128 * _n
WTOTAL = _cur


def _lhsT_slab(W, c0, c1):
    K = W.shape[0] // 128
    return np.ascontiguousarray(
        W.reshape(K, 128, W.shape[1])[:, :, c0:c1].transpose(1, 0, 2).reshape(128, -1)
    )


def pack_wstream(inp):
    w1_all = [np.asarray(inp["L0_ffn_shared_w1"][i], np.float32) for i in range(2)] + [
        np.asarray(inp["L0_ffn_routed_w1"][i], np.float32) for i in range(NRT)]
    w2_all = [np.asarray(inp["L0_ffn_shared_w2"][i], np.float32) for i in range(2)] + [
        np.asarray(inp["L0_ffn_routed_w2"][i], np.float32) for i in range(NRT)]
    Wk = np.asarray(inp["L0_attn_Wk"], np.float32)
    Wq = np.asarray(inp["L0_attn_Wq"], np.float32)
    Wv = np.asarray(inp["L0_attn_Wv"], np.float32)
    Wp = np.asarray(inp["L0_attn_Wp"], np.float32)
    out = np.empty(WTOTAL, np.float32)
    for key, n in WPLAN:
        off, _ = WOFFS[key]
        if key[0] == "wk":
            sl = _lhsT_slab(Wk, key[1] * 128, (key[1] + 1) * 128)
        elif key[0] == "wq":
            sl = _lhsT_slab(Wq, key[1] * 128, (key[1] + 1) * 128)
        elif key[0] == "wv":
            sl = _lhsT_slab(Wv, key[1] * 128, (key[1] + 2) * 128)
        elif key[0] == "wp":
            sl = Wp[key[1] * 128:(key[1] + 1) * 128, :]
        elif key[0] == "w1":
            f0, f1 = F_TILES[key[2]]
            sl = _lhsT_slab(w1_all[key[1]], f0, f1)
        else:
            sl = _lhsT_slab(w2_all[key[1]], key[2] * 512, (key[2] + 1) * 512)
        out[off:off + 128 * n] = sl.reshape(-1)
    return out


class Plan:
    def __init__(self):
        self.cnt = {s: 0 for s in SEMS}
        self.ops = {e: [] for e in ("tensor", "scalar", "vector", "gpsimd", "sync")}
        self.waited = {e: {} for e in self.ops}

    def on(self, eng, fn, wait=(), inc=None, n=1):
        ws = []
        for t in wait:
            if t is None:
                continue
            s, v = t
            if self.waited[eng].get(s, 0) >= v:
                continue
            ws.append((s, v))
            self.waited[eng][s] = v
        tok = None
        if inc is not None:
            self.cnt[inc] += n
            tok = (inc, self.cnt[inc])
        self.ops[eng].append((ws, fn, inc, n))
        return tok

    def barrier_toks(self):
        return [(s, v) for s, v in self.cnt.items() if v > 0]


class Buf:
    def __init__(self):
        self.wrd = {}
        self.rdd = {}

    @staticmethod
    def _m(d, tok):
        if tok is None:
            return
        s, v = tok
        if d.get(s, 0) < v:
            d[s] = v

    def r(self):
        return list(self.wrd.items())

    def w(self):
        return list(self.wrd.items()) + list(self.rdd.items())

    def did_r(self, tok):
        self._m(self.rdd, tok)

    def did_w(self, tok):
        self._m(self.wrd, tok)

    def seed(self, toks):
        for t in toks:
            self._m(self.rdd, t)


def build_nc(dbg=False, phases=("p0", "p1", "p2")):
    nc = bass.Bass("TRN2", target_bir_lowering=False)

    d_x = nc.dram_tensor("xin", [LCH, 128, DM], F32, kind="ExternalInput").ap()
    d_w = nc.dram_tensor("wstream", [WTOTAL], F32, kind="ExternalInput").ap()
    d_bqt = nc.dram_tensor("bqt", [128, H], F32, kind="ExternalInput").ap()
    d_bkt = nc.dram_tensor("bkt", [128, H], F32, kind="ExternalInput").ap()
    d_bvt = nc.dram_tensor("bvt", [128, H], F32, kind="ExternalInput").ap()
    d_b1t = nc.dram_tensor("b1t", [128, NEXP, FCH], F32, kind="ExternalInput").ap()
    d_bgr = nc.dram_tensor("bgrep", [128, NRT], F32, kind="ExternalInput").ap()
    d_b2s = nc.dram_tensor("b2s", [19, DM], F32, kind="ExternalInput").ap()
    d_cen = nc.dram_tensor("cen", [NRT, DM], F32, kind="ExternalInput").ap()
    d_idf = nc.dram_tensor("identf", [128, 128], F32, kind="ExternalInput").ap()
    d_msk = nc.dram_tensor("msk8", [128, 32], F32, kind="ExternalInput").ap()
    d_dgm = nc.dram_tensor("diagm", [128, 128], F32, kind="ExternalInput").ap()
    d_y = nc.dram_tensor("y", [OWN, 128, DM], F32, kind="ExternalOutput").ap()

    def wsrc(key):
        off, n = WOFFS[key]
        return d_w[off:off + 128 * n].rearrange("(p x) -> p x", p=128)

    dbg_outs = {}
    if dbg:
        dbg_outs["dacc0"] = nc.dram_tensor("dacc0", [OWN, 128, DM], F32, kind="ExternalOutput").ap()
        dbg_outs["dacc1"] = nc.dram_tensor("dacc1", [OWN, 128, DM], F32, kind="ExternalOutput").ap()
        dbg_outs["dgat"] = nc.dram_tensor("dgat", [128, OWN, NRT], F32, kind="ExternalOutput").ap()
        dbg_outs["dxnf"] = nc.dram_tensor("dxnf", [128, NCH, 2048], BF16, kind="ExternalOutput").ap()

    probe = nc.alloc_sbuf_tensor("base_probe", [128, 32], F32)
    BASE = nc.lookup_mloc(probe).addr + 32

    cur = [BASE]

    def alloc(name, shape, dtype, off=None):
        if off is None:
            off = cur[0]
            per = int(np.prod(shape[1:])) * mybir.dt.size(dtype)
            cur[0] = (off + per + 31) & ~31
        return nc.alloc_sbuf_tensor_at(name, shape, dtype, offset=off)

    acc = alloc("acc", [128, OWN, DM], F32)
    xnf_own = alloc("xnf_own", [128, NCH, 1024], BF16)
    gat = alloc("gat", [128, OWN, NRT], F32)
    stab = alloc("stab", [128, OWN, 2], F32)
    G2 = alloc("G2", [32, 1024], F32)
    b1t = alloc("b1t", [128, NEXP, FCH], F32)
    bqt = alloc("bqt", [128, H], F32)
    bkt = alloc("bkt", [128, H], F32)
    bvt = alloc("bvt", [128, H], F32)
    identf = alloc("identf", [128, 128], F32)
    bgr = alloc("bgr", [128, NRT], F32)
    chl = alloc("chl", [128, NCH, 32], BF16)
    mv = alloc("mv", [128, 2, 2], F32)
    bno = alloc("bno", [128, 2, 4, 6], F32)
    vt = alloc("vt", [128, 2, 4], F32)
    m8 = alloc("m8", [128, 8], F32)
    sg = alloc("sg", [128, NRT], F32)
    gsc = alloc("gsc", [128, 2], F32)
    amx = alloc("amx", [128, 3, 4], F32)
    dgm = alloc("dgm", [128, 128], F32)

    SIB = (cur[0] + 31) & ~31
    xnf_sib = alloc("xnf_sib", [128, NCH, 1024], BF16, off=SIB)
    b2s = alloc("b2s", [19, DM], F32, off=SIB)
    w2b = alloc("w2b", [128, 2, FCH, 512], BF16, off=SIB + 8192)
    OVL = SIB + 32768

    xstage = alloc("xstage", [128, 2, DM], F32, off=OVL + 0)
    xnst = alloc("xnst", [128, DM], F32, off=OVL + 16384)
    xnbf = alloc("xnbf", [128, 2, DM], BF16, off=OVL + 24576)
    lobf = alloc("lobf", [128, 2, DM], BF16, off=OVL + 32768)
    lofm = alloc("lofm", [128, NCH, 128], BF16, off=OVL + 40960)
    cstage = alloc("cstage", [NRT, DM], F32, off=OVL + 45056)

    kbuf = alloc("kbuf", [128, 2, 2048], BF16, off=OVL + 0)
    qbuf = alloc("qbuf", [128, 2, 1024], BF16, off=OVL + 8192)
    vbuf = alloc("vbuf", [128, LCH, 256], BF16, off=OVL + 12288)
    wrA = alloc("wrA", [128, NCH, 128], BF16, off=OVL + 20480)
    wrB = alloc("wrB", [128, NCH, 128], BF16, off=OVL + 24576)
    wvb = alloc("wvb", [128, NCH, 256], BF16, off=OVL + 28672)
    atT = alloc("atT", [128, 16, 128], BF16, off=OVL + 28672)
    scob = alloc("scob", [128, 3, 2048], F32, off=OVL + 36864)
    cfm = alloc("cfm", [128, NCH, NRT], F32, off=OVL + 53248)
    prob = alloc("prob", [128, 2, 2048], BF16, off=OVL + 60416)
    outg = alloc("outg", [128, 2, 1024], BF16, off=OVL + 68608)
    msk8 = alloc("msk8", [128, 32], F32, off=OVL + 72704)

    hbuf = alloc("hbuf", [128, FCH, 1024], BF16, off=OVL + 0)
    w1b = alloc("w1b", [128, 2, NCH, 256], BF16, off=OVL + 22528)
    OVL_END = OVL + 71680
    assert OVL_END <= 229376 - 64, (BASE, OVL_END)

    pstack = contextlib.ExitStack()
    psF = [pstack.enter_context(nc.psum_tensor("psF%d" % i, [128, 512], F32))
           for i in range(8)]

    pl = Plan()
    bufs = {}

    def B_(name):
        if name not in bufs:
            bufs[name] = Buf()
        return bufs[name]

    def T(fn, wait=(), grp_end=True):
        return pl.on("tensor", fn, wait, inc="pe" if grp_end else None)

    def V(fn, wait=()):
        return pl.on("vector", fn, wait, inc="ve")

    def A(fn, wait=()):
        return pl.on("scalar", fn, wait, inc="ac")

    def PO(fn, wait=()):
        return pl.on("gpsimd", fn, wait, inc="po")

    def DMA(queue, stream, fn, wait=()):
        return pl.on(queue, fn, wait, inc=stream, n=16)

    for nm, src, dsth in (("bqt", d_bqt, bqt), ("bkt", d_bkt, bkt), ("bvt", d_bvt, bvt),
                          ("b1t", d_b1t, b1t), ("bgr", d_bgr, bgr),
                          ("identf", d_idf, identf),
                          ("cstage", d_cen, cstage), ("dgm", d_dgm, dgm)):
        t = DMA("sync", "dcc", lambda s, a=src, b=dsth: s.dma_start(out=b[:], in_=a))
        B_(nm).did_w(t)
    _cctot = ("dcc", pl.cnt["dcc"])
    for nm in ("bqt", "bkt", "bvt", "b1t", "bgr", "identf", "cstage", "dgm"):
        B_(nm).wrd = {}
        B_(nm).did_w(_cctot)

    t = V(lambda v: v.memset(G2[:], 1.0))
    B_("G2").did_w(t)

    for dc in range(NCH):
        ps, pb = psF[dc % 2], B_("ps%d" % (dc % 2))
        tk = T(lambda t_, dc=dc, ps=ps: t_.transpose(
            ps[0:128, 0:NRT], cstage[0:NRT, dc * 128:(dc + 1) * 128],
            identf[0:NRT, 0:NRT]),
            wait=B_("cstage").r() + B_("identf").r() + pb.w())
        pb.did_w(tk)
        B_("cstage").did_r(tk)
        tc_ = V(lambda v, dc=dc, ps=ps: v.tensor_copy(cfm[:, dc, :], ps[0:128, 0:NRT]),
                wait=[tk])
        pb.did_r(tc_)
        B_("cfm").did_w(tc_)
    th = V(lambda v: v.tensor_copy(chl[:, :, 0:NRT], cfm[:]), wait=B_("cfm").r())
    B_("chl").did_w(th)
    tl = V(lambda v: v.tensor_tensor(chl[:, :, NRT:32], cfm[:], chl[:, :, 0:NRT],
                                     ALU.subtract),
           wait=B_("cfm").r() + [th])
    B_("chl").did_w(tl)
    B_("cfm").did_r(tl)

    if "p0" in phases:
        for ct in list(range(OWN, LCH)) + list(range(OWN)):
            own = ct < OWN
            sb = ct % 2
            xs = B_("xstage%d" % sb)
            td = DMA("sync", "dxa" if sb == 0 else "dxb", lambda s, ct=ct, sb=sb:
                     s.dma_start(out=xstage[:, sb, :], in_=d_x[ct]), wait=xs.w())
            B_("xstage%d" % sb).wrd = {}
            B_("xstage%d" % sb).rdd = {}
            xs.did_w(td)
            tprev = B_("mv%d" % sb).w() + B_("bno%d" % sb).w()
            ts = []
            for i in range(4):
                ts.append(V(lambda v, sb=sb, i=i: v.bn_stats(
                    bno[:, sb, i, :], xstage[:, sb, i * 512:(i + 1) * 512]),
                    wait=[td] + tprev))
            bnb = B_("bno%d" % sb)
            bnb.wrd = {}
            bnb.rdd = {}
            for t_ in ts:
                bnb.did_w(t_)
            ta = V(lambda v, sb=sb: v.bn_aggr(mv[:, sb, :], bno[:, sb, :, :]),
                   wait=bnb.r() + B_("mv%d" % sb).w())
            B_("mv%d" % sb).wrd = {}
            B_("mv%d" % sb).rdd = {}
            B_("mv%d" % sb).did_w(ta)
            bnb.did_r(ta)
            t0 = V(lambda v, sb=sb: v.tensor_scalar(
                vt[:, sb, 0:1], mv[:, sb, 1:2], EPS, None, ALU.add),
                wait=[ta] + B_("vt%d" % sb).w())
            t1 = A(lambda a, sb=sb: a.activation(vt[:, sb, 1:2], vt[:, sb, 0:1], AF.Ln),
                   wait=[t0])
            t2 = A(lambda a, sb=sb: a.activation(vt[:, sb, 2:3], vt[:, sb, 1:2], AF.Exp,
                                                 scale=-0.5), wait=[t1])
            B_("vt%d" % sb).wrd = {}
            B_("vt%d" % sb).rdd = {}
            B_("vt%d" % sb).did_w(t2)
            if own:
                tsa = V(lambda v, ct=ct, sb=sb: v.tensor_scalar(
                    stab[:, ct, 0:1], vt[:, sb, 2:3], 1.0, None, ALU.add), wait=[t2])
                tsb = V(lambda v, ct=ct, sb=sb: v.scalar_tensor_tensor(
                    stab[:, ct, 1:2], mv[:, sb, 0:1], -1.0, vt[:, sb, 2:3],
                    ALU.mult, ALU.mult), wait=[t2, ta])
                B_("stab").did_w(tsa)
                B_("stab").did_w(tsb)
                B_("vt%d" % sb).did_r(tsb)
                B_("mv%d" % sb).did_r(tsb)
                tai = A(lambda a, ct=ct, sb=sb: a.activation(
                    acc[:, ct, :], xstage[:, sb, :], AF.Identity,
                    scale=stab[:, ct, 0:1], bias=stab[:, ct, 1:2]),
                    wait=[tsa, tsb, td])
                B_("acc").did_w(tai)
                B_("stab").did_r(tai)
                xs.did_r(tai)
                tx = V(lambda v, sb=sb: v.tensor_scalar(
                    xnst[:], xstage[:, sb, :], mv[:, sb, 0:1], vt[:, sb, 2:3],
                    ALU.subtract, ALU.mult),
                    wait=[t2, ta, td] + B_("xnst").w())
                B_("xnst").wrd = {}
                B_("xnst").rdd = {}
                B_("xnst").did_w(tx)
                B_("mv%d" % sb).did_r(tx)
                B_("vt%d" % sb).did_r(tx)
                xs.did_r(tx)
                xb = B_("xnbf%d" % sb)
                tph = A(lambda a, sb=sb: a.activation(xnbf[:, sb, :], xnst[:], AF.Copy),
                        wait=[tx] + xb.w())
                xb.wrd = {}
                xb.rdd = {}
                xb.did_w(tph)
                lb = B_("lobf%d" % sb)
                tpl = V(lambda v, sb=sb: v.tensor_tensor(
                    lobf[:, sb, :], xnst[:], xnbf[:, sb, :], ALU.subtract),
                    wait=[tx, tph] + lb.w())
                lb.wrd = {}
                lb.rdd = {}
                lb.did_w(tpl)
                B_("xnst").did_r(tpl)
                xb.did_r(tpl)
            else:
                xb = B_("xnbf%d" % sb)
                tss = V(lambda v, sb=sb: v.scalar_tensor_tensor(
                    vt[:, sb, 3:4], mv[:, sb, 0:1], -1.0, vt[:, sb, 2:3],
                    ALU.mult, ALU.mult), wait=[t2, ta])
                B_("vt%d" % sb).did_w(tss)
                tph = A(lambda a, sb=sb: a.activation(
                    xnbf[:, sb, :], xstage[:, sb, :], AF.Identity,
                    scale=vt[:, sb, 2:3], bias=vt[:, sb, 3:4]),
                    wait=[tss, td] + xb.w())
                xb.wrd = {}
                xb.rdd = {}
                xb.did_w(tph)
                B_("mv%d" % sb).did_r(tph)
                B_("vt%d" % sb).did_r(tph)
                xs.did_r(tph)

            dstt = xnf_own if own else xnf_sib
            dstn = "xnf_own" if own else "xnf_sib"
            col = (ct % 8) * 128
            strm = "dta" if sb == 0 else "dtb"
            blkn = dstn + str((ct % 8) // 4)
            te = DMA("sync", strm, lambda s, sb=sb, dstt=dstt, col=col: s.dma_start(
                out=dstt[:, :, col:col + 128],
                in_=xnbf[:, sb, :].rearrange("p (c x) -> p c x", c=NCH),
                transpose=True), wait=[tph] + B_(blkn).w())
            xb.did_r(te)
            B_(blkn).did_w(te)

            if own:
                tlo = DMA("sync", "dsb" if sb else "dsa",
                          lambda s, sb=sb: s.dma_start(
                    out=lofm[:, :, :],
                    in_=lobf[:, sb, :].rearrange("p (c x) -> p c x", c=NCH),
                    transpose=True), wait=[tpl] + B_("lofm").w())
                lb.did_r(tlo)
                B_("lofm").wrd = {}
                B_("lofm").rdd = {}
                B_("lofm").did_w(tlo)
                lts = [tlo] * NCH
                mml = None
                for dc in range(NCH):
                    T(lambda t_, dc=dc, col=col: t_.matmul(
                        psF[0][:, 0:16], xnf_own[:, dc, col:col + 128],
                        chl[:, dc, 0:16], start=(dc == 0), stop=False),
                        wait=(B_("xnf_own%d" % ((ct % 8) // 4)).r() + B_("chl").r() +
                              (B_("ps0").w() if dc == 0 else [])),
                        grp_end=False)
                    T(lambda t_, dc=dc, col=col: t_.matmul(
                        psF[0][:, 0:16], xnf_own[:, dc, col:col + 128],
                        chl[:, dc, 16:32], start=False, stop=False), grp_end=False)
                for dc in range(NCH):
                    mml = T(lambda t_, dc=dc: t_.matmul(
                        psF[0][:, 0:16], lofm[:, dc, :], chl[:, dc, 0:16],
                        start=False, stop=(dc == NCH - 1)),
                        wait=[lts[dc]], grp_end=(dc == NCH - 1))
                B_("ps0").wrd = {}
                B_("ps0").rdd = {}
                B_("ps0").did_w(mml)
                B_("lofm").did_r(mml)
                t2g = V(lambda v: v.tensor_copy(sg[:], psF[0][:, 0:16]),
                        wait=[mml] + B_("sg").w())
                B_("ps0").did_r(t2g)
                t3g = A(lambda a: a.activation(sg[:], sg[:], AF.Sigmoid), wait=[t2g])
                t4g = V(lambda v: v.tensor_tensor(sg[:], sg[:], bgr[:], ALU.add),
                        wait=[t3g] + B_("bgr").r())
                B_("sg").wrd = {}
                B_("sg").rdd = {}
                B_("sg").did_w(t4g)
                t5g = V(lambda v: v.max(out=m8[:], in_=sg[:]), wait=[t4g] + B_("m8").w())
                B_("m8").wrd = {}
                B_("m8").rdd = {}
                B_("m8").did_w(t5g)
                t6g = V(lambda v, ct=ct: v.scalar_tensor_tensor(
                    gat[:, ct, :], sg[:], m8[:, 3:4], sg[:], ALU.is_ge, ALU.mult),
                    wait=[t5g])
                B_("gat").did_w(t6g)
                B_("m8").did_r(t6g)
                B_("sg").did_r(t6g)
                t7g = V(lambda v, ct=ct: v.reduce_sum(gsc[:, 0:1], gat[:, ct, :], axis=AX),
                        wait=[t6g] + B_("gsc").w())
                t8g = V(lambda v: v.tensor_scalar(gsc[:, 0:1], gsc[:, 0:1], 1e-8, None,
                                                  ALU.max), wait=[t7g])
                t9g = V(lambda v: v.reciprocal(gsc[:, 1:2], gsc[:, 0:1]), wait=[t8g])
                B_("gsc").wrd = {}
                B_("gsc").rdd = {}
                B_("gsc").did_w(t9g)
                tag = V(lambda v, ct=ct: v.tensor_scalar(
                    gat[:, ct, :], gat[:, ct, :], gsc[:, 1:2], None, ALU.mult),
                    wait=[t9g, t6g])
                B_("gat").did_w(tag)
                B_("gsc").did_r(tag)
                tk = T(lambda t_, ct=ct: t_.transpose(
                    psF[2][0:16, 0:128], gat[:, ct, :], identf[:]),
                    wait=[tag] + B_("ps2").w())
                B_("ps2").wrd = {}
                B_("ps2").rdd = {}
                B_("ps2").did_w(tk)
                B_("gat").did_r(tk)
                tg = V(lambda v, col=col: v.tensor_copy(
                    G2[0:16, col:col + 128], psF[2][0:16, 0:128]),
                    wait=[tk] + B_("G2").r())
                B_("ps2").did_r(tg)
                B_("G2").did_w(tg)

        if dbg:
            for ct in range(OWN):
                t_ = DMA("scalar", "ds", lambda s, ct=ct: s.dma_start(
                    out=dbg_outs["dacc0"][ct], in_=acc[:, ct, :]), wait=B_("acc").r())
                B_("acc").did_r(t_)
            for dc in range(NCH):
                t_ = DMA("scalar", "ds", lambda s, dc=dc: s.dma_start(
                    out=dbg_outs["dxnf"][:, dc, 0:1024], in_=xnf_own[:, dc, :]),
                    wait=B_("xnf_own0").r() + B_("xnf_own1").r())
                B_("xnf_own0").did_r(t_)
                B_("xnf_own1").did_r(t_)
                t_ = DMA("scalar", "ds", lambda s, dc=dc: s.dma_start(
                    out=dbg_outs["dxnf"][:, dc, 1024:2048], in_=xnf_sib[:, dc, :]),
                    wait=B_("xnf_sib0").r() + B_("xnf_sib1").r())
                B_("xnf_sib0").did_r(t_)
                B_("xnf_sib1").did_r(t_)
            t_ = DMA("scalar", "ds", lambda s: s.dma_start(out=dbg_outs["dgat"], in_=gat[:]),
                     wait=B_("gat").r())
            B_("gat").did_r(t_)
            for nm in ("acc", "xnf_own0", "xnf_own1", "xnf_sib0", "xnf_sib1", "gat"):
                B_(nm).rdd["ds"] = pl.cnt["ds"]

    def xtile(tt):
        return (xnf_own, tt * 512) if tt < 2 else (xnf_sib, (tt - 2) * 512)

    def xtbuf(tt):
        return ("xnf_own%d" % tt) if tt < 2 else ("xnf_sib%d" % (tt - 2))

    if "p1" in phases:
        bar = pl.barrier_toks()
        for nm in ("kbuf0", "kbuf1", "vbuf", "qbuf0", "qbuf1", "wrA", "wrB",
                   "wvpb", "scob0", "scob1", "scob2", "prob0", "prob1", "outg",
                   "msk8", "amx0", "amx1", "amx2"):
            B_(nm).seed(bar)
        tm = DMA("sync", "dcc", lambda s: s.dma_start(out=msk8[:], in_=d_msk),
                 wait=B_("msk8").w())
        B_("msk8").did_w(tm)

        def make_prep(h):
            ksl = h % 2
            th = []

            def t_issue():
                twk = DMA("gpsimd", "dwk", lambda s, h=h: s.dma_start(
                    out=wrA[:], in_=wsrc(("wk", h))), wait=B_("wrA").w())
                B_("wrA").wrd = {}
                B_("wrA").rdd = {}
                B_("wrA").did_w(twk)
                twq = DMA("gpsimd", "dwq", lambda s, h=h: s.dma_start(
                    out=wrB[:], in_=wsrc(("wq", h))), wait=B_("wrB").w())
                B_("wrB").wrd = {}
                B_("wrB").rdd = {}
                B_("wrB").did_w(twq)
                prep_tok[h] = (twk, twq)
            th.append(t_issue)

            def mk_ktile(tt):
                def t_k():
                    twk, _ = prep_tok[h]
                    kb = B_("kbuf%d" % ksl)
                    if tt == 0:
                        kold = kb.w()
                        kb.wrd = {}
                        kb.rdd = {}
                        kprep_old[h] = kold
                    xt, xo = xtile(tt)
                    bk = 4 + tt % 2
                    ps, pb = psF[bk], B_("ps%d" % bk)
                    mm = None
                    for dc in range(NCH):
                        mm = T(lambda t_, dc=dc, xt=xt, xo=xo, ps=ps: t_.matmul(
                            ps[:], wrA[:, dc, :], xt[:, dc, xo:xo + 512],
                            start=(dc == 0), stop=(dc == NCH - 1)),
                            wait=([twk] + B_(xtbuf(tt)).r() +
                                  (pb.w() if dc == 0 else [])),
                            grp_end=(dc == NCH - 1))
                    pb.wrd = {}
                    pb.rdd = {}
                    pb.did_w(mm)
                    B_(xtbuf(tt)).did_r(mm)
                    B_("wrA").did_r(mm)
                    kold = kprep_old[h]
                    if tt % 2 == 0:
                        te = A(lambda a, tt=tt, h=h, ksl=ksl, ps=ps: a.activation(
                            kbuf[:, ksl, tt * 512:(tt + 1) * 512], ps[:], AF.Identity,
                            bias=bkt[:, h:h + 1]),
                            wait=[mm] + (kold if tt == 0 else []) + B_("bkt").r())
                    else:
                        te = V(lambda v, tt=tt, h=h, ksl=ksl, ps=ps: v.tensor_scalar(
                            kbuf[:, ksl, tt * 512:(tt + 1) * 512], ps[:],
                            bkt[:, h:h + 1], None, ALU.add),
                            wait=[mm] + (kold if tt == 0 else []) + B_("bkt").r())
                    pb.did_r(te)
                    B_("kbuf%d" % ksl).did_w(te)
                return t_k
            for tt in range(4):
                th.append(mk_ktile(tt))

            def mk_qtile(tt):
                def t_q():
                    _, twq = prep_tok[h]
                    qb = B_("qbuf%d" % ksl)
                    if tt == 0:
                        qold = qb.w()
                        qb.wrd = {}
                        qb.rdd = {}
                        qprep_old[h] = qold
                    xt, xo = xtile(tt)
                    bk = 4 + tt % 2
                    ps, pb = psF[bk], B_("ps%d" % bk)
                    mm = None
                    for dc in range(NCH):
                        mm = T(lambda t_, dc=dc, xt=xt, xo=xo, ps=ps: t_.matmul(
                            ps[:], wrB[:, dc, :], xt[:, dc, xo:xo + 512],
                            start=(dc == 0), stop=(dc == NCH - 1)),
                            wait=[twq] + B_(xtbuf(tt)).r() +
                                 (pb.w() if dc == 0 else []),
                            grp_end=(dc == NCH - 1))
                    pb.wrd = {}
                    pb.rdd = {}
                    pb.did_w(mm)
                    B_(xtbuf(tt)).did_r(mm)
                    B_("wrB").did_r(mm)
                    qold = qprep_old[h]
                    te = A(lambda a, tt=tt, h=h, ksl=ksl, ps=ps: a.activation(
                        qbuf[:, ksl, tt * 512:(tt + 1) * 512], ps[:], AF.Identity,
                        bias=bqt[:, h:h + 1]),
                        wait=[mm] + (qold if tt == 0 else []) + B_("bqt").r())
                    pb.did_r(te)
                    B_("qbuf%d" % ksl).did_w(te)
                return t_q
            for tt in range(2):
                th.append(mk_qtile(tt))
            return th

        prep_tok = {}
        kprep_old = {}
        qprep_old = {}

        for h in range(H):
            hp = h % 2
            ksl = h % 2
            wv = B_("wvpb")
            kb = B_("kbuf%d" % ksl)
            qb = B_("qbuf%d" % ksl)
            vb = B_("vbuf")
            if h == 0:
                for th in make_prep(0):
                    th()
            nxt = make_prep(h + 1) if h + 1 < H else []

            tnm_tok = {}
            smx_tok = {}

            def plan_scores(tc, h=h, ksl=ksl):
                sl = tc % 3
                LA = (tc + 1) * 128
                L = LA + 1024
                kb = B_("kbuf%d" % ksl)
                qb = B_("qbuf%d" % ksl)
                sc = B_("scob%d" % sl)
                scold = sc.w()
                sc.wrd = {}
                sc.rdd = {}
                for j in range(2):
                    ps, pb = psF[j % 2], B_("ps%d" % (j % 2))
                    mm = T(lambda t_, j=j, tc=tc, ksl=ksl, ps=ps: t_.matmul(
                        ps[:], qbuf[:, ksl, tc * 128:(tc + 1) * 128],
                        kbuf[:, ksl, 1024 + j * 512:1536 + j * 512],
                        start=True, stop=True),
                        wait=qb.r() + kb.r() + pb.w())
                    pb.wrd = {}
                    pb.rdd = {}
                    pb.did_w(mm)
                    kb.did_r(mm)
                    qb.did_r(mm)
                    te = A(lambda a, j=j, sl=sl, ps=ps: a.activation(
                        scob[:, sl, j * 512:(j + 1) * 512], ps[:],
                        AF.Identity, scale=SCALE, bias=msk8[:, 0:1]),
                        wait=[mm] + B_("msk8").r() + (scold if j == 0 else []))
                    pb.did_r(te)
                    B_("msk8").did_r(te)
                    sc.did_w(te)
                atiles = [(a0, min(a0 + 512, LA)) for a0 in range(0, LA, 512)]
                lastev = None
                for i, (a0, a1) in enumerate(atiles):
                    ps, pb = psF[i % 2], B_("ps%d" % (i % 2))
                    mm = T(lambda t_, a0=a0, a1=a1, tc=tc, ksl=ksl, ps=ps: t_.matmul(
                        ps[:, 0:a1 - a0], qbuf[:, ksl, tc * 128:(tc + 1) * 128],
                        kbuf[:, ksl, a0:a1], start=True, stop=True),
                        wait=pb.w())
                    pb.wrd = {}
                    pb.rdd = {}
                    pb.did_w(mm)
                    qb.did_r(mm)
                    kb.did_r(mm)
                    if i == len(atiles) - 1:
                        te = V(lambda v, a0=a0, a1=a1, sl=sl, ps=ps: v.tensor_scalar(
                            scob[:, sl, 1024 + a0:1024 + a1], ps[:, 0:a1 - a0],
                            SCALE, None, ALU.mult), wait=[mm])
                    else:
                        te = A(lambda a, a0=a0, a1=a1, sl=sl, ps=ps: a.activation(
                            scob[:, sl, 1024 + a0:1024 + a1], ps[:, 0:a1 - a0],
                            AF.Identity, scale=SCALE), wait=[mm])
                    pb.did_r(te)
                    sc.did_w(te)
                    lastev = te
                tdg = V(lambda v, tc=tc, sl=sl: v.tensor_tensor(
                    scob[:, sl, 1024 + tc * 128:1024 + (tc + 1) * 128],
                    scob[:, sl, 1024 + tc * 128:1024 + (tc + 1) * 128],
                    dgm[:], ALU.add), wait=[lastev] + B_("dgm").r())
                sc.did_w(tdg)
                B_("dgm").did_r(tdg)
                am = B_("amx%d" % sl)
                amold = am.w()
                am.wrd = {}
                am.rdd = {}
                tmx = V(lambda v, L=L, sl=sl: v.reduce_max(
                    amx[:, sl, 0:1], scob[:, sl, 0:L], axis=AX, negate=True),
                    wait=sc.w() + amold)
                sc.did_r(tmx)
                am.did_w(tmx)
                smx_tok[tc] = tmx

            def plan_scores_b(tc, h=h):
                sl = tc % 3
                L = (tc + 9) * 128
                sc = B_("scob%d" % sl)
                am = B_("amx%d" % sl)
                pr = B_("prob%d" % (tc % 2))
                prold = pr.w()
                pr.wrd = {}
                pr.rdd = {}
                tex = A(lambda a, L=L, sl=sl, pp=tc % 2: a.activation(
                    prob[:, pp, 0:L], scob[:, sl, 0:L], AF.Exp, bias=amx[:, sl, 0:1],
                    scale=1.0, accum_out=amx[:, sl, 2:3]),
                    wait=[smx_tok[tc]] + prold)
                pr.did_w(tex)
                sc.did_r(tex)
                am.did_w(tex)
                trc = V(lambda v, sl=sl: v.reciprocal(amx[:, sl, 3:4], amx[:, sl, 2:3]),
                        wait=[tex])
                am.did_w(trc)
                tnm = V(lambda v, L=L, sl=sl, pp=tc % 2: v.tensor_scalar(
                    prob[:, pp, 0:L], prob[:, pp, 0:L], amx[:, sl, 3:4], None, ALU.mult),
                    wait=[trc, tex])
                pr.did_w(tnm)
                am.did_r(tnm)
                tnm_tok[tc] = tnm

            def plan_transposes(tc, h=h):
                sl = tc % 2
                pr = B_("prob%d" % sl)
                strm = "dta" if tc % 2 == 0 else "dtb"
                t1 = DMA("sync", strm, lambda s, tc=tc, sl=sl: s.dma_start(
                    out=atT[:, 0:tc + 9, 0:128],
                    in_=prob[:, sl, 0:(tc + 9) * 128].rearrange(
                        "p (c x) -> p c x", c=tc + 9),
                    transpose=True),
                    wait=[tnm_tok[tc]] + wv.w())
                wv.did_w(t1)
                pr.did_r(t1)

            def plan_av(tc, h=h, hp=hp):
                out_ps = psF[6 + tc % 2]
                opb = B_("ps%d" % (6 + tc % 2))
                opold = opb.w()
                opb.wrd = {}
                opb.rdd = {}
                order = [("sib", j) for j in range(8)] + \
                        [("own", i) for i in range(tc + 1)]
                mmf = None
                for oi, (reg, i) in enumerate(order):
                    slot = i if reg == "sib" else 8 + i
                    lci = 8 + i if reg == "sib" else i
                    mmf = T(lambda t_, lci=lci, hp=hp, slot=slot, oi=oi,
                            out_ps=out_ps, n=len(order): t_.matmul(
                        out_ps[:, 0:128], vbuf[:, lci, hp * 128:(hp + 1) * 128],
                        atT[:, slot, 0:128], start=(oi == 0), stop=(oi == n - 1)),
                        wait=wv.r() + vb.r() + (opold if oi == 0 else []),
                        grp_end=(oi == len(order) - 1))
                    wv.did_r(mmf)
                    vb.did_r(mmf)
                opb.did_w(mmf)
                toe = A(lambda a, hp=hp, tc=tc, h=h, out_ps=out_ps: a.activation(
                    outg[:, hp, tc * 128:(tc + 1) * 128], out_ps[:, 0:128],
                    AF.Identity, bias=bvt[:, h:h + 1]),
                    wait=[mmf] + (B_("outg").w() if (hp == 0 and tc == 0) else []) +
                         B_("bvt").r())
                opb.did_r(toe)
                if hp == 0 and tc == 0:
                    B_("outg").wrd = {}
                    B_("outg").rdd = {}
                B_("outg").did_w(toe)

            plan_scores(0)
            plan_scores(1)
            plan_scores_b(0)
            plan_scores(2)

            if hp == 0:
                twv = DMA("gpsimd", "dwv", lambda s, h=h: s.dma_start(
                    out=wvb[:], in_=wsrc(("wv", h))), wait=wv.w())
                wv.wrd = {}
                wv.rdd = {}
                wv.did_w(twv)
                vold = vb.w()
                vb.wrd = {}
                vb.rdd = {}
                for lc in range(LCH):
                    xt = xnf_own if lc < 8 else xnf_sib
                    xo = (lc % 8) * 128
                    bk = 4 + lc % 2
                    ps, pb = psF[bk], B_("ps%d" % bk)
                    mm = None
                    for dc in range(NCH):
                        mm = T(lambda t_, dc=dc, xt=xt, xo=xo, ps=ps: t_.matmul(
                            ps[:, 0:256], xt[:, dc, xo:xo + 128], wvb[:, dc, :],
                            start=(dc == 0), stop=(dc == NCH - 1)),
                            wait=[twv] + B_(xtbuf(lc // 4)).r() +
                                 (pb.w() if dc == 0 else []),
                            grp_end=(dc == NCH - 1))
                    pb.wrd = {}
                    pb.rdd = {}
                    pb.did_w(mm)
                    B_(xtbuf(lc // 4)).did_r(mm)
                    wv.did_r(mm)
                    if lc % 2 == 0:
                        te = A(lambda a, lc=lc, ps=ps: a.activation(
                            vbuf[:, lc, :], ps[:, 0:256], AF.Copy),
                            wait=[mm] + (vold if lc == 0 else []))
                    else:
                        te = V(lambda v, lc=lc, ps=ps: v.tensor_copy(
                            vbuf[:, lc, :], ps[:, 0:256]),
                            wait=[mm] + (vold if lc == 0 else []))
                    pb.did_r(te)
                    vb.did_w(te)

            for tc in range(OWN):
                if tc + 1 < OWN:
                    plan_scores_b(tc + 1)
                if tc + 3 < OWN:
                    plan_scores(tc + 3)
                if nxt:
                    nxt.pop(0)()
                plan_transposes(tc)
                plan_av(tc)
            while nxt:
                nxt.pop(0)()

            if hp == 1:
                g2i = h // 2
                wps = [kbuf[:, 1, :], atT[:].rearrange("p c x -> p (c x)")]
                wpB = [B_("kbuf1"), wv]
                tws = []
                for hc in range(2):
                    wpo = wpB[hc].w()
                    wpB[hc].wrd = {}
                    wpB[hc].rdd = {}
                    tw = DMA("gpsimd", "dwp", lambda s, g2i=g2i, hc=hc: s.dma_start(
                        out=wps[hc], in_=wsrc(("wp", g2i * 2 + hc))),
                        wait=wpo)
                    wpB[hc].did_w(tw)
                    tws.append(tw)
                wptok = tws[1]
                for hc in range(2):
                    wpB[hc].wrd["dwp"] = pl.cnt["dwp"]
                for tcq in range(OWN):
                    mms = []
                    for hc in range(2):
                        for dt in range(4):
                            bk = (tcq % 2) * 4 + dt
                            ps, pb = psF[bk], B_("ps%d" % bk)
                            mm = T(lambda t_, hc=hc, tcq=tcq, dt=dt, ps=ps: t_.matmul(
                                ps[:], outg[:, hc, tcq * 128:(tcq + 1) * 128],
                                wps[hc][:, dt * 512:(dt + 1) * 512],
                                start=(hc == 0), stop=(hc == 1)),
                                wait=[wptok] + B_("outg").r() +
                                     (pb.w() if hc == 0 else []),
                                grp_end=(hc == 1))
                            if hc == 0:
                                pb.wrd = {}
                                pb.rdd = {}
                            if mm is not None:
                                pb.did_w(mm)
                                wpB[0].did_r(mm)
                                wpB[1].did_r(mm)
                                B_("outg").did_r(mm)
                                mms.append((dt, ps, pb, mm))
                    for dt, ps, pb, mm in mms:
                        tev = V(lambda v, tcq=tcq, dt=dt, ps=ps: v.tensor_tensor(
                            acc[:, tcq, dt * 512:(dt + 1) * 512],
                            acc[:, tcq, dt * 512:(dt + 1) * 512], ps[:], ALU.add),
                            wait=[mm] + B_("acc").w())
                        pb.did_r(tev)
                        B_("acc").did_w(tev)

        if dbg:
            for ct in range(OWN):
                t_ = DMA("scalar", "ds", lambda s, ct=ct: s.dma_start(
                    out=dbg_outs["dacc1"][ct], in_=acc[:, ct, :]), wait=B_("acc").r())
                B_("acc").did_r(t_)
            B_("acc").rdd["ds"] = pl.cnt["ds"]

    if "p2" in phases:
        bar = pl.barrier_toks()
        for nm in ("hbuf", "w1b0", "w1b1", "w2b0", "w2b1", "b2sB"):
            B_(nm).seed(bar)
        tb2 = DMA("sync", "dcc", lambda s: s.dma_start(out=b2s[:], in_=d_b2s),
                  wait=B_("b2sB").w() + B_("xnf_sib0").w() + B_("xnf_sib1").w())
        B_("b2sB").did_w(tb2)
        for tcq in range(OWN):
            for dt in range(4):
                ps, pb = psF[(tcq * 4 + dt) % 4], B_("ps%d" % ((tcq * 4 + dt) % 4))
                mm = T(lambda t_, tcq=tcq, dt=dt, ps=ps: t_.matmul(
                    ps[:], G2[0:19, tcq * 128:(tcq + 1) * 128],
                    b2s[0:19, dt * 512:(dt + 1) * 512], start=True, stop=True),
                    wait=[tb2] + B_("G2").r() + pb.w())
                pb.wrd = {}
                pb.rdd = {}
                pb.did_w(mm)
                B_("G2").did_r(mm)
                te = V(lambda v, tcq=tcq, dt=dt, ps=ps: v.tensor_tensor(
                    acc[:, tcq, dt * 512:(dt + 1) * 512],
                    acc[:, tcq, dt * 512:(dt + 1) * 512], ps[:], ALU.add),
                    wait=[mm] + B_("acc").w())
                pb.did_r(te)
                B_("acc").did_w(te)

        for e in range(NEXP):
            hb = B_("hbuf")
            hold = hb.w()
            hb.wrd = {}
            hb.rdd = {}
            first_h = True
            for ft, (f0, f1) in enumerate(F_TILES):
                wslot = ft % 2
                wb = B_("w1b%d" % wslot)
                tw = DMA("gpsimd", "dw1a" if wslot == 0 else "dw1b",
                         lambda s, e=e, ft=ft, wslot=wslot, w=f1 - f0:
                         s.dma_start(out=w1b[:, wslot, :, 0:w], in_=wsrc(("w1", e, ft))),
                         wait=wb.w())
                wb.wrd = {}
                wb.rdd = {}
                wb.did_w(tw)
                nfc = (f1 - f0) // 128
                for fci in range(nfc):
                    Fg = (f0 // 128) + fci
                    bk0, bk1 = (6, 7) if Fg % 2 else (0, 1)
                    ps0, ps1 = psF[bk0], psF[bk1]
                    pb0, pb1 = B_("ps%d" % bk0), B_("ps%d" % bk1)
                    mm0 = mm1 = None
                    for dc in range(NCH):
                        mm0 = T(lambda t_, dc=dc, fci=fci, wslot=wslot, ps=ps0: t_.matmul(
                            ps[:], w1b[:, wslot, dc, fci * 128:(fci + 1) * 128],
                            xnf_own[:, dc, 0:512],
                            start=(dc == 0), stop=(dc == NCH - 1)),
                            wait=[tw] + B_("xnf_own0").r() +
                                 (pb0.w() if dc == 0 else []),
                            grp_end=(dc == NCH - 1))
                        mm1 = T(lambda t_, dc=dc, fci=fci, wslot=wslot, ps=ps1: t_.matmul(
                            ps[:], w1b[:, wslot, dc, fci * 128:(fci + 1) * 128],
                            xnf_own[:, dc, 512:1024],
                            start=(dc == 0), stop=(dc == NCH - 1)),
                            wait=B_("xnf_own1").r() + (pb1.w() if dc == 0 else []),
                            grp_end=(dc == NCH - 1))
                    pb0.wrd = {}
                    pb0.rdd = {}
                    pb0.did_w(mm0)
                    pb1.wrd = {}
                    pb1.rdd = {}
                    pb1.did_w(mm1)
                    wb.did_r(mm1)
                    for tt, (mm, ps, pb) in enumerate(((mm0, ps0, pb0), (mm1, ps1, pb1))):
                        te = A(lambda a, Fg=Fg, tt=tt, e=e, ps=ps: a.activation(
                            hbuf[:, Fg, tt * 512:(tt + 1) * 512], ps[:], AF.Relu,
                            bias=b1t[:, e, Fg:Fg + 1]),
                            wait=[mm] + (hold if first_h else []) + B_("b1t").r())
                        first_h = False
                        pb.did_r(te)
                        hb.did_w(te)
            for dt in range(4):
                wslot = dt % 2
                wb = B_("w2b%d" % wslot)
                tw = DMA("gpsimd", "dw2a" if wslot == 0 else "dw2b",
                         lambda s, e=e, dt=dt, wslot=wslot: s.dma_start(
                    out=w2b[:, wslot, :, :], in_=wsrc(("w2", e, dt))), wait=wb.w())
                wb.wrd = {}
                wb.rdd = {}
                wb.did_w(tw)
                for tcq in range(OWN):
                    bk = 2 + (tcq % 4)
                    ps, pb = psF[bk], B_("ps%d" % bk)
                    mm = None
                    for fc in range(FCH):
                        mm = T(lambda t_, fc=fc, tcq=tcq, wslot=wslot, ps=ps: t_.matmul(
                            ps[:], hbuf[:, fc, tcq * 128:(tcq + 1) * 128],
                            w2b[:, wslot, fc, :], start=(fc == 0), stop=(fc == FCH - 1)),
                            wait=[tw] + hb.r() + (pb.w() if fc == 0 else []),
                            grp_end=(fc == FCH - 1))
                    pb.wrd = {}
                    pb.rdd = {}
                    pb.did_w(mm)
                    wb.did_r(mm)
                    hb.did_r(mm)
                    if e < 2:
                        te = V(lambda v, tcq=tcq, dt=dt, ps=ps: v.tensor_tensor(
                            acc[:, tcq, dt * 512:(dt + 1) * 512],
                            acc[:, tcq, dt * 512:(dt + 1) * 512], ps[:], ALU.add),
                            wait=[mm] + B_("acc").w())
                    else:
                        te = V(lambda v, tcq=tcq, dt=dt, e=e, ps=ps:
                               v.scalar_tensor_tensor(
                            acc[:, tcq, dt * 512:(dt + 1) * 512], ps[:],
                            gat[:, tcq, e - 2:e - 1],
                            acc[:, tcq, dt * 512:(dt + 1) * 512], ALU.mult, ALU.add),
                            wait=[mm] + B_("acc").w() + B_("gat").r())
                        B_("gat").did_r(te)
                    pb.did_r(te)
                    B_("acc").did_w(te)

    last = None
    for tcq in range(OWN):
        last = DMA("scalar", "ds", lambda s, tcq=tcq: s.dma_start(
            out=d_y[tcq], in_=acc[:, tcq, :]), wait=B_("acc").r())
        B_("acc").did_r(last)
    pl.on("scalar", None, [last])

    sstack = contextlib.ExitStack()
    semh = {s: sstack.enter_context(nc.semaphore(s)) for s in SEMS}

    with nc.Block() as blk:
        def emit(ops):
            def _f(eng):
                for ws, fn, inc, n in ops:
                    for s, v in ws:
                        eng.wait_ge(semh[s], v)
                    if fn is not None:
                        instr = fn(eng)
                        if inc is not None:
                            instr.then_inc(semh[inc], n)
            return _f

        blk.tensor(emit(pl.ops["tensor"]))
        blk.scalar(emit(pl.ops["scalar"]))
        blk.vector(emit(pl.ops["vector"]))
        blk.gpsimd(emit(pl.ops["gpsimd"]))
        blk.sync(emit(pl.ops["sync"]))

    sstack.close()
    pstack.close()
    return nc, dbg_outs


def make_inputs(inp):
    x = np.asarray(inp["L0_attn_x"], np.float32)
    wstream = pack_wstream(inp)
    bqt = np.ascontiguousarray(np.asarray(inp["L0_attn_bq"], np.float32).reshape(H, 128).T)
    bkt = np.ascontiguousarray(np.asarray(inp["L0_attn_bk"], np.float32).reshape(H, 128).T)
    bvt = np.ascontiguousarray(np.asarray(inp["L0_attn_bv"], np.float32).reshape(H, 128).T)
    b1_all = np.concatenate([np.asarray(inp["L0_ffn_shared_b1"], np.float32),
                             np.asarray(inp["L0_ffn_routed_b1"], np.float32)], 0)
    b1t = np.ascontiguousarray(b1_all.reshape(NEXP, FCH, 128).transpose(2, 0, 1))
    bgrep = np.ascontiguousarray(
        np.broadcast_to(np.asarray(inp["L0_ffn_bias"], np.float32)[None, :], (128, NRT)))
    b2s = np.ascontiguousarray(np.concatenate([
        np.asarray(inp["L0_ffn_routed_b2"], np.float32),
        np.asarray(inp["L0_attn_bp"], np.float32)[None, :],
        np.asarray(inp["L0_ffn_shared_b2"], np.float32)], 0))
    cen = np.ascontiguousarray(np.asarray(inp["L0_ffn_centroids"], np.float32))
    identf = np.eye(128, dtype=np.float32)
    jj = np.arange(128)
    diagm = np.where(jj[None, :] <= jj[:, None], 0.0, NEG).astype(np.float32)

    in_maps = []
    for c in range(8):
        b, hh = c // 2, c % 2
        xb = x[b]
        xl = xb if hh == 0 else np.concatenate([xb[1024:], xb[:1024]], 0)
        xl = np.ascontiguousarray(xl.reshape(LCH, 128, DM))
        msk = np.full((128, 32), NEG if hh == 0 else 0.0, dtype=np.float32)
        in_maps.append({
            "xin": xl, "wstream": wstream, "bqt": bqt, "bkt": bkt, "bvt": bvt,
            "b1t": b1t, "bgrep": bgrep, "b2s": b2s, "cen": cen,
            "identf": identf, "msk8": msk, "diagm": diagm,
        })
    return in_maps


def kernel(**inputs):
    nc, _ = build_nc(dbg=False)
    in_maps = make_inputs(inputs)
    res = run_bass_kernel_spmd(nc, in_maps, core_ids=list(range(8)))
    out = np.empty((B, S, DM), np.float32)
    for c in range(8):
        b, hh = c // 2, c % 2
        y = res.results[c]["y"].reshape(1024, DM)
        out[b, hh * 1024:(hh + 1) * 1024, :] = y
    return out


# version 15
# speedup vs baseline: 1.0523x; 1.0523x over previous
import contextlib
import math
import numpy as np
import ml_dtypes
import concourse.bass as bass
import concourse.mybir as mybir
from concourse.bass_utils import run_bass_kernel_spmd
from concourse.alu_op_type import AluOpType

F32 = mybir.dt.float32
BF16 = mybir.dt.bfloat16
AF = mybir.ActivationFunctionType
AX = mybir.AxisListType.X
ALU = AluOpType
BF = ml_dtypes.bfloat16

B, S, DM, H, HD = 4, 2048, 2048, 16, 128
NCH = DM // 128
LCH = 16
OWN = 8
SCALE = 1.0 / math.sqrt(2048.0)
EPS = 1e-5
DFF = 1408
FCH = DFF // 128
NEXP = 18
NRT = 16
NEG = -1.0e30
F_TILES = [(0, 256), (256, 512), (512, 768), (768, 1024), (1024, 1280), (1280, 1408)]
SEMS = ("pe", "ac", "ve", "po", "dxa", "dxb", "dcc", "dwk", "dwq", "dwv", "dwp", "dw1a", "dw1b", "dw2a", "dw2b", "ds", "dta", "dtb", "dsa", "dsb")


def wplan():
    plan = []
    for h in range(H):
        if h % 2 == 0:
            plan.append((("wv", h), NCH * 256))
        plan.append((("wk", h), NCH * 128))
        plan.append((("wq", h), NCH * 128))
    for hc in range(H):
        plan.append((("wp", hc), DM))
    for e in range(NEXP):
        for ft, (f0, f1) in enumerate(F_TILES):
            plan.append((("w1", e, ft), NCH * (f1 - f0)))
        for dt in range(4):
            plan.append((("w2", e, dt), FCH * 512))
    return plan


WPLAN = wplan()
WOFFS = {}
_cur = 0
for _key, _n in WPLAN:
    WOFFS[_key] = (_cur, _n)
    _cur += 128 * _n
WTOTAL = _cur


def _lhsT_slab(W, c0, c1):
    K = W.shape[0] // 128
    return np.ascontiguousarray(
        W.reshape(K, 128, W.shape[1])[:, :, c0:c1].transpose(1, 0, 2).reshape(128, -1)
    )


def pack_wstream(inp):
    w1_all = [np.asarray(inp["L0_ffn_shared_w1"][i], np.float32) for i in range(2)] + [
        np.asarray(inp["L0_ffn_routed_w1"][i], np.float32) for i in range(NRT)]
    w2_all = [np.asarray(inp["L0_ffn_shared_w2"][i], np.float32) for i in range(2)] + [
        np.asarray(inp["L0_ffn_routed_w2"][i], np.float32) for i in range(NRT)]
    Wk = np.asarray(inp["L0_attn_Wk"], np.float32)
    Wq = np.asarray(inp["L0_attn_Wq"], np.float32)
    Wv = np.asarray(inp["L0_attn_Wv"], np.float32)
    Wp = np.asarray(inp["L0_attn_Wp"], np.float32)
    out = np.empty(WTOTAL, np.float32)
    for key, n in WPLAN:
        off, _ = WOFFS[key]
        if key[0] == "wk":
            sl = _lhsT_slab(Wk, key[1] * 128, (key[1] + 1) * 128)
        elif key[0] == "wq":
            sl = _lhsT_slab(Wq, key[1] * 128, (key[1] + 1) * 128)
        elif key[0] == "wv":
            sl = _lhsT_slab(Wv, key[1] * 128, (key[1] + 2) * 128)
        elif key[0] == "wp":
            sl = Wp[key[1] * 128:(key[1] + 1) * 128, :]
        elif key[0] == "w1":
            f0, f1 = F_TILES[key[2]]
            sl = _lhsT_slab(w1_all[key[1]], f0, f1)
        else:
            sl = _lhsT_slab(w2_all[key[1]], key[2] * 512, (key[2] + 1) * 512)
        out[off:off + 128 * n] = sl.reshape(-1)
    return out


class Plan:
    def __init__(self):
        self.cnt = {s: 0 for s in SEMS}
        self.ops = {e: [] for e in ("tensor", "scalar", "vector", "gpsimd", "sync")}
        self.waited = {e: {} for e in self.ops}

    def on(self, eng, fn, wait=(), inc=None, n=1):
        ws = []
        for t in wait:
            if t is None:
                continue
            s, v = t
            if self.waited[eng].get(s, 0) >= v:
                continue
            ws.append((s, v))
            self.waited[eng][s] = v
        tok = None
        if inc is not None:
            self.cnt[inc] += n
            tok = (inc, self.cnt[inc])
        self.ops[eng].append((ws, fn, inc, n))
        return tok

    def barrier_toks(self):
        return [(s, v) for s, v in self.cnt.items() if v > 0]


class Buf:
    def __init__(self):
        self.wrd = {}
        self.rdd = {}

    @staticmethod
    def _m(d, tok):
        if tok is None:
            return
        s, v = tok
        if d.get(s, 0) < v:
            d[s] = v

    def r(self):
        return list(self.wrd.items())

    def w(self):
        return list(self.wrd.items()) + list(self.rdd.items())

    def did_r(self, tok):
        self._m(self.rdd, tok)

    def did_w(self, tok):
        self._m(self.wrd, tok)

    def seed(self, toks):
        for t in toks:
            self._m(self.rdd, t)


def build_nc(dbg=False, phases=("p0", "p1", "p2")):
    nc = bass.Bass("TRN2", target_bir_lowering=False)

    d_x = nc.dram_tensor("xin", [LCH, 128, DM], F32, kind="ExternalInput").ap()
    d_w = nc.dram_tensor("wstream", [WTOTAL], F32, kind="ExternalInput").ap()
    d_bqt = nc.dram_tensor("bqt", [128, H], F32, kind="ExternalInput").ap()
    d_bkt = nc.dram_tensor("bkt", [128, H], F32, kind="ExternalInput").ap()
    d_bvt = nc.dram_tensor("bvt", [128, H], F32, kind="ExternalInput").ap()
    d_b1t = nc.dram_tensor("b1t", [128, NEXP, FCH], F32, kind="ExternalInput").ap()
    d_bgr = nc.dram_tensor("bgrep", [128, NRT], F32, kind="ExternalInput").ap()
    d_b2s = nc.dram_tensor("b2s", [19, DM], F32, kind="ExternalInput").ap()
    d_cen = nc.dram_tensor("cen", [NRT, DM], F32, kind="ExternalInput").ap()
    d_idf = nc.dram_tensor("identf", [128, 128], F32, kind="ExternalInput").ap()
    d_msk = nc.dram_tensor("msk8", [128, 32], F32, kind="ExternalInput").ap()
    d_dgm = nc.dram_tensor("diagm", [128, 128], F32, kind="ExternalInput").ap()
    d_y = nc.dram_tensor("y", [OWN, 128, DM], F32, kind="ExternalOutput").ap()

    def wsrc(key):
        off, n = WOFFS[key]
        return d_w[off:off + 128 * n].rearrange("(p x) -> p x", p=128)

    dbg_outs = {}
    if dbg:
        dbg_outs["dacc0"] = nc.dram_tensor("dacc0", [OWN, 128, DM], F32, kind="ExternalOutput").ap()
        dbg_outs["dacc1"] = nc.dram_tensor("dacc1", [OWN, 128, DM], F32, kind="ExternalOutput").ap()
        dbg_outs["dgat"] = nc.dram_tensor("dgat", [128, OWN, NRT], F32, kind="ExternalOutput").ap()
        dbg_outs["dxnf"] = nc.dram_tensor("dxnf", [128, NCH, 2048], BF16, kind="ExternalOutput").ap()

    probe = nc.alloc_sbuf_tensor("base_probe", [128, 32], F32)
    BASE = nc.lookup_mloc(probe).addr + 32

    cur = [BASE]

    def alloc(name, shape, dtype, off=None):
        if off is None:
            off = cur[0]
            per = int(np.prod(shape[1:])) * mybir.dt.size(dtype)
            cur[0] = (off + per + 31) & ~31
        return nc.alloc_sbuf_tensor_at(name, shape, dtype, offset=off)

    acc = alloc("acc", [128, OWN, DM], F32)
    xnf_own = alloc("xnf_own", [128, NCH, 1024], BF16)
    gat = alloc("gat", [128, OWN, NRT], F32)
    stab = alloc("stab", [128, OWN, 2], F32)
    G2 = alloc("G2", [32, 1024], F32)
    b1t = alloc("b1t", [128, NEXP, FCH], F32)
    bqt = alloc("bqt", [128, H], F32)
    bkt = alloc("bkt", [128, H], F32)
    bvt = alloc("bvt", [128, H], F32)
    identf = alloc("identf", [128, 128], F32)
    bgr = alloc("bgr", [128, NRT], F32)
    chl = alloc("chl", [128, NCH, 32], BF16)
    mv = alloc("mv", [128, 2, 2], F32)
    bno = alloc("bno", [128, 2, 4, 6], F32)
    vt = alloc("vt", [128, 2, 4], F32)
    m8 = alloc("m8", [128, 8], F32)
    sg = alloc("sg", [128, NRT], F32)
    gsc = alloc("gsc", [128, 2], F32)
    amx = alloc("amx", [128, 3, 8], F32)
    dgm = alloc("dgm", [128, 128], F32)

    SIB = (cur[0] + 31) & ~31
    xnf_sib = alloc("xnf_sib", [128, NCH, 1024], BF16, off=SIB)
    b2s = alloc("b2s", [19, DM], F32, off=SIB)
    w2b = alloc("w2b", [128, 2, FCH, 512], BF16, off=SIB + 8192)
    OVL = SIB + 32768

    xstage = alloc("xstage", [128, 2, DM], F32, off=OVL + 0)
    xnst = alloc("xnst", [128, DM], F32, off=OVL + 16384)
    xnbf = alloc("xnbf", [128, 2, DM], BF16, off=OVL + 24576)
    lobf = alloc("lobf", [128, 2, DM], BF16, off=OVL + 32768)
    lofm = alloc("lofm", [128, NCH, 128], BF16, off=OVL + 40960)
    cstage = alloc("cstage", [NRT, DM], F32, off=OVL + 45056)

    kbuf = alloc("kbuf", [128, 2, 2048], BF16, off=OVL + 0)
    qbuf = alloc("qbuf", [128, 2, 1024], BF16, off=OVL + 8192)
    vbuf = alloc("vbuf", [128, LCH, 256], BF16, off=OVL + 12288)
    wrA = alloc("wrA", [128, NCH, 128], BF16, off=OVL + 20480)
    wrB = alloc("wrB", [128, NCH, 128], BF16, off=OVL + 24576)
    wvb = alloc("wvb", [128, NCH, 256], BF16, off=OVL + 28672)
    atT = alloc("atT", [128, 16, 128], BF16, off=OVL + 28672)
    scob = alloc("scob", [128, 3, 2048], F32, off=OVL + 36864)
    cfm = alloc("cfm", [128, NCH, NRT], F32, off=OVL + 53248)
    prob = alloc("prob", [128, 2, 2048], BF16, off=OVL + 60416)
    outg = alloc("outg", [128, 2, 1024], BF16, off=OVL + 68608)
    msk8 = alloc("msk8", [128, 32], F32, off=OVL + 72704)

    hbuf = alloc("hbuf", [128, FCH, 1024], BF16, off=OVL + 0)
    w1b = alloc("w1b", [128, 2, NCH, 256], BF16, off=OVL + 22528)
    OVL_END = OVL + 71680
    assert OVL_END <= 229376 - 64, (BASE, OVL_END)

    pstack = contextlib.ExitStack()
    psF = [pstack.enter_context(nc.psum_tensor("psF%d" % i, [128, 512], F32))
           for i in range(8)]

    pl = Plan()
    bufs = {}

    def B_(name):
        if name not in bufs:
            bufs[name] = Buf()
        return bufs[name]

    def T(fn, wait=(), grp_end=True):
        return pl.on("tensor", fn, wait, inc="pe" if grp_end else None)

    def V(fn, wait=()):
        return pl.on("vector", fn, wait, inc="ve")

    def A(fn, wait=()):
        return pl.on("scalar", fn, wait, inc="ac")

    def PO(fn, wait=()):
        return pl.on("gpsimd", fn, wait, inc="po")

    def DMA(queue, stream, fn, wait=()):
        return pl.on(queue, fn, wait, inc=stream, n=16)

    for nm, src, dsth in (("bqt", d_bqt, bqt), ("bkt", d_bkt, bkt), ("bvt", d_bvt, bvt),
                          ("b1t", d_b1t, b1t), ("bgr", d_bgr, bgr),
                          ("identf", d_idf, identf),
                          ("cstage", d_cen, cstage), ("dgm", d_dgm, dgm)):
        t = DMA("sync", "dcc", lambda s, a=src, b=dsth: s.dma_start(out=b[:], in_=a))
        B_(nm).did_w(t)
    _cctot = ("dcc", pl.cnt["dcc"])
    for nm in ("bqt", "bkt", "bvt", "b1t", "bgr", "identf", "cstage", "dgm"):
        B_(nm).wrd = {}
        B_(nm).did_w(_cctot)

    t = V(lambda v: v.memset(G2[:], 1.0))
    B_("G2").did_w(t)

    for dc in range(NCH):
        ps, pb = psF[dc % 2], B_("ps%d" % (dc % 2))
        tk = T(lambda t_, dc=dc, ps=ps: t_.transpose(
            ps[0:128, 0:NRT], cstage[0:NRT, dc * 128:(dc + 1) * 128],
            identf[0:NRT, 0:NRT]),
            wait=B_("cstage").r() + B_("identf").r() + pb.w())
        pb.did_w(tk)
        B_("cstage").did_r(tk)
        tc_ = V(lambda v, dc=dc, ps=ps: v.tensor_copy(cfm[:, dc, :], ps[0:128, 0:NRT]),
                wait=[tk])
        pb.did_r(tc_)
        B_("cfm").did_w(tc_)
    th = V(lambda v: v.tensor_copy(chl[:, :, 0:NRT], cfm[:]), wait=B_("cfm").r())
    B_("chl").did_w(th)
    tl = V(lambda v: v.tensor_tensor(chl[:, :, NRT:32], cfm[:], chl[:, :, 0:NRT],
                                     ALU.subtract),
           wait=B_("cfm").r() + [th])
    B_("chl").did_w(tl)
    B_("cfm").did_r(tl)

    if "p0" in phases:
        for ct in list(range(OWN, LCH)) + list(range(OWN)):
            own = ct < OWN
            sb = ct % 2
            xs = B_("xstage%d" % sb)
            td = DMA("sync", "dxa" if sb == 0 else "dxb", lambda s, ct=ct, sb=sb:
                     s.dma_start(out=xstage[:, sb, :], in_=d_x[ct]), wait=xs.w())
            B_("xstage%d" % sb).wrd = {}
            B_("xstage%d" % sb).rdd = {}
            xs.did_w(td)
            tprev = B_("mv%d" % sb).w() + B_("bno%d" % sb).w()
            ts = []
            for i in range(4):
                ts.append(V(lambda v, sb=sb, i=i: v.bn_stats(
                    bno[:, sb, i, :], xstage[:, sb, i * 512:(i + 1) * 512]),
                    wait=[td] + tprev))
            bnb = B_("bno%d" % sb)
            bnb.wrd = {}
            bnb.rdd = {}
            for t_ in ts:
                bnb.did_w(t_)
            ta = V(lambda v, sb=sb: v.bn_aggr(mv[:, sb, :], bno[:, sb, :, :]),
                   wait=bnb.r() + B_("mv%d" % sb).w())
            B_("mv%d" % sb).wrd = {}
            B_("mv%d" % sb).rdd = {}
            B_("mv%d" % sb).did_w(ta)
            bnb.did_r(ta)
            t0 = V(lambda v, sb=sb: v.tensor_scalar(
                vt[:, sb, 0:1], mv[:, sb, 1:2], EPS, None, ALU.add),
                wait=[ta] + B_("vt%d" % sb).w())
            t1 = A(lambda a, sb=sb: a.activation(vt[:, sb, 1:2], vt[:, sb, 0:1], AF.Ln),
                   wait=[t0])
            t2 = A(lambda a, sb=sb: a.activation(vt[:, sb, 2:3], vt[:, sb, 1:2], AF.Exp,
                                                 scale=-0.5), wait=[t1])
            B_("vt%d" % sb).wrd = {}
            B_("vt%d" % sb).rdd = {}
            B_("vt%d" % sb).did_w(t2)
            if own:
                tsa = V(lambda v, ct=ct, sb=sb: v.tensor_scalar(
                    stab[:, ct, 0:1], vt[:, sb, 2:3], 1.0, None, ALU.add), wait=[t2])
                tsb = V(lambda v, ct=ct, sb=sb: v.scalar_tensor_tensor(
                    stab[:, ct, 1:2], mv[:, sb, 0:1], -1.0, vt[:, sb, 2:3],
                    ALU.mult, ALU.mult), wait=[t2, ta])
                B_("stab").did_w(tsa)
                B_("stab").did_w(tsb)
                B_("vt%d" % sb).did_r(tsb)
                B_("mv%d" % sb).did_r(tsb)
                tai = A(lambda a, ct=ct, sb=sb: a.activation(
                    acc[:, ct, :], xstage[:, sb, :], AF.Identity,
                    scale=stab[:, ct, 0:1], bias=stab[:, ct, 1:2]),
                    wait=[tsa, tsb, td])
                B_("acc").did_w(tai)
                B_("stab").did_r(tai)
                xs.did_r(tai)
                tx = A(lambda a, ct=ct, sb=sb: a.activation(
                    xnst[:], xstage[:, sb, :], AF.Identity,
                    scale=vt[:, sb, 2:3], bias=stab[:, ct, 1:2]),
                    wait=[t2, tsb, td] + B_("xnst").w())
                B_("xnst").wrd = {}
                B_("xnst").rdd = {}
                B_("xnst").did_w(tx)
                B_("mv%d" % sb).did_r(tx)
                B_("vt%d" % sb).did_r(tx)
                xs.did_r(tx)
                xb = B_("xnbf%d" % sb)
                tph = A(lambda a, sb=sb: a.activation(xnbf[:, sb, :], xnst[:], AF.Copy),
                        wait=[tx] + xb.w())
                xb.wrd = {}
                xb.rdd = {}
                xb.did_w(tph)
                lb = B_("lobf%d" % sb)
                tpl = V(lambda v, sb=sb: v.tensor_tensor(
                    lobf[:, sb, :], xnst[:], xnbf[:, sb, :], ALU.subtract),
                    wait=[tx, tph] + lb.w())
                lb.wrd = {}
                lb.rdd = {}
                lb.did_w(tpl)
                B_("xnst").did_r(tpl)
                xb.did_r(tpl)
            else:
                xb = B_("xnbf%d" % sb)
                tss = V(lambda v, sb=sb: v.scalar_tensor_tensor(
                    vt[:, sb, 3:4], mv[:, sb, 0:1], -1.0, vt[:, sb, 2:3],
                    ALU.mult, ALU.mult), wait=[t2, ta])
                B_("vt%d" % sb).did_w(tss)
                tph = A(lambda a, sb=sb: a.activation(
                    xnbf[:, sb, :], xstage[:, sb, :], AF.Identity,
                    scale=vt[:, sb, 2:3], bias=vt[:, sb, 3:4]),
                    wait=[tss, td] + xb.w())
                xb.wrd = {}
                xb.rdd = {}
                xb.did_w(tph)
                B_("mv%d" % sb).did_r(tph)
                B_("vt%d" % sb).did_r(tph)
                xs.did_r(tph)

            dstt = xnf_own if own else xnf_sib
            dstn = "xnf_own" if own else "xnf_sib"
            col = (ct % 8) * 128
            strm = "dta" if sb == 0 else "dtb"
            blkn = dstn + str((ct % 8) // 4)
            te = DMA("sync", strm, lambda s, sb=sb, dstt=dstt, col=col: s.dma_start(
                out=dstt[:, :, col:col + 128],
                in_=xnbf[:, sb, :].rearrange("p (c x) -> p c x", c=NCH),
                transpose=True), wait=[tph] + B_(blkn).w())
            xb.did_r(te)
            B_(blkn).did_w(te)

            if own:
                tlo = DMA("sync", "dsb" if sb else "dsa",
                          lambda s, sb=sb: s.dma_start(
                    out=lofm[:, :, :],
                    in_=lobf[:, sb, :].rearrange("p (c x) -> p c x", c=NCH),
                    transpose=True), wait=[tpl] + B_("lofm").w())
                lb.did_r(tlo)
                B_("lofm").wrd = {}
                B_("lofm").rdd = {}
                B_("lofm").did_w(tlo)
                lts = [tlo] * NCH
                mml = None
                for dc in range(NCH):
                    T(lambda t_, dc=dc, col=col: t_.matmul(
                        psF[0][:, 0:16], xnf_own[:, dc, col:col + 128],
                        chl[:, dc, 0:16], start=(dc == 0), stop=False),
                        wait=(B_("xnf_own%d" % ((ct % 8) // 4)).r() + B_("chl").r() +
                              (B_("ps0").w() if dc == 0 else [])),
                        grp_end=False)
                    T(lambda t_, dc=dc, col=col: t_.matmul(
                        psF[0][:, 0:16], xnf_own[:, dc, col:col + 128],
                        chl[:, dc, 16:32], start=False, stop=False), grp_end=False)
                for dc in range(NCH):
                    mml = T(lambda t_, dc=dc: t_.matmul(
                        psF[0][:, 0:16], lofm[:, dc, :], chl[:, dc, 0:16],
                        start=False, stop=(dc == NCH - 1)),
                        wait=[lts[dc]], grp_end=(dc == NCH - 1))
                B_("ps0").wrd = {}
                B_("ps0").rdd = {}
                B_("ps0").did_w(mml)
                B_("lofm").did_r(mml)
                t2g = V(lambda v: v.tensor_copy(sg[:], psF[0][:, 0:16]),
                        wait=[mml] + B_("sg").w())
                B_("ps0").did_r(t2g)
                t3g = A(lambda a: a.activation(sg[:], sg[:], AF.Sigmoid), wait=[t2g])
                t4g = V(lambda v: v.tensor_tensor(sg[:], sg[:], bgr[:], ALU.add),
                        wait=[t3g] + B_("bgr").r())
                B_("sg").wrd = {}
                B_("sg").rdd = {}
                B_("sg").did_w(t4g)
                t5g = V(lambda v: v.max(out=m8[:], in_=sg[:]), wait=[t4g] + B_("m8").w())
                B_("m8").wrd = {}
                B_("m8").rdd = {}
                B_("m8").did_w(t5g)
                t6g = V(lambda v, ct=ct: v.scalar_tensor_tensor(
                    gat[:, ct, :], sg[:], m8[:, 3:4], sg[:], ALU.is_ge, ALU.mult),
                    wait=[t5g])
                B_("gat").did_w(t6g)
                B_("m8").did_r(t6g)
                B_("sg").did_r(t6g)
                t7g = V(lambda v, ct=ct: v.reduce_sum(gsc[:, 0:1], gat[:, ct, :], axis=AX),
                        wait=[t6g] + B_("gsc").w())
                t8g = V(lambda v: v.tensor_scalar(gsc[:, 0:1], gsc[:, 0:1], 1e-8, None,
                                                  ALU.max), wait=[t7g])
                t9g = V(lambda v: v.reciprocal(gsc[:, 1:2], gsc[:, 0:1]), wait=[t8g])
                B_("gsc").wrd = {}
                B_("gsc").rdd = {}
                B_("gsc").did_w(t9g)
                tag = V(lambda v, ct=ct: v.tensor_scalar(
                    gat[:, ct, :], gat[:, ct, :], gsc[:, 1:2], None, ALU.mult),
                    wait=[t9g, t6g])
                B_("gat").did_w(tag)
                B_("gsc").did_r(tag)
                tk = T(lambda t_, ct=ct: t_.transpose(
                    psF[2][0:16, 0:128], gat[:, ct, :], identf[:]),
                    wait=[tag] + B_("ps2").w())
                B_("ps2").wrd = {}
                B_("ps2").rdd = {}
                B_("ps2").did_w(tk)
                B_("gat").did_r(tk)
                tg = V(lambda v, col=col: v.tensor_copy(
                    G2[0:16, col:col + 128], psF[2][0:16, 0:128]),
                    wait=[tk] + B_("G2").r())
                B_("ps2").did_r(tg)
                B_("G2").did_w(tg)

        if dbg:
            for ct in range(OWN):
                t_ = DMA("scalar", "ds", lambda s, ct=ct: s.dma_start(
                    out=dbg_outs["dacc0"][ct], in_=acc[:, ct, :]), wait=B_("acc").r())
                B_("acc").did_r(t_)
            for dc in range(NCH):
                t_ = DMA("scalar", "ds", lambda s, dc=dc: s.dma_start(
                    out=dbg_outs["dxnf"][:, dc, 0:1024], in_=xnf_own[:, dc, :]),
                    wait=B_("xnf_own0").r() + B_("xnf_own1").r())
                B_("xnf_own0").did_r(t_)
                B_("xnf_own1").did_r(t_)
                t_ = DMA("scalar", "ds", lambda s, dc=dc: s.dma_start(
                    out=dbg_outs["dxnf"][:, dc, 1024:2048], in_=xnf_sib[:, dc, :]),
                    wait=B_("xnf_sib0").r() + B_("xnf_sib1").r())
                B_("xnf_sib0").did_r(t_)
                B_("xnf_sib1").did_r(t_)
            t_ = DMA("scalar", "ds", lambda s: s.dma_start(out=dbg_outs["dgat"], in_=gat[:]),
                     wait=B_("gat").r())
            B_("gat").did_r(t_)
            for nm in ("acc", "xnf_own0", "xnf_own1", "xnf_sib0", "xnf_sib1", "gat"):
                B_(nm).rdd["ds"] = pl.cnt["ds"]

    def xtile(tt):
        return (xnf_own, tt * 512) if tt < 2 else (xnf_sib, (tt - 2) * 512)

    def xtbuf(tt):
        return ("xnf_own%d" % tt) if tt < 2 else ("xnf_sib%d" % (tt - 2))

    if "p1" in phases:
        bar = pl.barrier_toks()
        for nm in ("kbuf0", "kbuf1", "vbuf", "qbuf0", "qbuf1", "wrA", "wrB",
                   "wvpb", "scob0", "scob1", "scob2", "prob0", "prob1", "outg",
                   "msk8", "amx0", "amx1", "amx2"):
            B_(nm).seed(bar)
        tm = DMA("sync", "dcc", lambda s: s.dma_start(out=msk8[:], in_=d_msk),
                 wait=B_("msk8").w())
        B_("msk8").did_w(tm)

        def make_prep(h):
            ksl = h % 2
            th = []

            def t_issue():
                twk = DMA("gpsimd", "dwk", lambda s, h=h: s.dma_start(
                    out=wrA[:], in_=wsrc(("wk", h))), wait=B_("wrA").w())
                B_("wrA").wrd = {}
                B_("wrA").rdd = {}
                B_("wrA").did_w(twk)
                twq = DMA("gpsimd", "dwq", lambda s, h=h: s.dma_start(
                    out=wrB[:], in_=wsrc(("wq", h))), wait=B_("wrB").w())
                B_("wrB").wrd = {}
                B_("wrB").rdd = {}
                B_("wrB").did_w(twq)
                prep_tok[h] = (twk, twq)
            th.append(t_issue)

            def mk_ktile(tt):
                def t_k():
                    twk, _ = prep_tok[h]
                    kb = B_("kbuf%d" % ksl)
                    if tt == 0:
                        kold = kb.w()
                        kb.wrd = {}
                        kb.rdd = {}
                        kprep_old[h] = kold
                    xt, xo = xtile(tt)
                    bk = 4 + tt % 2
                    ps, pb = psF[bk], B_("ps%d" % bk)
                    mm = None
                    for dc in range(NCH):
                        mm = T(lambda t_, dc=dc, xt=xt, xo=xo, ps=ps: t_.matmul(
                            ps[:], wrA[:, dc, :], xt[:, dc, xo:xo + 512],
                            start=(dc == 0), stop=(dc == NCH - 1)),
                            wait=([twk] + B_(xtbuf(tt)).r() +
                                  (pb.w() if dc == 0 else [])),
                            grp_end=(dc == NCH - 1))
                    pb.wrd = {}
                    pb.rdd = {}
                    pb.did_w(mm)
                    B_(xtbuf(tt)).did_r(mm)
                    B_("wrA").did_r(mm)
                    kold = kprep_old[h]
                    if tt % 2 == 0:
                        te = A(lambda a, tt=tt, h=h, ksl=ksl, ps=ps: a.activation(
                            kbuf[:, ksl, tt * 512:(tt + 1) * 512], ps[:], AF.Identity,
                            bias=bkt[:, h:h + 1]),
                            wait=[mm] + (kold if tt == 0 else []) + B_("bkt").r())
                    else:
                        te = V(lambda v, tt=tt, h=h, ksl=ksl, ps=ps: v.tensor_scalar(
                            kbuf[:, ksl, tt * 512:(tt + 1) * 512], ps[:],
                            bkt[:, h:h + 1], None, ALU.add),
                            wait=[mm] + (kold if tt == 0 else []) + B_("bkt").r())
                    pb.did_r(te)
                    B_("kbuf%d" % ksl).did_w(te)
                return t_k
            for tt in range(4):
                th.append(mk_ktile(tt))

            def mk_qtile(tt):
                def t_q():
                    _, twq = prep_tok[h]
                    qb = B_("qbuf%d" % ksl)
                    if tt == 0:
                        qold = qb.w()
                        qb.wrd = {}
                        qb.rdd = {}
                        qprep_old[h] = qold
                    xt, xo = xtile(tt)
                    bk = 4 + tt % 2
                    ps, pb = psF[bk], B_("ps%d" % bk)
                    mm = None
                    for dc in range(NCH):
                        mm = T(lambda t_, dc=dc, xt=xt, xo=xo, ps=ps: t_.matmul(
                            ps[:], wrB[:, dc, :], xt[:, dc, xo:xo + 512],
                            start=(dc == 0), stop=(dc == NCH - 1)),
                            wait=[twq] + B_(xtbuf(tt)).r() +
                                 (pb.w() if dc == 0 else []),
                            grp_end=(dc == NCH - 1))
                    pb.wrd = {}
                    pb.rdd = {}
                    pb.did_w(mm)
                    B_(xtbuf(tt)).did_r(mm)
                    B_("wrB").did_r(mm)
                    qold = qprep_old[h]
                    te = A(lambda a, tt=tt, h=h, ksl=ksl, ps=ps: a.activation(
                        qbuf[:, ksl, tt * 512:(tt + 1) * 512], ps[:], AF.Identity,
                        bias=bqt[:, h:h + 1]),
                        wait=[mm] + (qold if tt == 0 else []) + B_("bqt").r())
                    pb.did_r(te)
                    B_("qbuf%d" % ksl).did_w(te)
                return t_q
            for tt in range(2):
                th.append(mk_qtile(tt))
            return th

        prep_tok = {}
        kprep_old = {}
        qprep_old = {}

        for h in range(H):
            hp = h % 2
            ksl = h % 2
            wv = B_("wvpb")
            kb = B_("kbuf%d" % ksl)
            qb = B_("qbuf%d" % ksl)
            vb = B_("vbuf")
            if h == 0:
                for th in make_prep(0):
                    th()
            nxt = make_prep(h + 1) if h + 1 < H else []

            tnm_tok = {}
            smx_tok = {}

            def plan_scores(tc, h=h, ksl=ksl):
                sl = tc % 3
                LA = (tc + 1) * 128
                L = LA + 1024
                kb = B_("kbuf%d" % ksl)
                qb = B_("qbuf%d" % ksl)
                sc = B_("scob%d" % sl)
                scold = sc.w()
                sc.wrd = {}
                sc.rdd = {}
                am = B_("amx%d" % sl)
                amold = am.w()
                am.wrd = {}
                am.rdd = {}
                for j in range(2):
                    ps, pb = psF[j % 2], B_("ps%d" % (j % 2))
                    mm = T(lambda t_, j=j, tc=tc, ksl=ksl, ps=ps: t_.matmul(
                        ps[:], qbuf[:, ksl, tc * 128:(tc + 1) * 128],
                        kbuf[:, ksl, 1024 + j * 512:1536 + j * 512],
                        start=True, stop=True),
                        wait=qb.r() + kb.r() + pb.w())
                    pb.wrd = {}
                    pb.rdd = {}
                    pb.did_w(mm)
                    kb.did_r(mm)
                    qb.did_r(mm)
                    te = A(lambda a, j=j, sl=sl, ps=ps: a.activation(
                        scob[:, sl, j * 512:(j + 1) * 512], ps[:],
                        AF.Identity, scale=SCALE, bias=msk8[:, 0:1]),
                        wait=[mm] + B_("msk8").r() + (scold if j == 0 else []))
                    pb.did_r(te)
                    B_("msk8").did_r(te)
                    sc.did_w(te)
                atiles = [(a0, min(a0 + 512, LA)) for a0 in range(0, LA, 512)]
                lastev = None
                for i, (a0, a1) in enumerate(atiles):
                    ps, pb = psF[i % 2], B_("ps%d" % (i % 2))
                    mm = T(lambda t_, a0=a0, a1=a1, tc=tc, ksl=ksl, ps=ps: t_.matmul(
                        ps[:, 0:a1 - a0], qbuf[:, ksl, tc * 128:(tc + 1) * 128],
                        kbuf[:, ksl, a0:a1], start=True, stop=True),
                        wait=pb.w())
                    pb.wrd = {}
                    pb.rdd = {}
                    pb.did_w(mm)
                    qb.did_r(mm)
                    kb.did_r(mm)
                    if i == len(atiles) - 1:
                        te = V(lambda v, a0=a0, a1=a1, sl=sl, ps=ps: v.tensor_scalar(
                            scob[:, sl, 1024 + a0:1024 + a1], ps[:, 0:a1 - a0],
                            SCALE, None, ALU.mult), wait=[mm])
                    else:
                        te = A(lambda a, a0=a0, a1=a1, sl=sl, ps=ps: a.activation(
                            scob[:, sl, 1024 + a0:1024 + a1], ps[:, 0:a1 - a0],
                            AF.Identity, scale=SCALE), wait=[mm])
                    pb.did_r(te)
                    sc.did_w(te)
                    lastev = te
                tdg = V(lambda v, tc=tc, sl=sl: v.tensor_tensor(
                    scob[:, sl, 1024 + tc * 128:1024 + (tc + 1) * 128],
                    scob[:, sl, 1024 + tc * 128:1024 + (tc + 1) * 128],
                    dgm[:], ALU.add), wait=[lastev] + B_("dgm").r())
                sc.did_w(tdg)
                B_("dgm").did_r(tdg)
                tmx = V(lambda v, L=L, sl=sl: v.reduce_max(
                    amx[:, sl, 0:1], scob[:, sl, 0:L], axis=AX, negate=True),
                    wait=sc.w() + amold)
                sc.did_r(tmx)
                am.did_w(tmx)
                smx_tok[tc] = tmx

            def plan_scores_b(tc, h=h):
                sl = tc % 3
                L = (tc + 9) * 128
                sc = B_("scob%d" % sl)
                am = B_("amx%d" % sl)
                pr = B_("prob%d" % (tc % 2))
                prold = pr.w()
                pr.wrd = {}
                pr.rdd = {}
                tex = A(lambda a, L=L, sl=sl, pp=tc % 2: a.activation(
                    prob[:, pp, 0:L], scob[:, sl, 0:L], AF.Exp, bias=amx[:, sl, 0:1],
                    scale=1.0, accum_out=amx[:, sl, 2:3]),
                    wait=[smx_tok[tc]] + prold)
                pr.did_w(tex)
                sc.did_r(tex)
                am.did_w(tex)
                trc = V(lambda v, sl=sl: v.reciprocal(amx[:, sl, 3:4], amx[:, sl, 2:3]),
                        wait=[tex])
                am.did_w(trc)
                tnm = V(lambda v, L=L, sl=sl, pp=tc % 2: v.tensor_scalar(
                    prob[:, pp, 0:L], prob[:, pp, 0:L], amx[:, sl, 3:4], None, ALU.mult),
                    wait=[trc, tex])
                pr.did_w(tnm)
                am.did_r(tnm)
                tnm_tok[tc] = tnm

            def plan_transposes(tc, h=h):
                sl = tc % 2
                pr = B_("prob%d" % sl)
                strm = "dta" if tc % 2 == 0 else "dtb"
                t1 = DMA("sync", strm, lambda s, tc=tc, sl=sl: s.dma_start(
                    out=atT[:, 0:tc + 9, 0:128],
                    in_=prob[:, sl, 0:(tc + 9) * 128].rearrange(
                        "p (c x) -> p c x", c=tc + 9),
                    transpose=True),
                    wait=[tnm_tok[tc]] + wv.w())
                wv.did_w(t1)
                pr.did_r(t1)

            def plan_av(tc, h=h, hp=hp):
                out_ps = psF[6 + tc % 2]
                opb = B_("ps%d" % (6 + tc % 2))
                opold = opb.w()
                opb.wrd = {}
                opb.rdd = {}
                order = [("sib", j) for j in range(8)] + \
                        [("own", i) for i in range(tc + 1)]
                mmf = None
                for oi, (reg, i) in enumerate(order):
                    slot = i if reg == "sib" else 8 + i
                    lci = 8 + i if reg == "sib" else i
                    mmf = T(lambda t_, lci=lci, hp=hp, slot=slot, oi=oi,
                            out_ps=out_ps, n=len(order): t_.matmul(
                        out_ps[:, 0:128], vbuf[:, lci, hp * 128:(hp + 1) * 128],
                        atT[:, slot, 0:128], start=(oi == 0), stop=(oi == n - 1)),
                        wait=wv.r() + vb.r() + (opold if oi == 0 else []),
                        grp_end=(oi == len(order) - 1))
                    wv.did_r(mmf)
                    vb.did_r(mmf)
                opb.did_w(mmf)
                toe = A(lambda a, hp=hp, tc=tc, h=h, out_ps=out_ps: a.activation(
                    outg[:, hp, tc * 128:(tc + 1) * 128], out_ps[:, 0:128],
                    AF.Identity, bias=bvt[:, h:h + 1]),
                    wait=[mmf] + (B_("outg").w() if (hp == 0 and tc == 0) else []) +
                         B_("bvt").r())
                opb.did_r(toe)
                if hp == 0 and tc == 0:
                    B_("outg").wrd = {}
                    B_("outg").rdd = {}
                B_("outg").did_w(toe)

            plan_scores(0)
            plan_scores(1)
            plan_scores_b(0)
            plan_scores(2)

            if hp == 0:
                twv = DMA("gpsimd", "dwv", lambda s, h=h: s.dma_start(
                    out=wvb[:], in_=wsrc(("wv", h))), wait=wv.w())
                wv.wrd = {}
                wv.rdd = {}
                wv.did_w(twv)
                vold = vb.w()
                vb.wrd = {}
                vb.rdd = {}
                for lc in range(LCH):
                    xt = xnf_own if lc < 8 else xnf_sib
                    xo = (lc % 8) * 128
                    bk = 4 + lc % 2
                    ps, pb = psF[bk], B_("ps%d" % bk)
                    mm = None
                    for dc in range(NCH):
                        mm = T(lambda t_, dc=dc, xt=xt, xo=xo, ps=ps: t_.matmul(
                            ps[:, 0:256], xt[:, dc, xo:xo + 128], wvb[:, dc, :],
                            start=(dc == 0), stop=(dc == NCH - 1)),
                            wait=[twv] + B_(xtbuf(lc // 4)).r() +
                                 (pb.w() if dc == 0 else []),
                            grp_end=(dc == NCH - 1))
                    pb.wrd = {}
                    pb.rdd = {}
                    pb.did_w(mm)
                    B_(xtbuf(lc // 4)).did_r(mm)
                    wv.did_r(mm)
                    if lc % 2 == 0:
                        te = A(lambda a, lc=lc, ps=ps: a.activation(
                            vbuf[:, lc, :], ps[:, 0:256], AF.Copy),
                            wait=[mm] + (vold if lc == 0 else []))
                    else:
                        te = V(lambda v, lc=lc, ps=ps: v.tensor_copy(
                            vbuf[:, lc, :], ps[:, 0:256]),
                            wait=[mm] + (vold if lc == 0 else []))
                    pb.did_r(te)
                    vb.did_w(te)

            for tc in range(OWN):
                if tc + 1 < OWN:
                    plan_scores_b(tc + 1)
                if tc + 3 < OWN:
                    plan_scores(tc + 3)
                if nxt:
                    nxt.pop(0)()
                plan_transposes(tc)
                plan_av(tc)
            while nxt:
                nxt.pop(0)()

            if hp == 1:
                g2i = h // 2
                wps = [kbuf[:, 1, :], atT[:].rearrange("p c x -> p (c x)")]
                wpB = [B_("kbuf1"), wv]
                tws = []
                for hc in range(2):
                    wpo = wpB[hc].w()
                    wpB[hc].wrd = {}
                    wpB[hc].rdd = {}
                    tw = DMA("gpsimd", "dwp", lambda s, g2i=g2i, hc=hc: s.dma_start(
                        out=wps[hc], in_=wsrc(("wp", g2i * 2 + hc))),
                        wait=wpo)
                    wpB[hc].did_w(tw)
                    tws.append(tw)
                wptok = tws[1]
                for hc in range(2):
                    wpB[hc].wrd["dwp"] = pl.cnt["dwp"]
                for tcq in range(OWN):
                    mms = []
                    for hc in range(2):
                        for dt in range(4):
                            bk = (tcq % 2) * 4 + dt
                            ps, pb = psF[bk], B_("ps%d" % bk)
                            mm = T(lambda t_, hc=hc, tcq=tcq, dt=dt, ps=ps: t_.matmul(
                                ps[:], outg[:, hc, tcq * 128:(tcq + 1) * 128],
                                wps[hc][:, dt * 512:(dt + 1) * 512],
                                start=(hc == 0), stop=(hc == 1)),
                                wait=[wptok] + B_("outg").r() +
                                     (pb.w() if hc == 0 else []),
                                grp_end=(hc == 1))
                            if hc == 0:
                                pb.wrd = {}
                                pb.rdd = {}
                            if mm is not None:
                                pb.did_w(mm)
                                wpB[0].did_r(mm)
                                wpB[1].did_r(mm)
                                B_("outg").did_r(mm)
                                mms.append((dt, ps, pb, mm))
                    for dt, ps, pb, mm in mms:
                        tev = V(lambda v, tcq=tcq, dt=dt, ps=ps: v.tensor_tensor(
                            acc[:, tcq, dt * 512:(dt + 1) * 512],
                            acc[:, tcq, dt * 512:(dt + 1) * 512], ps[:], ALU.add),
                            wait=[mm] + B_("acc").w())
                        pb.did_r(tev)
                        B_("acc").did_w(tev)

        if dbg:
            for ct in range(OWN):
                t_ = DMA("scalar", "ds", lambda s, ct=ct: s.dma_start(
                    out=dbg_outs["dacc1"][ct], in_=acc[:, ct, :]), wait=B_("acc").r())
                B_("acc").did_r(t_)
            B_("acc").rdd["ds"] = pl.cnt["ds"]

    if "p2" in phases:
        bar = pl.barrier_toks()
        for nm in ("hbuf", "w1b0", "w1b1", "w2b0", "w2b1", "b2sB"):
            B_(nm).seed(bar)
        tb2 = DMA("sync", "dcc", lambda s: s.dma_start(out=b2s[:], in_=d_b2s),
                  wait=B_("b2sB").w() + B_("xnf_sib0").w() + B_("xnf_sib1").w())
        B_("b2sB").did_w(tb2)
        for tcq in range(OWN):
            for dt in range(4):
                ps, pb = psF[(tcq * 4 + dt) % 4], B_("ps%d" % ((tcq * 4 + dt) % 4))
                mm = T(lambda t_, tcq=tcq, dt=dt, ps=ps: t_.matmul(
                    ps[:], G2[0:19, tcq * 128:(tcq + 1) * 128],
                    b2s[0:19, dt * 512:(dt + 1) * 512], start=True, stop=True),
                    wait=[tb2] + B_("G2").r() + pb.w())
                pb.wrd = {}
                pb.rdd = {}
                pb.did_w(mm)
                B_("G2").did_r(mm)
                te = V(lambda v, tcq=tcq, dt=dt, ps=ps: v.tensor_tensor(
                    acc[:, tcq, dt * 512:(dt + 1) * 512],
                    acc[:, tcq, dt * 512:(dt + 1) * 512], ps[:], ALU.add),
                    wait=[mm] + B_("acc").w())
                pb.did_r(te)
                B_("acc").did_w(te)

        for e in range(NEXP):
            hb = B_("hbuf")
            hold = hb.w()
            hb.wrd = {}
            hb.rdd = {}
            first_h = True
            for ft, (f0, f1) in enumerate(F_TILES):
                wslot = ft % 2
                wb = B_("w1b%d" % wslot)
                tw = DMA("gpsimd", "dw1a" if wslot == 0 else "dw1b",
                         lambda s, e=e, ft=ft, wslot=wslot, w=f1 - f0:
                         s.dma_start(out=w1b[:, wslot, :, 0:w], in_=wsrc(("w1", e, ft))),
                         wait=wb.w())
                wb.wrd = {}
                wb.rdd = {}
                wb.did_w(tw)
                nfc = (f1 - f0) // 128
                for fci in range(nfc):
                    Fg = (f0 // 128) + fci
                    bk0, bk1 = (6, 7) if Fg % 2 else (0, 1)
                    ps0, ps1 = psF[bk0], psF[bk1]
                    pb0, pb1 = B_("ps%d" % bk0), B_("ps%d" % bk1)
                    mm0 = mm1 = None
                    for dc in range(NCH):
                        mm0 = T(lambda t_, dc=dc, fci=fci, wslot=wslot, ps=ps0: t_.matmul(
                            ps[:], w1b[:, wslot, dc, fci * 128:(fci + 1) * 128],
                            xnf_own[:, dc, 0:512],
                            start=(dc == 0), stop=(dc == NCH - 1)),
                            wait=[tw] + B_("xnf_own0").r() +
                                 (pb0.w() if dc == 0 else []),
                            grp_end=(dc == NCH - 1))
                        mm1 = T(lambda t_, dc=dc, fci=fci, wslot=wslot, ps=ps1: t_.matmul(
                            ps[:], w1b[:, wslot, dc, fci * 128:(fci + 1) * 128],
                            xnf_own[:, dc, 512:1024],
                            start=(dc == 0), stop=(dc == NCH - 1)),
                            wait=B_("xnf_own1").r() + (pb1.w() if dc == 0 else []),
                            grp_end=(dc == NCH - 1))
                    pb0.wrd = {}
                    pb0.rdd = {}
                    pb0.did_w(mm0)
                    pb1.wrd = {}
                    pb1.rdd = {}
                    pb1.did_w(mm1)
                    wb.did_r(mm1)
                    for tt, (mm, ps, pb) in enumerate(((mm0, ps0, pb0), (mm1, ps1, pb1))):
                        te = A(lambda a, Fg=Fg, tt=tt, e=e, ps=ps: a.activation(
                            hbuf[:, Fg, tt * 512:(tt + 1) * 512], ps[:], AF.Relu,
                            bias=b1t[:, e, Fg:Fg + 1]),
                            wait=[mm] + (hold if first_h else []) + B_("b1t").r())
                        first_h = False
                        pb.did_r(te)
                        hb.did_w(te)
            for dt in range(4):
                wslot = dt % 2
                wb = B_("w2b%d" % wslot)
                tw = DMA("gpsimd", "dw2a" if wslot == 0 else "dw2b",
                         lambda s, e=e, dt=dt, wslot=wslot: s.dma_start(
                    out=w2b[:, wslot, :, :], in_=wsrc(("w2", e, dt))), wait=wb.w())
                wb.wrd = {}
                wb.rdd = {}
                wb.did_w(tw)
                for tcq in range(OWN):
                    bk = 2 + (tcq % 4)
                    ps, pb = psF[bk], B_("ps%d" % bk)
                    mm = None
                    for fc in range(FCH):
                        mm = T(lambda t_, fc=fc, tcq=tcq, wslot=wslot, ps=ps: t_.matmul(
                            ps[:], hbuf[:, fc, tcq * 128:(tcq + 1) * 128],
                            w2b[:, wslot, fc, :], start=(fc == 0), stop=(fc == FCH - 1)),
                            wait=[tw] + hb.r() + (pb.w() if fc == 0 else []),
                            grp_end=(fc == FCH - 1))
                    pb.wrd = {}
                    pb.rdd = {}
                    pb.did_w(mm)
                    wb.did_r(mm)
                    hb.did_r(mm)
                    if e < 2:
                        te = V(lambda v, tcq=tcq, dt=dt, ps=ps: v.tensor_tensor(
                            acc[:, tcq, dt * 512:(dt + 1) * 512],
                            acc[:, tcq, dt * 512:(dt + 1) * 512], ps[:], ALU.add),
                            wait=[mm] + B_("acc").w())
                    else:
                        te = V(lambda v, tcq=tcq, dt=dt, e=e, ps=ps:
                               v.scalar_tensor_tensor(
                            acc[:, tcq, dt * 512:(dt + 1) * 512], ps[:],
                            gat[:, tcq, e - 2:e - 1],
                            acc[:, tcq, dt * 512:(dt + 1) * 512], ALU.mult, ALU.add),
                            wait=[mm] + B_("acc").w() + B_("gat").r())
                        B_("gat").did_r(te)
                    pb.did_r(te)
                    B_("acc").did_w(te)

    last = None
    for tcq in range(OWN):
        last = DMA("scalar", "ds", lambda s, tcq=tcq: s.dma_start(
            out=d_y[tcq], in_=acc[:, tcq, :]), wait=B_("acc").r())
        B_("acc").did_r(last)
    pl.on("scalar", None, [last])

    sstack = contextlib.ExitStack()
    semh = {s: sstack.enter_context(nc.semaphore(s)) for s in SEMS}

    with nc.Block() as blk:
        def emit(ops):
            def _f(eng):
                for ws, fn, inc, n in ops:
                    for s, v in ws:
                        eng.wait_ge(semh[s], v)
                    if fn is not None:
                        instr = fn(eng)
                        if inc is not None:
                            instr.then_inc(semh[inc], n)
            return _f

        blk.tensor(emit(pl.ops["tensor"]))
        blk.scalar(emit(pl.ops["scalar"]))
        blk.vector(emit(pl.ops["vector"]))
        blk.gpsimd(emit(pl.ops["gpsimd"]))
        blk.sync(emit(pl.ops["sync"]))

    sstack.close()
    pstack.close()
    return nc, dbg_outs


def make_inputs(inp):
    x = np.asarray(inp["L0_attn_x"], np.float32)
    wstream = pack_wstream(inp)
    bqt = np.ascontiguousarray(np.asarray(inp["L0_attn_bq"], np.float32).reshape(H, 128).T)
    bkt = np.ascontiguousarray(np.asarray(inp["L0_attn_bk"], np.float32).reshape(H, 128).T)
    bvt = np.ascontiguousarray(np.asarray(inp["L0_attn_bv"], np.float32).reshape(H, 128).T)
    b1_all = np.concatenate([np.asarray(inp["L0_ffn_shared_b1"], np.float32),
                             np.asarray(inp["L0_ffn_routed_b1"], np.float32)], 0)
    b1t = np.ascontiguousarray(b1_all.reshape(NEXP, FCH, 128).transpose(2, 0, 1))
    bgrep = np.ascontiguousarray(
        np.broadcast_to(np.asarray(inp["L0_ffn_bias"], np.float32)[None, :], (128, NRT)))
    b2s = np.ascontiguousarray(np.concatenate([
        np.asarray(inp["L0_ffn_routed_b2"], np.float32),
        np.asarray(inp["L0_attn_bp"], np.float32)[None, :],
        np.asarray(inp["L0_ffn_shared_b2"], np.float32)], 0))
    cen = np.ascontiguousarray(np.asarray(inp["L0_ffn_centroids"], np.float32))
    identf = np.eye(128, dtype=np.float32)
    jj = np.arange(128)
    diagm = np.where(jj[None, :] <= jj[:, None], 0.0, NEG).astype(np.float32)

    in_maps = []
    for c in range(8):
        b, hh = c // 2, c % 2
        xb = x[b]
        xl = xb if hh == 0 else np.concatenate([xb[1024:], xb[:1024]], 0)
        xl = np.ascontiguousarray(xl.reshape(LCH, 128, DM))
        msk = np.full((128, 32), NEG if hh == 0 else 0.0, dtype=np.float32)
        in_maps.append({
            "xin": xl, "wstream": wstream, "bqt": bqt, "bkt": bkt, "bvt": bvt,
            "b1t": b1t, "bgrep": bgrep, "b2s": b2s, "cen": cen,
            "identf": identf, "msk8": msk, "diagm": diagm,
        })
    return in_maps


def kernel(**inputs):
    nc, _ = build_nc(dbg=False)
    in_maps = make_inputs(inputs)
    res = run_bass_kernel_spmd(nc, in_maps, core_ids=list(range(8)))
    out = np.empty((B, S, DM), np.float32)
    for c in range(8):
        b, hh = c // 2, c % 2
        y = res.results[c]["y"].reshape(1024, DM)
        out[b, hh * 1024:(hh + 1) * 1024, :] = y
    return out


# version 16
# speedup vs baseline: 1.0803x; 1.0266x over previous
import contextlib
import math
import numpy as np
import ml_dtypes
import concourse.bass as bass
import concourse.mybir as mybir
from concourse.bass_utils import run_bass_kernel_spmd
from concourse.alu_op_type import AluOpType

F32 = mybir.dt.float32
BF16 = mybir.dt.bfloat16
AF = mybir.ActivationFunctionType
AX = mybir.AxisListType.X
ALU = AluOpType
BF = ml_dtypes.bfloat16

B, S, DM, H, HD = 4, 2048, 2048, 16, 128
NCH = DM // 128
LCH = 16
OWN = 8
SCALE = 1.0 / math.sqrt(2048.0)
EPS = 1e-5
DFF = 1408
FCH = DFF // 128
NEXP = 18
NRT = 16
NEG = -1.0e30
F_TILES = [(0, 256), (256, 512), (512, 768), (768, 1024), (1024, 1280), (1280, 1408)]
SEMS = ("pe", "ac", "ve", "po", "dxa", "dxb", "dcc", "dwk", "dwq", "dwv", "dwp", "dw1a", "dw1b", "dw2a", "dw2b", "ds", "dta", "dtb", "dsa", "dsb")


def wplan():
    plan = []
    for h in range(H):
        if h % 2 == 0:
            plan.append((("wv", h), NCH * 256))
        plan.append((("wk", h), NCH * 128))
        plan.append((("wq", h), NCH * 128))
    for hc in range(H):
        plan.append((("wp", hc), DM))
    for e in range(NEXP):
        for ft, (f0, f1) in enumerate(F_TILES):
            plan.append((("w1", e, ft), NCH * (f1 - f0)))
        for dt in range(4):
            plan.append((("w2", e, dt), FCH * 512))
    return plan


WPLAN = wplan()
WOFFS = {}
_cur = 0
for _key, _n in WPLAN:
    WOFFS[_key] = (_cur, _n)
    _cur += 128 * _n
WTOTAL = _cur


def _lhsT_slab(W, c0, c1):
    K = W.shape[0] // 128
    return np.ascontiguousarray(
        W.reshape(K, 128, W.shape[1])[:, :, c0:c1].transpose(1, 0, 2).reshape(128, -1)
    )


def pack_wstream(inp):
    w1_all = [np.asarray(inp["L0_ffn_shared_w1"][i], np.float32) for i in range(2)] + [
        np.asarray(inp["L0_ffn_routed_w1"][i], np.float32) for i in range(NRT)]
    w2_all = [np.asarray(inp["L0_ffn_shared_w2"][i], np.float32) for i in range(2)] + [
        np.asarray(inp["L0_ffn_routed_w2"][i], np.float32) for i in range(NRT)]
    Wk = np.asarray(inp["L0_attn_Wk"], np.float32)
    Wq = np.asarray(inp["L0_attn_Wq"], np.float32)
    Wv = np.asarray(inp["L0_attn_Wv"], np.float32)
    Wp = np.asarray(inp["L0_attn_Wp"], np.float32)
    out = np.empty(WTOTAL, np.float32)
    for key, n in WPLAN:
        off, _ = WOFFS[key]
        if key[0] == "wk":
            sl = _lhsT_slab(Wk, key[1] * 128, (key[1] + 1) * 128)
        elif key[0] == "wq":
            sl = _lhsT_slab(Wq, key[1] * 128, (key[1] + 1) * 128)
        elif key[0] == "wv":
            sl = _lhsT_slab(Wv, key[1] * 128, (key[1] + 2) * 128)
        elif key[0] == "wp":
            sl = Wp[key[1] * 128:(key[1] + 1) * 128, :]
        elif key[0] == "w1":
            f0, f1 = F_TILES[key[2]]
            sl = _lhsT_slab(w1_all[key[1]], f0, f1)
        else:
            sl = _lhsT_slab(w2_all[key[1]], key[2] * 512, (key[2] + 1) * 512)
        out[off:off + 128 * n] = sl.reshape(-1)
    return out


class Plan:
    def __init__(self):
        self.cnt = {s: 0 for s in SEMS}
        self.ops = {e: [] for e in ("tensor", "scalar", "vector", "gpsimd", "sync")}
        self.waited = {e: {} for e in self.ops}

    def on(self, eng, fn, wait=(), inc=None, n=1):
        ws = []
        for t in wait:
            if t is None:
                continue
            s, v = t
            if self.waited[eng].get(s, 0) >= v:
                continue
            ws.append((s, v))
            self.waited[eng][s] = v
        tok = None
        if inc is not None:
            self.cnt[inc] += n
            tok = (inc, self.cnt[inc])
        self.ops[eng].append((ws, fn, inc, n))
        return tok

    def barrier_toks(self):
        return [(s, v) for s, v in self.cnt.items() if v > 0]


class Buf:
    def __init__(self):
        self.wrd = {}
        self.rdd = {}

    @staticmethod
    def _m(d, tok):
        if tok is None:
            return
        s, v = tok
        if d.get(s, 0) < v:
            d[s] = v

    def r(self):
        return list(self.wrd.items())

    def w(self):
        return list(self.wrd.items()) + list(self.rdd.items())

    def did_r(self, tok):
        self._m(self.rdd, tok)

    def did_w(self, tok):
        self._m(self.wrd, tok)

    def seed(self, toks):
        for t in toks:
            self._m(self.rdd, t)


def build_nc(dbg=False, phases=("p0", "p1", "p2")):
    nc = bass.Bass("TRN2", target_bir_lowering=False)

    d_x = nc.dram_tensor("xin", [LCH, 128, DM], F32, kind="ExternalInput").ap()
    d_w = nc.dram_tensor("wstream", [WTOTAL], F32, kind="ExternalInput").ap()
    d_bqt = nc.dram_tensor("bqt", [128, H], F32, kind="ExternalInput").ap()
    d_bkt = nc.dram_tensor("bkt", [128, H], F32, kind="ExternalInput").ap()
    d_bvt = nc.dram_tensor("bvt", [128, H], F32, kind="ExternalInput").ap()
    d_b1t = nc.dram_tensor("b1t", [128, NEXP, FCH], F32, kind="ExternalInput").ap()
    d_bgr = nc.dram_tensor("bgrep", [128, NRT], F32, kind="ExternalInput").ap()
    d_b2s = nc.dram_tensor("b2s", [19, DM], F32, kind="ExternalInput").ap()
    d_cen = nc.dram_tensor("cen", [NRT, DM], F32, kind="ExternalInput").ap()
    d_idf = nc.dram_tensor("identf", [128, 128], F32, kind="ExternalInput").ap()
    d_msk = nc.dram_tensor("msk8", [128, 32], F32, kind="ExternalInput").ap()
    d_dgm = nc.dram_tensor("diagm", [128, 128], F32, kind="ExternalInput").ap()
    d_y = nc.dram_tensor("y", [OWN, 128, DM], F32, kind="ExternalOutput").ap()

    def wsrc(key):
        off, n = WOFFS[key]
        return d_w[off:off + 128 * n].rearrange("(p x) -> p x", p=128)

    dbg_outs = {}
    if dbg:
        dbg_outs["dacc0"] = nc.dram_tensor("dacc0", [OWN, 128, DM], F32, kind="ExternalOutput").ap()
        dbg_outs["dacc1"] = nc.dram_tensor("dacc1", [OWN, 128, DM], F32, kind="ExternalOutput").ap()
        dbg_outs["dgat"] = nc.dram_tensor("dgat", [128, OWN, NRT], F32, kind="ExternalOutput").ap()
        dbg_outs["dxnf"] = nc.dram_tensor("dxnf", [128, NCH, 2048], BF16, kind="ExternalOutput").ap()

    probe = nc.alloc_sbuf_tensor("base_probe", [128, 32], F32)
    BASE = nc.lookup_mloc(probe).addr + 32

    cur = [BASE]

    def alloc(name, shape, dtype, off=None):
        if off is None:
            off = cur[0]
            per = int(np.prod(shape[1:])) * mybir.dt.size(dtype)
            cur[0] = (off + per + 31) & ~31
        return nc.alloc_sbuf_tensor_at(name, shape, dtype, offset=off)

    acc = alloc("acc", [128, OWN, DM], F32)
    xnf_own = alloc("xnf_own", [128, NCH, 1024], BF16)
    gat = alloc("gat", [128, OWN, NRT], F32)
    stab = alloc("stab", [128, OWN, 2], F32)
    G2 = alloc("G2", [32, 1024], F32)
    b1t = alloc("b1t", [128, NEXP, FCH], F32)
    bqt = alloc("bqt", [128, H], F32)
    bkt = alloc("bkt", [128, H], F32)
    bvt = alloc("bvt", [128, H], F32)
    identf = alloc("identf", [128, 128], F32)
    bgr = alloc("bgr", [128, NRT], F32)
    chl = alloc("chl", [128, NCH, 32], BF16)
    mv = alloc("mv", [128, 2, 2], F32)
    bno = alloc("bno", [128, 2, 4, 6], F32)
    vt = alloc("vt", [128, 2, 4], F32)
    m8 = alloc("m8", [128, 8], F32)
    sg = alloc("sg", [128, NRT], F32)
    gsc = alloc("gsc", [128, 2], F32)
    amx = alloc("amx", [128, 3, 8], F32)
    dgm = alloc("dgm", [128, 128], F32)

    SIB = (cur[0] + 31) & ~31
    xnf_sib = alloc("xnf_sib", [128, NCH, 1024], BF16, off=SIB)
    b2s = alloc("b2s", [19, DM], F32, off=SIB)
    w2b = alloc("w2b", [128, 2, FCH, 512], BF16, off=SIB + 8192)
    OVL = SIB + 32768

    xstage = alloc("xstage", [128, 2, DM], F32, off=OVL + 0)
    xnst = alloc("xnst", [128, DM], F32, off=OVL + 16384)
    xnbf = alloc("xnbf", [128, 2, DM], BF16, off=OVL + 24576)
    lobf = alloc("lobf", [128, 2, DM], BF16, off=OVL + 32768)
    lofm = alloc("lofm", [128, NCH, 128], BF16, off=OVL + 40960)
    cstage = alloc("cstage", [NRT, DM], F32, off=OVL + 45056)

    kbuf = alloc("kbuf", [128, 2, 2048], BF16, off=OVL + 0)
    qbuf = alloc("qbuf", [128, 2, 1024], BF16, off=OVL + 8192)
    vbuf = alloc("vbuf", [128, LCH, 256], BF16, off=OVL + 12288)
    wrA = alloc("wrA", [128, NCH, 128], BF16, off=OVL + 20480)
    wrB = alloc("wrB", [128, NCH, 128], BF16, off=OVL + 24576)
    wvb = alloc("wvb", [128, NCH, 256], BF16, off=OVL + 28672)
    atT = alloc("atT", [128, 16, 128], BF16, off=OVL + 28672)
    scob = alloc("scob", [128, 3, 2048], F32, off=OVL + 36864)
    cfm = alloc("cfm", [128, NCH, NRT], F32, off=OVL + 53248)
    prob = alloc("prob", [128, 2, 2048], BF16, off=OVL + 60416)
    outg = alloc("outg", [128, 2, 1024], BF16, off=OVL + 68608)
    msk8 = alloc("msk8", [128, 32], F32, off=OVL + 72704)

    hbuf = alloc("hbuf", [128, FCH, 1024], BF16, off=OVL + 0)
    w1b = alloc("w1b", [128, 2, NCH, 256], BF16, off=OVL + 22528)
    OVL_END = OVL + 71680
    assert OVL_END <= 229376 - 64, (BASE, OVL_END)

    pstack = contextlib.ExitStack()
    psF = [pstack.enter_context(nc.psum_tensor("psF%d" % i, [128, 512], F32))
           for i in range(8)]

    pl = Plan()
    bufs = {}

    def B_(name):
        if name not in bufs:
            bufs[name] = Buf()
        return bufs[name]

    def T(fn, wait=(), grp_end=True):
        return pl.on("tensor", fn, wait, inc="pe" if grp_end else None)

    def V(fn, wait=()):
        return pl.on("vector", fn, wait, inc="ve")

    def A(fn, wait=()):
        return pl.on("scalar", fn, wait, inc="ac")

    def PO(fn, wait=()):
        return pl.on("gpsimd", fn, wait, inc="po")

    def DMA(queue, stream, fn, wait=()):
        return pl.on(queue, fn, wait, inc=stream, n=16)

    for nm, src, dsth in (("bqt", d_bqt, bqt), ("bkt", d_bkt, bkt), ("bvt", d_bvt, bvt),
                          ("b1t", d_b1t, b1t), ("bgr", d_bgr, bgr),
                          ("identf", d_idf, identf),
                          ("cstage", d_cen, cstage), ("dgm", d_dgm, dgm)):
        t = DMA("sync", "dcc", lambda s, a=src, b=dsth: s.dma_start(out=b[:], in_=a))
        B_(nm).did_w(t)
    _cctot = ("dcc", pl.cnt["dcc"])
    for nm in ("bqt", "bkt", "bvt", "b1t", "bgr", "identf", "cstage", "dgm"):
        B_(nm).wrd = {}
        B_(nm).did_w(_cctot)

    t = V(lambda v: v.memset(G2[:], 1.0))
    B_("G2").did_w(t)

    for dc in range(NCH):
        ps, pb = psF[dc % 2], B_("ps%d" % (dc % 2))
        tk = T(lambda t_, dc=dc, ps=ps: t_.transpose(
            ps[0:128, 0:NRT], cstage[0:NRT, dc * 128:(dc + 1) * 128],
            identf[0:NRT, 0:NRT]),
            wait=B_("cstage").r() + B_("identf").r() + pb.w())
        pb.did_w(tk)
        B_("cstage").did_r(tk)
        tc_ = V(lambda v, dc=dc, ps=ps: v.tensor_copy(cfm[:, dc, :], ps[0:128, 0:NRT]),
                wait=[tk])
        pb.did_r(tc_)
        B_("cfm").did_w(tc_)
    th = V(lambda v: v.tensor_copy(chl[:, :, 0:NRT], cfm[:]), wait=B_("cfm").r())
    B_("chl").did_w(th)
    tl = V(lambda v: v.tensor_tensor(chl[:, :, NRT:32], cfm[:], chl[:, :, 0:NRT],
                                     ALU.subtract),
           wait=B_("cfm").r() + [th])
    B_("chl").did_w(tl)
    B_("cfm").did_r(tl)

    if "p0" in phases:
        for ct in list(range(OWN, LCH)) + list(range(OWN)):
            own = ct < OWN
            sb = ct % 2
            xs = B_("xstage%d" % sb)
            td = DMA("gpsimd", "dxa" if sb == 0 else "dxb", lambda s, ct=ct, sb=sb:
                     s.dma_start(out=xstage[:, sb, :], in_=d_x[ct]), wait=xs.w())
            B_("xstage%d" % sb).wrd = {}
            B_("xstage%d" % sb).rdd = {}
            xs.did_w(td)
            tprev = B_("mv%d" % sb).w() + B_("bno%d" % sb).w()
            ts = []
            for i in range(4):
                ts.append(V(lambda v, sb=sb, i=i: v.bn_stats(
                    bno[:, sb, i, :], xstage[:, sb, i * 512:(i + 1) * 512]),
                    wait=[td] + tprev))
            bnb = B_("bno%d" % sb)
            bnb.wrd = {}
            bnb.rdd = {}
            for t_ in ts:
                bnb.did_w(t_)
            ta = V(lambda v, sb=sb: v.bn_aggr(mv[:, sb, :], bno[:, sb, :, :]),
                   wait=bnb.r() + B_("mv%d" % sb).w())
            B_("mv%d" % sb).wrd = {}
            B_("mv%d" % sb).rdd = {}
            B_("mv%d" % sb).did_w(ta)
            bnb.did_r(ta)
            t0 = V(lambda v, sb=sb: v.tensor_scalar(
                vt[:, sb, 0:1], mv[:, sb, 1:2], EPS, None, ALU.add),
                wait=[ta] + B_("vt%d" % sb).w())
            t1 = A(lambda a, sb=sb: a.activation(vt[:, sb, 1:2], vt[:, sb, 0:1], AF.Ln),
                   wait=[t0])
            t2 = A(lambda a, sb=sb: a.activation(vt[:, sb, 2:3], vt[:, sb, 1:2], AF.Exp,
                                                 scale=-0.5), wait=[t1])
            B_("vt%d" % sb).wrd = {}
            B_("vt%d" % sb).rdd = {}
            B_("vt%d" % sb).did_w(t2)
            if own:
                tsa = V(lambda v, ct=ct, sb=sb: v.tensor_scalar(
                    stab[:, ct, 0:1], vt[:, sb, 2:3], 1.0, None, ALU.add), wait=[t2])
                tsb = V(lambda v, ct=ct, sb=sb: v.scalar_tensor_tensor(
                    stab[:, ct, 1:2], mv[:, sb, 0:1], -1.0, vt[:, sb, 2:3],
                    ALU.mult, ALU.mult), wait=[t2, ta])
                B_("stab").did_w(tsa)
                B_("stab").did_w(tsb)
                B_("vt%d" % sb).did_r(tsb)
                B_("mv%d" % sb).did_r(tsb)
                tai = A(lambda a, ct=ct, sb=sb: a.activation(
                    acc[:, ct, :], xstage[:, sb, :], AF.Identity,
                    scale=stab[:, ct, 0:1], bias=stab[:, ct, 1:2]),
                    wait=[tsa, tsb, td])
                B_("acc").did_w(tai)
                B_("stab").did_r(tai)
                xs.did_r(tai)
                tx = A(lambda a, ct=ct, sb=sb: a.activation(
                    xnst[:], xstage[:, sb, :], AF.Identity,
                    scale=vt[:, sb, 2:3], bias=stab[:, ct, 1:2]),
                    wait=[t2, tsb, td] + B_("xnst").w())
                B_("xnst").wrd = {}
                B_("xnst").rdd = {}
                B_("xnst").did_w(tx)
                B_("mv%d" % sb).did_r(tx)
                B_("vt%d" % sb).did_r(tx)
                xs.did_r(tx)
                xb = B_("xnbf%d" % sb)
                tph = A(lambda a, sb=sb: a.activation(xnbf[:, sb, :], xnst[:], AF.Copy),
                        wait=[tx] + xb.w())
                xb.wrd = {}
                xb.rdd = {}
                xb.did_w(tph)
                lb = B_("lobf%d" % sb)
                tpl = V(lambda v, sb=sb: v.tensor_tensor(
                    lobf[:, sb, :], xnst[:], xnbf[:, sb, :], ALU.subtract),
                    wait=[tx, tph] + lb.w())
                lb.wrd = {}
                lb.rdd = {}
                lb.did_w(tpl)
                B_("xnst").did_r(tpl)
                xb.did_r(tpl)
            else:
                xb = B_("xnbf%d" % sb)
                tss = V(lambda v, sb=sb: v.scalar_tensor_tensor(
                    vt[:, sb, 3:4], mv[:, sb, 0:1], -1.0, vt[:, sb, 2:3],
                    ALU.mult, ALU.mult), wait=[t2, ta])
                B_("vt%d" % sb).did_w(tss)
                tph = A(lambda a, sb=sb: a.activation(
                    xnbf[:, sb, :], xstage[:, sb, :], AF.Identity,
                    scale=vt[:, sb, 2:3], bias=vt[:, sb, 3:4]),
                    wait=[tss, td] + xb.w())
                xb.wrd = {}
                xb.rdd = {}
                xb.did_w(tph)
                B_("mv%d" % sb).did_r(tph)
                B_("vt%d" % sb).did_r(tph)
                xs.did_r(tph)

            dstt = xnf_own if own else xnf_sib
            dstn = "xnf_own" if own else "xnf_sib"
            col = (ct % 8) * 128
            strm = "dta" if sb == 0 else "dtb"
            blkn = dstn + str((ct % 8) // 4)
            te = DMA("sync", strm, lambda s, sb=sb, dstt=dstt, col=col: s.dma_start(
                out=dstt[:, :, col:col + 128],
                in_=xnbf[:, sb, :].rearrange("p (c x) -> p c x", c=NCH),
                transpose=True), wait=[tph] + B_(blkn).w())
            xb.did_r(te)
            B_(blkn).did_w(te)

            if own:
                tlo = DMA("sync", "dsb" if sb else "dsa",
                          lambda s, sb=sb: s.dma_start(
                    out=lofm[:, :, :],
                    in_=lobf[:, sb, :].rearrange("p (c x) -> p c x", c=NCH),
                    transpose=True), wait=[tpl] + B_("lofm").w())
                lb.did_r(tlo)
                B_("lofm").wrd = {}
                B_("lofm").rdd = {}
                B_("lofm").did_w(tlo)
                lts = [tlo] * NCH
                mml = None
                for dc in range(NCH):
                    T(lambda t_, dc=dc, col=col: t_.matmul(
                        psF[0][:, 0:16], xnf_own[:, dc, col:col + 128],
                        chl[:, dc, 0:16], start=(dc == 0), stop=False),
                        wait=(B_("xnf_own%d" % ((ct % 8) // 4)).r() + B_("chl").r() +
                              (B_("ps0").w() if dc == 0 else [])),
                        grp_end=False)
                    T(lambda t_, dc=dc, col=col: t_.matmul(
                        psF[0][:, 0:16], xnf_own[:, dc, col:col + 128],
                        chl[:, dc, 16:32], start=False, stop=False), grp_end=False)
                for dc in range(NCH):
                    mml = T(lambda t_, dc=dc: t_.matmul(
                        psF[0][:, 0:16], lofm[:, dc, :], chl[:, dc, 0:16],
                        start=False, stop=(dc == NCH - 1)),
                        wait=[lts[dc]], grp_end=(dc == NCH - 1))
                B_("ps0").wrd = {}
                B_("ps0").rdd = {}
                B_("ps0").did_w(mml)
                B_("lofm").did_r(mml)
                t2g = V(lambda v: v.tensor_copy(sg[:], psF[0][:, 0:16]),
                        wait=[mml] + B_("sg").w())
                B_("ps0").did_r(t2g)
                t3g = A(lambda a: a.activation(sg[:], sg[:], AF.Sigmoid), wait=[t2g])
                t4g = V(lambda v: v.tensor_tensor(sg[:], sg[:], bgr[:], ALU.add),
                        wait=[t3g] + B_("bgr").r())
                B_("sg").wrd = {}
                B_("sg").rdd = {}
                B_("sg").did_w(t4g)
                t5g = V(lambda v: v.max(out=m8[:], in_=sg[:]), wait=[t4g] + B_("m8").w())
                B_("m8").wrd = {}
                B_("m8").rdd = {}
                B_("m8").did_w(t5g)
                t6g = V(lambda v, ct=ct: v.scalar_tensor_tensor(
                    gat[:, ct, :], sg[:], m8[:, 3:4], sg[:], ALU.is_ge, ALU.mult),
                    wait=[t5g])
                B_("gat").did_w(t6g)
                B_("m8").did_r(t6g)
                B_("sg").did_r(t6g)
                t7g = V(lambda v, ct=ct: v.reduce_sum(gsc[:, 0:1], gat[:, ct, :], axis=AX),
                        wait=[t6g] + B_("gsc").w())
                t8g = V(lambda v: v.tensor_scalar(gsc[:, 0:1], gsc[:, 0:1], 1e-8, None,
                                                  ALU.max), wait=[t7g])
                t9g = V(lambda v: v.reciprocal(gsc[:, 1:2], gsc[:, 0:1]), wait=[t8g])
                B_("gsc").wrd = {}
                B_("gsc").rdd = {}
                B_("gsc").did_w(t9g)
                tag = V(lambda v, ct=ct: v.tensor_scalar(
                    gat[:, ct, :], gat[:, ct, :], gsc[:, 1:2], None, ALU.mult),
                    wait=[t9g, t6g])
                B_("gat").did_w(tag)
                B_("gsc").did_r(tag)
                tk = T(lambda t_, ct=ct: t_.transpose(
                    psF[2][0:16, 0:128], gat[:, ct, :], identf[:]),
                    wait=[tag] + B_("ps2").w())
                B_("ps2").wrd = {}
                B_("ps2").rdd = {}
                B_("ps2").did_w(tk)
                B_("gat").did_r(tk)
                tg = V(lambda v, col=col: v.tensor_copy(
                    G2[0:16, col:col + 128], psF[2][0:16, 0:128]),
                    wait=[tk] + B_("G2").r())
                B_("ps2").did_r(tg)
                B_("G2").did_w(tg)

        if dbg:
            for ct in range(OWN):
                t_ = DMA("scalar", "ds", lambda s, ct=ct: s.dma_start(
                    out=dbg_outs["dacc0"][ct], in_=acc[:, ct, :]), wait=B_("acc").r())
                B_("acc").did_r(t_)
            for dc in range(NCH):
                t_ = DMA("scalar", "ds", lambda s, dc=dc: s.dma_start(
                    out=dbg_outs["dxnf"][:, dc, 0:1024], in_=xnf_own[:, dc, :]),
                    wait=B_("xnf_own0").r() + B_("xnf_own1").r())
                B_("xnf_own0").did_r(t_)
                B_("xnf_own1").did_r(t_)
                t_ = DMA("scalar", "ds", lambda s, dc=dc: s.dma_start(
                    out=dbg_outs["dxnf"][:, dc, 1024:2048], in_=xnf_sib[:, dc, :]),
                    wait=B_("xnf_sib0").r() + B_("xnf_sib1").r())
                B_("xnf_sib0").did_r(t_)
                B_("xnf_sib1").did_r(t_)
            t_ = DMA("scalar", "ds", lambda s: s.dma_start(out=dbg_outs["dgat"], in_=gat[:]),
                     wait=B_("gat").r())
            B_("gat").did_r(t_)
            for nm in ("acc", "xnf_own0", "xnf_own1", "xnf_sib0", "xnf_sib1", "gat"):
                B_(nm).rdd["ds"] = pl.cnt["ds"]

    def xtile(tt):
        return (xnf_own, tt * 512) if tt < 2 else (xnf_sib, (tt - 2) * 512)

    def xtbuf(tt):
        return ("xnf_own%d" % tt) if tt < 2 else ("xnf_sib%d" % (tt - 2))

    if "p1" in phases:
        bar = pl.barrier_toks()
        for nm in ("kbuf0", "kbuf1", "vbuf", "qbuf0", "qbuf1", "wrA", "wrB",
                   "wvpb", "scob0", "scob1", "scob2", "prob0", "prob1", "outg",
                   "msk8", "amx0", "amx1", "amx2"):
            B_(nm).seed(bar)
        tm = DMA("sync", "dcc", lambda s: s.dma_start(out=msk8[:], in_=d_msk),
                 wait=B_("msk8").w())
        B_("msk8").did_w(tm)

        def make_prep(h):
            ksl = h % 2
            th = []

            def t_issue():
                twk = DMA("gpsimd", "dwk", lambda s, h=h: s.dma_start(
                    out=wrA[:], in_=wsrc(("wk", h))), wait=B_("wrA").w())
                B_("wrA").wrd = {}
                B_("wrA").rdd = {}
                B_("wrA").did_w(twk)
                twq = DMA("gpsimd", "dwq", lambda s, h=h: s.dma_start(
                    out=wrB[:], in_=wsrc(("wq", h))), wait=B_("wrB").w())
                B_("wrB").wrd = {}
                B_("wrB").rdd = {}
                B_("wrB").did_w(twq)
                prep_tok[h] = (twk, twq)
            th.append(t_issue)

            def mk_ktile(tt):
                def t_k():
                    twk, _ = prep_tok[h]
                    kb = B_("kbuf%d" % ksl)
                    if h not in kprep_old:
                        kold = kb.w()
                        kb.wrd = {}
                        kb.rdd = {}
                        kprep_old[h] = kold
                    xt, xo = xtile(tt)
                    bk = 4 + tt % 2
                    ps, pb = psF[bk], B_("ps%d" % bk)
                    mm = None
                    for dc in range(NCH):
                        mm = T(lambda t_, dc=dc, xt=xt, xo=xo, ps=ps: t_.matmul(
                            ps[:], wrA[:, dc, :], xt[:, dc, xo:xo + 512],
                            start=(dc == 0), stop=(dc == NCH - 1)),
                            wait=([twk] + B_(xtbuf(tt)).r() +
                                  (pb.w() if dc == 0 else [])),
                            grp_end=(dc == NCH - 1))
                    pb.wrd = {}
                    pb.rdd = {}
                    pb.did_w(mm)
                    B_(xtbuf(tt)).did_r(mm)
                    B_("wrA").did_r(mm)
                    kold = kprep_old[h]
                    if tt % 2 == 0:
                        te = A(lambda a, tt=tt, h=h, ksl=ksl, ps=ps: a.activation(
                            kbuf[:, ksl, tt * 512:(tt + 1) * 512], ps[:], AF.Identity,
                            bias=bkt[:, h:h + 1]),
                            wait=[mm] + kold + B_("bkt").r())
                    else:
                        te = V(lambda v, tt=tt, h=h, ksl=ksl, ps=ps: v.tensor_scalar(
                            kbuf[:, ksl, tt * 512:(tt + 1) * 512], ps[:],
                            bkt[:, h:h + 1], None, ALU.add),
                            wait=[mm] + kold + B_("bkt").r())
                    pb.did_r(te)
                    B_("kbuf%d" % ksl).did_w(te)
                return t_k
            for tt in ((2, 3, 0, 1) if h == 0 else (0, 1, 2, 3)):
                th.append(mk_ktile(tt))

            def mk_qtile(tt):
                def t_q():
                    _, twq = prep_tok[h]
                    qb = B_("qbuf%d" % ksl)
                    if tt == 0:
                        qold = qb.w()
                        qb.wrd = {}
                        qb.rdd = {}
                        qprep_old[h] = qold
                    xt, xo = xtile(tt)
                    bk = 4 + tt % 2
                    ps, pb = psF[bk], B_("ps%d" % bk)
                    mm = None
                    for dc in range(NCH):
                        mm = T(lambda t_, dc=dc, xt=xt, xo=xo, ps=ps: t_.matmul(
                            ps[:], wrB[:, dc, :], xt[:, dc, xo:xo + 512],
                            start=(dc == 0), stop=(dc == NCH - 1)),
                            wait=[twq] + B_(xtbuf(tt)).r() +
                                 (pb.w() if dc == 0 else []),
                            grp_end=(dc == NCH - 1))
                    pb.wrd = {}
                    pb.rdd = {}
                    pb.did_w(mm)
                    B_(xtbuf(tt)).did_r(mm)
                    B_("wrB").did_r(mm)
                    qold = qprep_old[h]
                    te = A(lambda a, tt=tt, h=h, ksl=ksl, ps=ps: a.activation(
                        qbuf[:, ksl, tt * 512:(tt + 1) * 512], ps[:], AF.Identity,
                        bias=bqt[:, h:h + 1]),
                        wait=[mm] + (qold if tt == 0 else []) + B_("bqt").r())
                    pb.did_r(te)
                    B_("qbuf%d" % ksl).did_w(te)
                return t_q
            for tt in range(2):
                th.append(mk_qtile(tt))
            return th

        prep_tok = {}
        kprep_old = {}
        qprep_old = {}

        for h in range(H):
            hp = h % 2
            ksl = h % 2
            wv = B_("wvpb")
            kb = B_("kbuf%d" % ksl)
            qb = B_("qbuf%d" % ksl)
            vb = B_("vbuf")
            if h == 0:
                for th in make_prep(0):
                    th()
            nxt = make_prep(h + 1) if h + 1 < H else []

            tnm_tok = {}
            smx_tok = {}

            def plan_scores(tc, h=h, ksl=ksl):
                sl = tc % 3
                LA = (tc + 1) * 128
                L = LA + 1024
                kb = B_("kbuf%d" % ksl)
                qb = B_("qbuf%d" % ksl)
                sc = B_("scob%d" % sl)
                scold = sc.w()
                sc.wrd = {}
                sc.rdd = {}
                am = B_("amx%d" % sl)
                amold = am.w()
                am.wrd = {}
                am.rdd = {}
                for j in range(2):
                    ps, pb = psF[j % 2], B_("ps%d" % (j % 2))
                    mm = T(lambda t_, j=j, tc=tc, ksl=ksl, ps=ps: t_.matmul(
                        ps[:], qbuf[:, ksl, tc * 128:(tc + 1) * 128],
                        kbuf[:, ksl, 1024 + j * 512:1536 + j * 512],
                        start=True, stop=True),
                        wait=qb.r() + kb.r() + pb.w())
                    pb.wrd = {}
                    pb.rdd = {}
                    pb.did_w(mm)
                    kb.did_r(mm)
                    qb.did_r(mm)
                    te = A(lambda a, j=j, sl=sl, ps=ps: a.activation(
                        scob[:, sl, j * 512:(j + 1) * 512], ps[:],
                        AF.Identity, scale=SCALE, bias=msk8[:, 0:1]),
                        wait=[mm] + B_("msk8").r() + (scold if j == 0 else []))
                    pb.did_r(te)
                    B_("msk8").did_r(te)
                    sc.did_w(te)
                atiles = [(a0, min(a0 + 512, LA)) for a0 in range(0, LA, 512)]
                lastev = None
                for i, (a0, a1) in enumerate(atiles):
                    ps, pb = psF[i % 2], B_("ps%d" % (i % 2))
                    mm = T(lambda t_, a0=a0, a1=a1, tc=tc, ksl=ksl, ps=ps: t_.matmul(
                        ps[:, 0:a1 - a0], qbuf[:, ksl, tc * 128:(tc + 1) * 128],
                        kbuf[:, ksl, a0:a1], start=True, stop=True),
                        wait=pb.w())
                    pb.wrd = {}
                    pb.rdd = {}
                    pb.did_w(mm)
                    qb.did_r(mm)
                    kb.did_r(mm)
                    if i == len(atiles) - 1:
                        te = V(lambda v, a0=a0, a1=a1, sl=sl, ps=ps: v.tensor_scalar(
                            scob[:, sl, 1024 + a0:1024 + a1], ps[:, 0:a1 - a0],
                            SCALE, None, ALU.mult), wait=[mm])
                    else:
                        te = A(lambda a, a0=a0, a1=a1, sl=sl, ps=ps: a.activation(
                            scob[:, sl, 1024 + a0:1024 + a1], ps[:, 0:a1 - a0],
                            AF.Identity, scale=SCALE), wait=[mm])
                    pb.did_r(te)
                    sc.did_w(te)
                    lastev = te
                tdg = V(lambda v, tc=tc, sl=sl: v.tensor_tensor(
                    scob[:, sl, 1024 + tc * 128:1024 + (tc + 1) * 128],
                    scob[:, sl, 1024 + tc * 128:1024 + (tc + 1) * 128],
                    dgm[:], ALU.add), wait=[lastev] + B_("dgm").r())
                sc.did_w(tdg)
                B_("dgm").did_r(tdg)
                tmx = V(lambda v, L=L, sl=sl: v.reduce_max(
                    amx[:, sl, 0:1], scob[:, sl, 0:L], axis=AX, negate=True),
                    wait=sc.w() + amold)
                sc.did_r(tmx)
                am.did_w(tmx)
                smx_tok[tc] = tmx

            def plan_scores_b(tc, h=h):
                sl = tc % 3
                L = (tc + 9) * 128
                sc = B_("scob%d" % sl)
                am = B_("amx%d" % sl)
                pr = B_("prob%d" % (tc % 2))
                prold = pr.w()
                pr.wrd = {}
                pr.rdd = {}
                tex = A(lambda a, L=L, sl=sl, pp=tc % 2: a.activation(
                    prob[:, pp, 0:L], scob[:, sl, 0:L], AF.Exp, bias=amx[:, sl, 0:1],
                    scale=1.0, accum_out=amx[:, sl, 2:3]),
                    wait=[smx_tok[tc]] + prold)
                pr.did_w(tex)
                sc.did_r(tex)
                am.did_w(tex)
                trc = V(lambda v, sl=sl: v.reciprocal(amx[:, sl, 3:4], amx[:, sl, 2:3]),
                        wait=[tex])
                am.did_w(trc)
                tnm = V(lambda v, L=L, sl=sl, pp=tc % 2: v.tensor_scalar(
                    prob[:, pp, 0:L], prob[:, pp, 0:L], amx[:, sl, 3:4], None, ALU.mult),
                    wait=[trc, tex])
                pr.did_w(tnm)
                am.did_r(tnm)
                tnm_tok[tc] = tnm

            def plan_transposes(tc, h=h):
                sl = tc % 2
                pr = B_("prob%d" % sl)
                strm = "dta" if tc % 2 == 0 else "dtb"
                t1 = DMA("sync", strm, lambda s, tc=tc, sl=sl: s.dma_start(
                    out=atT[:, 0:tc + 9, 0:128],
                    in_=prob[:, sl, 0:(tc + 9) * 128].rearrange(
                        "p (c x) -> p c x", c=tc + 9),
                    transpose=True),
                    wait=[tnm_tok[tc]] + wv.w())
                wv.did_w(t1)
                pr.did_r(t1)

            def plan_av(tc, h=h, hp=hp):
                out_ps = psF[6 + tc % 2]
                opb = B_("ps%d" % (6 + tc % 2))
                opold = opb.w()
                opb.wrd = {}
                opb.rdd = {}
                order = [("sib", j) for j in range(8)] + \
                        [("own", i) for i in range(tc + 1)]
                mmf = None
                for oi, (reg, i) in enumerate(order):
                    slot = i if reg == "sib" else 8 + i
                    lci = 8 + i if reg == "sib" else i
                    mmf = T(lambda t_, lci=lci, hp=hp, slot=slot, oi=oi,
                            out_ps=out_ps, n=len(order): t_.matmul(
                        out_ps[:, 0:128], vbuf[:, lci, hp * 128:(hp + 1) * 128],
                        atT[:, slot, 0:128], start=(oi == 0), stop=(oi == n - 1)),
                        wait=wv.r() + vb.r() + (opold if oi == 0 else []),
                        grp_end=(oi == len(order) - 1))
                    wv.did_r(mmf)
                    vb.did_r(mmf)
                opb.did_w(mmf)
                toe = A(lambda a, hp=hp, tc=tc, h=h, out_ps=out_ps: a.activation(
                    outg[:, hp, tc * 128:(tc + 1) * 128], out_ps[:, 0:128],
                    AF.Identity, bias=bvt[:, h:h + 1]),
                    wait=[mmf] + (B_("outg").w() if (hp == 0 and tc == 0) else []) +
                         B_("bvt").r())
                opb.did_r(toe)
                if hp == 0 and tc == 0:
                    B_("outg").wrd = {}
                    B_("outg").rdd = {}
                B_("outg").did_w(toe)

            plan_scores(0)
            plan_scores(1)
            plan_scores_b(0)
            plan_scores(2)

            if hp == 0:
                twv = DMA("gpsimd", "dwv", lambda s, h=h: s.dma_start(
                    out=wvb[:], in_=wsrc(("wv", h))), wait=wv.w())
                wv.wrd = {}
                wv.rdd = {}
                wv.did_w(twv)
                vold = vb.w()
                vb.wrd = {}
                vb.rdd = {}
                for lc in range(LCH):
                    xt = xnf_own if lc < 8 else xnf_sib
                    xo = (lc % 8) * 128
                    bk = 4 + lc % 2
                    ps, pb = psF[bk], B_("ps%d" % bk)
                    mm = None
                    for dc in range(NCH):
                        mm = T(lambda t_, dc=dc, xt=xt, xo=xo, ps=ps: t_.matmul(
                            ps[:, 0:256], xt[:, dc, xo:xo + 128], wvb[:, dc, :],
                            start=(dc == 0), stop=(dc == NCH - 1)),
                            wait=[twv] + B_(xtbuf(lc // 4)).r() +
                                 (pb.w() if dc == 0 else []),
                            grp_end=(dc == NCH - 1))
                    pb.wrd = {}
                    pb.rdd = {}
                    pb.did_w(mm)
                    B_(xtbuf(lc // 4)).did_r(mm)
                    wv.did_r(mm)
                    if lc % 2 == 0:
                        te = A(lambda a, lc=lc, ps=ps: a.activation(
                            vbuf[:, lc, :], ps[:, 0:256], AF.Copy),
                            wait=[mm] + (vold if lc == 0 else []))
                    else:
                        te = V(lambda v, lc=lc, ps=ps: v.tensor_copy(
                            vbuf[:, lc, :], ps[:, 0:256]),
                            wait=[mm] + (vold if lc == 0 else []))
                    pb.did_r(te)
                    vb.did_w(te)

            for tc in range(OWN):
                if tc + 1 < OWN:
                    plan_scores_b(tc + 1)
                if tc + 3 < OWN:
                    plan_scores(tc + 3)
                if nxt:
                    nxt.pop(0)()
                plan_transposes(tc)
                plan_av(tc)
            while nxt:
                nxt.pop(0)()

            if hp == 1:
                g2i = h // 2
                wps = [kbuf[:, 1, :], atT[:].rearrange("p c x -> p (c x)")]
                wpB = [B_("kbuf1"), wv]
                tws = []
                for hc in range(2):
                    wpo = wpB[hc].w()
                    wpB[hc].wrd = {}
                    wpB[hc].rdd = {}
                    tw = DMA("gpsimd", "dwp", lambda s, g2i=g2i, hc=hc: s.dma_start(
                        out=wps[hc], in_=wsrc(("wp", g2i * 2 + hc))),
                        wait=wpo)
                    wpB[hc].did_w(tw)
                    tws.append(tw)
                wptok = tws[1]
                for hc in range(2):
                    wpB[hc].wrd["dwp"] = pl.cnt["dwp"]
                for tcq in range(OWN):
                    mms = []
                    for hc in range(2):
                        for dt in range(4):
                            bk = (tcq % 2) * 4 + dt
                            ps, pb = psF[bk], B_("ps%d" % bk)
                            mm = T(lambda t_, hc=hc, tcq=tcq, dt=dt, ps=ps: t_.matmul(
                                ps[:], outg[:, hc, tcq * 128:(tcq + 1) * 128],
                                wps[hc][:, dt * 512:(dt + 1) * 512],
                                start=(hc == 0), stop=(hc == 1)),
                                wait=[wptok] + B_("outg").r() +
                                     (pb.w() if hc == 0 else []),
                                grp_end=(hc == 1))
                            if hc == 0:
                                pb.wrd = {}
                                pb.rdd = {}
                            if mm is not None:
                                pb.did_w(mm)
                                wpB[0].did_r(mm)
                                wpB[1].did_r(mm)
                                B_("outg").did_r(mm)
                                mms.append((dt, ps, pb, mm))
                    for dt, ps, pb, mm in mms:
                        tev = V(lambda v, tcq=tcq, dt=dt, ps=ps: v.tensor_tensor(
                            acc[:, tcq, dt * 512:(dt + 1) * 512],
                            acc[:, tcq, dt * 512:(dt + 1) * 512], ps[:], ALU.add),
                            wait=[mm] + B_("acc").w())
                        pb.did_r(tev)
                        B_("acc").did_w(tev)

        if dbg:
            for ct in range(OWN):
                t_ = DMA("scalar", "ds", lambda s, ct=ct: s.dma_start(
                    out=dbg_outs["dacc1"][ct], in_=acc[:, ct, :]), wait=B_("acc").r())
                B_("acc").did_r(t_)
            B_("acc").rdd["ds"] = pl.cnt["ds"]

    if "p2" in phases:
        bar = pl.barrier_toks()
        for nm in ("hbuf", "w1b0", "w1b1", "w2b0", "w2b1", "b2sB"):
            B_(nm).seed(bar)
        tb2 = DMA("sync", "dcc", lambda s: s.dma_start(out=b2s[:], in_=d_b2s),
                  wait=B_("b2sB").w() + B_("xnf_sib0").w() + B_("xnf_sib1").w())
        B_("b2sB").did_w(tb2)
        for tcq in range(OWN):
            for dt in range(4):
                ps, pb = psF[(tcq * 4 + dt) % 4], B_("ps%d" % ((tcq * 4 + dt) % 4))
                mm = T(lambda t_, tcq=tcq, dt=dt, ps=ps: t_.matmul(
                    ps[:], G2[0:19, tcq * 128:(tcq + 1) * 128],
                    b2s[0:19, dt * 512:(dt + 1) * 512], start=True, stop=True),
                    wait=[tb2] + B_("G2").r() + pb.w())
                pb.wrd = {}
                pb.rdd = {}
                pb.did_w(mm)
                B_("G2").did_r(mm)
                te = V(lambda v, tcq=tcq, dt=dt, ps=ps: v.tensor_tensor(
                    acc[:, tcq, dt * 512:(dt + 1) * 512],
                    acc[:, tcq, dt * 512:(dt + 1) * 512], ps[:], ALU.add),
                    wait=[mm] + B_("acc").w())
                pb.did_r(te)
                B_("acc").did_w(te)

        for e in range(NEXP):
            hb = B_("hbuf")
            hold = hb.w()
            hb.wrd = {}
            hb.rdd = {}
            first_h = True
            for ft, (f0, f1) in enumerate(F_TILES):
                wslot = ft % 2
                wb = B_("w1b%d" % wslot)
                tw = DMA("gpsimd", "dw1a" if wslot == 0 else "dw1b",
                         lambda s, e=e, ft=ft, wslot=wslot, w=f1 - f0:
                         s.dma_start(out=w1b[:, wslot, :, 0:w], in_=wsrc(("w1", e, ft))),
                         wait=wb.w())
                wb.wrd = {}
                wb.rdd = {}
                wb.did_w(tw)
                nfc = (f1 - f0) // 128
                for fci in range(nfc):
                    Fg = (f0 // 128) + fci
                    bk0, bk1 = (6, 7) if Fg % 2 else (0, 1)
                    ps0, ps1 = psF[bk0], psF[bk1]
                    pb0, pb1 = B_("ps%d" % bk0), B_("ps%d" % bk1)
                    mm0 = mm1 = None
                    for dc in range(NCH):
                        mm0 = T(lambda t_, dc=dc, fci=fci, wslot=wslot, ps=ps0: t_.matmul(
                            ps[:], w1b[:, wslot, dc, fci * 128:(fci + 1) * 128],
                            xnf_own[:, dc, 0:512],
                            start=(dc == 0), stop=(dc == NCH - 1)),
                            wait=[tw] + B_("xnf_own0").r() +
                                 (pb0.w() if dc == 0 else []),
                            grp_end=(dc == NCH - 1))
                        mm1 = T(lambda t_, dc=dc, fci=fci, wslot=wslot, ps=ps1: t_.matmul(
                            ps[:], w1b[:, wslot, dc, fci * 128:(fci + 1) * 128],
                            xnf_own[:, dc, 512:1024],
                            start=(dc == 0), stop=(dc == NCH - 1)),
                            wait=B_("xnf_own1").r() + (pb1.w() if dc == 0 else []),
                            grp_end=(dc == NCH - 1))
                    pb0.wrd = {}
                    pb0.rdd = {}
                    pb0.did_w(mm0)
                    pb1.wrd = {}
                    pb1.rdd = {}
                    pb1.did_w(mm1)
                    wb.did_r(mm1)
                    for tt, (mm, ps, pb) in enumerate(((mm0, ps0, pb0), (mm1, ps1, pb1))):
                        te = A(lambda a, Fg=Fg, tt=tt, e=e, ps=ps: a.activation(
                            hbuf[:, Fg, tt * 512:(tt + 1) * 512], ps[:], AF.Relu,
                            bias=b1t[:, e, Fg:Fg + 1]),
                            wait=[mm] + (hold if first_h else []) + B_("b1t").r())
                        first_h = False
                        pb.did_r(te)
                        hb.did_w(te)
            for dt in range(4):
                wslot = dt % 2
                wb = B_("w2b%d" % wslot)
                tw = DMA("gpsimd", "dw2a" if wslot == 0 else "dw2b",
                         lambda s, e=e, dt=dt, wslot=wslot: s.dma_start(
                    out=w2b[:, wslot, :, :], in_=wsrc(("w2", e, dt))), wait=wb.w())
                wb.wrd = {}
                wb.rdd = {}
                wb.did_w(tw)
                for tcq in range(OWN):
                    bk = 2 + (tcq % 4)
                    ps, pb = psF[bk], B_("ps%d" % bk)
                    mm = None
                    for fc in range(FCH):
                        mm = T(lambda t_, fc=fc, tcq=tcq, wslot=wslot, ps=ps: t_.matmul(
                            ps[:], hbuf[:, fc, tcq * 128:(tcq + 1) * 128],
                            w2b[:, wslot, fc, :], start=(fc == 0), stop=(fc == FCH - 1)),
                            wait=[tw] + hb.r() + (pb.w() if fc == 0 else []),
                            grp_end=(fc == FCH - 1))
                    pb.wrd = {}
                    pb.rdd = {}
                    pb.did_w(mm)
                    wb.did_r(mm)
                    hb.did_r(mm)
                    if e < 2:
                        te = V(lambda v, tcq=tcq, dt=dt, ps=ps: v.tensor_tensor(
                            acc[:, tcq, dt * 512:(dt + 1) * 512],
                            acc[:, tcq, dt * 512:(dt + 1) * 512], ps[:], ALU.add),
                            wait=[mm] + B_("acc").w())
                    else:
                        te = V(lambda v, tcq=tcq, dt=dt, e=e, ps=ps:
                               v.scalar_tensor_tensor(
                            acc[:, tcq, dt * 512:(dt + 1) * 512], ps[:],
                            gat[:, tcq, e - 2:e - 1],
                            acc[:, tcq, dt * 512:(dt + 1) * 512], ALU.mult, ALU.add),
                            wait=[mm] + B_("acc").w() + B_("gat").r())
                        B_("gat").did_r(te)
                    pb.did_r(te)
                    B_("acc").did_w(te)

    last = None
    for tcq in range(OWN):
        last = DMA("scalar", "ds", lambda s, tcq=tcq: s.dma_start(
            out=d_y[tcq], in_=acc[:, tcq, :]), wait=B_("acc").r())
        B_("acc").did_r(last)
    pl.on("scalar", None, [last])

    sstack = contextlib.ExitStack()
    semh = {s: sstack.enter_context(nc.semaphore(s)) for s in SEMS}

    with nc.Block() as blk:
        def emit(ops):
            def _f(eng):
                for ws, fn, inc, n in ops:
                    for s, v in ws:
                        eng.wait_ge(semh[s], v)
                    if fn is not None:
                        instr = fn(eng)
                        if inc is not None:
                            instr.then_inc(semh[inc], n)
            return _f

        blk.tensor(emit(pl.ops["tensor"]))
        blk.scalar(emit(pl.ops["scalar"]))
        blk.vector(emit(pl.ops["vector"]))
        blk.gpsimd(emit(pl.ops["gpsimd"]))
        blk.sync(emit(pl.ops["sync"]))

    sstack.close()
    pstack.close()
    return nc, dbg_outs


def make_inputs(inp):
    x = np.asarray(inp["L0_attn_x"], np.float32)
    wstream = pack_wstream(inp)
    bqt = np.ascontiguousarray(np.asarray(inp["L0_attn_bq"], np.float32).reshape(H, 128).T)
    bkt = np.ascontiguousarray(np.asarray(inp["L0_attn_bk"], np.float32).reshape(H, 128).T)
    bvt = np.ascontiguousarray(np.asarray(inp["L0_attn_bv"], np.float32).reshape(H, 128).T)
    b1_all = np.concatenate([np.asarray(inp["L0_ffn_shared_b1"], np.float32),
                             np.asarray(inp["L0_ffn_routed_b1"], np.float32)], 0)
    b1t = np.ascontiguousarray(b1_all.reshape(NEXP, FCH, 128).transpose(2, 0, 1))
    bgrep = np.ascontiguousarray(
        np.broadcast_to(np.asarray(inp["L0_ffn_bias"], np.float32)[None, :], (128, NRT)))
    b2s = np.ascontiguousarray(np.concatenate([
        np.asarray(inp["L0_ffn_routed_b2"], np.float32),
        np.asarray(inp["L0_attn_bp"], np.float32)[None, :],
        np.asarray(inp["L0_ffn_shared_b2"], np.float32)], 0))
    cen = np.ascontiguousarray(np.asarray(inp["L0_ffn_centroids"], np.float32))
    identf = np.eye(128, dtype=np.float32)
    jj = np.arange(128)
    diagm = np.where(jj[None, :] <= jj[:, None], 0.0, NEG).astype(np.float32)

    in_maps = []
    for c in range(8):
        b, hh = c // 2, c % 2
        xb = x[b]
        xl = xb if hh == 0 else np.concatenate([xb[1024:], xb[:1024]], 0)
        xl = np.ascontiguousarray(xl.reshape(LCH, 128, DM))
        msk = np.full((128, 32), NEG if hh == 0 else 0.0, dtype=np.float32)
        in_maps.append({
            "xin": xl, "wstream": wstream, "bqt": bqt, "bkt": bkt, "bvt": bvt,
            "b1t": b1t, "bgrep": bgrep, "b2s": b2s, "cen": cen,
            "identf": identf, "msk8": msk, "diagm": diagm,
        })
    return in_maps


def kernel(**inputs):
    nc, _ = build_nc(dbg=False)
    in_maps = make_inputs(inputs)
    res = run_bass_kernel_spmd(nc, in_maps, core_ids=list(range(8)))
    out = np.empty((B, S, DM), np.float32)
    for c in range(8):
        b, hh = c // 2, c % 2
        y = res.results[c]["y"].reshape(1024, DM)
        out[b, hh * 1024:(hh + 1) * 1024, :] = y
    return out


# version 17 (final)
# speedup vs baseline: 1.1054x; 1.0232x over previous
import contextlib
import math
import numpy as np
import ml_dtypes
import concourse.bass as bass
import concourse.mybir as mybir
from concourse.bass_utils import run_bass_kernel_spmd
from concourse.alu_op_type import AluOpType

F32 = mybir.dt.float32
BF16 = mybir.dt.bfloat16
AF = mybir.ActivationFunctionType
AX = mybir.AxisListType.X
ALU = AluOpType
BF = ml_dtypes.bfloat16

B, S, DM, H, HD = 4, 2048, 2048, 16, 128
NCH = DM // 128
LCH = 16
OWN = 8
SCALE = 1.0 / math.sqrt(2048.0)
EPS = 1e-5
DFF = 1408
FCH = DFF // 128
NEXP = 18
NRT = 16
NEG = -1.0e30
F_TILES = [(0, 256), (256, 512), (512, 768), (768, 1024), (1024, 1280), (1280, 1408)]
SEMS = ("pe", "ac", "ve", "po", "dxa", "dxb", "dcc", "dwk", "dwq", "dwv", "dwp", "dw1a", "dw1b", "dw2a", "dw2b", "ds", "dta", "dtb", "dsa", "dsb")


def wplan():
    plan = []
    for h in range(H):
        if h % 2 == 0:
            plan.append((("wv", h), NCH * 256))
        plan.append((("wk", h), NCH * 128))
        plan.append((("wq", h), NCH * 128))
    for hc in range(H):
        plan.append((("wp", hc), DM))
    for e in range(NEXP):
        for ft, (f0, f1) in enumerate(F_TILES):
            plan.append((("w1", e, ft), NCH * (f1 - f0)))
        for dt in range(4):
            plan.append((("w2", e, dt), FCH * 512))
    return plan


WPLAN = wplan()
WOFFS = {}
_cur = 0
for _key, _n in WPLAN:
    WOFFS[_key] = (_cur, _n)
    _cur += 128 * _n
WTOTAL = _cur


def _lhsT_slab(W, c0, c1):
    K = W.shape[0] // 128
    return np.ascontiguousarray(
        W.reshape(K, 128, W.shape[1])[:, :, c0:c1].transpose(1, 0, 2).reshape(128, -1)
    )


def pack_wstream(inp):
    w1_all = [np.asarray(inp["L0_ffn_shared_w1"][i], np.float32) for i in range(2)] + [
        np.asarray(inp["L0_ffn_routed_w1"][i], np.float32) for i in range(NRT)]
    w2_all = [np.asarray(inp["L0_ffn_shared_w2"][i], np.float32) for i in range(2)] + [
        np.asarray(inp["L0_ffn_routed_w2"][i], np.float32) for i in range(NRT)]
    Wk = np.asarray(inp["L0_attn_Wk"], np.float32)
    Wq = np.asarray(inp["L0_attn_Wq"], np.float32)
    Wv = np.asarray(inp["L0_attn_Wv"], np.float32)
    Wp = np.asarray(inp["L0_attn_Wp"], np.float32)
    out = np.empty(WTOTAL, np.float32)
    for key, n in WPLAN:
        off, _ = WOFFS[key]
        if key[0] == "wk":
            sl = _lhsT_slab(Wk, key[1] * 128, (key[1] + 1) * 128)
        elif key[0] == "wq":
            sl = _lhsT_slab(Wq, key[1] * 128, (key[1] + 1) * 128)
        elif key[0] == "wv":
            sl = _lhsT_slab(Wv, key[1] * 128, (key[1] + 2) * 128)
        elif key[0] == "wp":
            sl = Wp[key[1] * 128:(key[1] + 1) * 128, :]
        elif key[0] == "w1":
            f0, f1 = F_TILES[key[2]]
            sl = _lhsT_slab(w1_all[key[1]], f0, f1)
        else:
            sl = _lhsT_slab(w2_all[key[1]], key[2] * 512, (key[2] + 1) * 512)
        out[off:off + 128 * n] = sl.reshape(-1)
    return out


class Plan:
    def __init__(self):
        self.cnt = {s: 0 for s in SEMS}
        self.ops = {e: [] for e in ("tensor", "scalar", "vector", "gpsimd", "sync")}
        self.waited = {e: {} for e in self.ops}

    def on(self, eng, fn, wait=(), inc=None, n=1):
        ws = []
        for t in wait:
            if t is None:
                continue
            s, v = t
            if self.waited[eng].get(s, 0) >= v:
                continue
            ws.append((s, v))
            self.waited[eng][s] = v
        tok = None
        if inc is not None:
            self.cnt[inc] += n
            tok = (inc, self.cnt[inc])
        self.ops[eng].append((ws, fn, inc, n))
        return tok

    def barrier_toks(self):
        return [(s, v) for s, v in self.cnt.items() if v > 0]


class Buf:
    def __init__(self):
        self.wrd = {}
        self.rdd = {}

    @staticmethod
    def _m(d, tok):
        if tok is None:
            return
        s, v = tok
        if d.get(s, 0) < v:
            d[s] = v

    def r(self):
        return list(self.wrd.items())

    def w(self):
        return list(self.wrd.items()) + list(self.rdd.items())

    def did_r(self, tok):
        self._m(self.rdd, tok)

    def did_w(self, tok):
        self._m(self.wrd, tok)

    def seed(self, toks):
        for t in toks:
            self._m(self.rdd, t)


def build_nc(dbg=False, phases=("p0", "p1", "p2")):
    nc = bass.Bass("TRN2", target_bir_lowering=False)

    d_x = nc.dram_tensor("xin", [LCH, 128, DM], F32, kind="ExternalInput").ap()
    d_w = nc.dram_tensor("wstream", [WTOTAL], F32, kind="ExternalInput").ap()
    d_bqt = nc.dram_tensor("bqt", [128, H], F32, kind="ExternalInput").ap()
    d_bkt = nc.dram_tensor("bkt", [128, H], F32, kind="ExternalInput").ap()
    d_bvt = nc.dram_tensor("bvt", [128, H], F32, kind="ExternalInput").ap()
    d_b1t = nc.dram_tensor("b1t", [128, NEXP, FCH], F32, kind="ExternalInput").ap()
    d_bgr = nc.dram_tensor("bgrep", [128, NRT], F32, kind="ExternalInput").ap()
    d_b2s = nc.dram_tensor("b2s", [19, DM], F32, kind="ExternalInput").ap()
    d_cen = nc.dram_tensor("cen", [NRT, DM], F32, kind="ExternalInput").ap()
    d_idf = nc.dram_tensor("identf", [128, 128], F32, kind="ExternalInput").ap()
    d_msk = nc.dram_tensor("msk8", [128, 32], F32, kind="ExternalInput").ap()
    d_dgm = nc.dram_tensor("diagm", [128, 128], F32, kind="ExternalInput").ap()
    d_y = nc.dram_tensor("y", [OWN, 128, DM], F32, kind="ExternalOutput").ap()

    def wsrc(key):
        off, n = WOFFS[key]
        return d_w[off:off + 128 * n].rearrange("(p x) -> p x", p=128)

    dbg_outs = {}
    if dbg:
        dbg_outs["dacc0"] = nc.dram_tensor("dacc0", [OWN, 128, DM], F32, kind="ExternalOutput").ap()
        dbg_outs["dacc1"] = nc.dram_tensor("dacc1", [OWN, 128, DM], F32, kind="ExternalOutput").ap()
        dbg_outs["dgat"] = nc.dram_tensor("dgat", [128, OWN, NRT], F32, kind="ExternalOutput").ap()
        dbg_outs["dxnf"] = nc.dram_tensor("dxnf", [128, NCH, 2048], BF16, kind="ExternalOutput").ap()

    probe = nc.alloc_sbuf_tensor("base_probe", [128, 32], F32)
    BASE = nc.lookup_mloc(probe).addr + 32

    cur = [BASE]

    def alloc(name, shape, dtype, off=None):
        if off is None:
            off = cur[0]
            per = int(np.prod(shape[1:])) * mybir.dt.size(dtype)
            cur[0] = (off + per + 31) & ~31
        return nc.alloc_sbuf_tensor_at(name, shape, dtype, offset=off)

    acc = alloc("acc", [128, OWN, DM], F32)
    xnf_own = alloc("xnf_own", [128, NCH, 1024], BF16)
    gat = alloc("gat", [128, OWN, NRT], F32)
    stab = alloc("stab", [128, OWN, 2], F32)
    G2 = alloc("G2", [32, 1024], F32)
    b1t = alloc("b1t", [128, NEXP, FCH], F32)
    bqt = alloc("bqt", [128, H], F32)
    bkt = alloc("bkt", [128, H], F32)
    bvt = alloc("bvt", [128, H], F32)
    identf = alloc("identf", [128, 128], F32)
    bgr = alloc("bgr", [128, NRT], F32)
    chl = alloc("chl", [128, NCH, 32], BF16)
    mv = alloc("mv", [128, 2, 2], F32)
    bno = alloc("bno", [128, 2, 4, 6], F32)
    vt = alloc("vt", [128, 2, 4], F32)
    m8 = alloc("m8", [128, 2, 8], F32)
    sg = alloc("sg", [128, 2, NRT], F32)
    gsc = alloc("gsc", [128, 2, 2], F32)
    amx = alloc("amx", [128, 3, 8], F32)
    dgm = alloc("dgm", [128, 128], F32)

    SIB = (cur[0] + 31) & ~31
    xnf_sib = alloc("xnf_sib", [128, NCH, 1024], BF16, off=SIB)
    b2s = alloc("b2s", [19, DM], F32, off=SIB)
    w2b = alloc("w2b", [128, 2, FCH, 512], BF16, off=SIB + 8192)
    OVL = SIB + 32768

    xstage = alloc("xstage", [128, 2, DM], F32, off=OVL + 0)
    xnst = alloc("xnst", [128, DM], F32, off=OVL + 16384)
    xnst2 = alloc("xnst2", [128, DM], F32, off=OVL + 60416)
    xnbf = alloc("xnbf", [128, 2, DM], BF16, off=OVL + 24576)
    lobf = alloc("lobf", [128, 2, DM], BF16, off=OVL + 32768)
    lofm = alloc("lofm", [128, NCH, 128], BF16, off=OVL + 40960)
    lofm2 = alloc("lofm2", [128, NCH, 128], BF16, off=OVL + 45056)
    cstage = alloc("cstage", [NRT, DM], F32, off=OVL + 45056)

    kbuf = alloc("kbuf", [128, 2, 2048], BF16, off=OVL + 0)
    qbuf = alloc("qbuf", [128, 2, 1024], BF16, off=OVL + 8192)
    vbuf = alloc("vbuf", [128, LCH, 256], BF16, off=OVL + 12288)
    wrA = alloc("wrA", [128, NCH, 128], BF16, off=OVL + 20480)
    wrB = alloc("wrB", [128, NCH, 128], BF16, off=OVL + 24576)
    wvb = alloc("wvb", [128, NCH, 256], BF16, off=OVL + 28672)
    atT = alloc("atT", [128, 16, 128], BF16, off=OVL + 28672)
    scob = alloc("scob", [128, 3, 2048], F32, off=OVL + 36864)
    cfm = alloc("cfm", [128, NCH, NRT], F32, off=OVL + 53248)
    prob = alloc("prob", [128, 2, 2048], BF16, off=OVL + 60416)
    outg = alloc("outg", [128, 2, 1024], BF16, off=OVL + 68608)
    msk8 = alloc("msk8", [128, 32], F32, off=OVL + 72704)

    hbuf = alloc("hbuf", [128, FCH, 1024], BF16, off=OVL + 0)
    w1b = alloc("w1b", [128, 2, NCH, 256], BF16, off=OVL + 22528)
    OVL_END = OVL + 71680
    assert OVL_END <= 229376 - 64, (BASE, OVL_END)

    pstack = contextlib.ExitStack()
    psF = [pstack.enter_context(nc.psum_tensor("psF%d" % i, [128, 512], F32))
           for i in range(8)]

    pl = Plan()
    bufs = {}

    def B_(name):
        if name not in bufs:
            bufs[name] = Buf()
        return bufs[name]

    def T(fn, wait=(), grp_end=True):
        return pl.on("tensor", fn, wait, inc="pe" if grp_end else None)

    def V(fn, wait=()):
        return pl.on("vector", fn, wait, inc="ve")

    def A(fn, wait=()):
        return pl.on("scalar", fn, wait, inc="ac")

    def PO(fn, wait=()):
        return pl.on("gpsimd", fn, wait, inc="po")

    def DMA(queue, stream, fn, wait=()):
        return pl.on(queue, fn, wait, inc=stream, n=16)

    for nm, src, dsth in (("bqt", d_bqt, bqt), ("bkt", d_bkt, bkt), ("bvt", d_bvt, bvt),
                          ("b1t", d_b1t, b1t), ("bgr", d_bgr, bgr),
                          ("identf", d_idf, identf),
                          ("cstage", d_cen, cstage), ("dgm", d_dgm, dgm)):
        t = DMA("sync", "dcc", lambda s, a=src, b=dsth: s.dma_start(out=b[:], in_=a))
        B_(nm).did_w(t)
    _cctot = ("dcc", pl.cnt["dcc"])
    for nm in ("bqt", "bkt", "bvt", "b1t", "bgr", "identf", "cstage", "dgm"):
        B_(nm).wrd = {}
        B_(nm).did_w(_cctot)

    t = V(lambda v: v.memset(G2[:], 1.0))
    B_("G2").did_w(t)

    for dc in range(NCH):
        ps, pb = psF[dc % 2], B_("ps%d" % (dc % 2))
        tk = T(lambda t_, dc=dc, ps=ps: t_.transpose(
            ps[0:128, 0:NRT], cstage[0:NRT, dc * 128:(dc + 1) * 128],
            identf[0:NRT, 0:NRT]),
            wait=B_("cstage").r() + B_("identf").r() + pb.w())
        pb.did_w(tk)
        B_("cstage").did_r(tk)
        tc_ = V(lambda v, dc=dc, ps=ps: v.tensor_copy(cfm[:, dc, :], ps[0:128, 0:NRT]),
                wait=[tk])
        pb.did_r(tc_)
        B_("cfm").did_w(tc_)
    th = V(lambda v: v.tensor_copy(chl[:, :, 0:NRT], cfm[:]), wait=B_("cfm").r())
    B_("chl").did_w(th)
    tl = V(lambda v: v.tensor_tensor(chl[:, :, NRT:32], cfm[:], chl[:, :, 0:NRT],
                                     ALU.subtract),
           wait=B_("cfm").r() + [th])
    B_("chl").did_w(tl)
    B_("cfm").did_r(tl)

    if "p0" in phases:
        for ct in list(range(OWN, LCH)) + list(range(OWN)):
            own = ct < OWN
            sb = ct % 2
            xs = B_("xstage%d" % sb)
            td = DMA("gpsimd", "dxa" if sb == 0 else "dxb", lambda s, ct=ct, sb=sb:
                     s.dma_start(out=xstage[:, sb, :], in_=d_x[ct]), wait=xs.w())
            B_("xstage%d" % sb).wrd = {}
            B_("xstage%d" % sb).rdd = {}
            xs.did_w(td)
            tprev = B_("mv%d" % sb).w() + B_("bno%d" % sb).w()
            ts = []
            for i in range(4):
                ts.append(V(lambda v, sb=sb, i=i: v.bn_stats(
                    bno[:, sb, i, :], xstage[:, sb, i * 512:(i + 1) * 512]),
                    wait=[td] + tprev))
            bnb = B_("bno%d" % sb)
            bnb.wrd = {}
            bnb.rdd = {}
            for t_ in ts:
                bnb.did_w(t_)
            ta = V(lambda v, sb=sb: v.bn_aggr(mv[:, sb, :], bno[:, sb, :, :]),
                   wait=bnb.r() + B_("mv%d" % sb).w())
            B_("mv%d" % sb).wrd = {}
            B_("mv%d" % sb).rdd = {}
            B_("mv%d" % sb).did_w(ta)
            bnb.did_r(ta)
            t0 = V(lambda v, sb=sb: v.tensor_scalar(
                vt[:, sb, 0:1], mv[:, sb, 1:2], EPS, None, ALU.add),
                wait=[ta] + B_("vt%d" % sb).w())
            t1 = A(lambda a, sb=sb: a.activation(vt[:, sb, 1:2], vt[:, sb, 0:1], AF.Ln),
                   wait=[t0])
            t2 = A(lambda a, sb=sb: a.activation(vt[:, sb, 2:3], vt[:, sb, 1:2], AF.Exp,
                                                 scale=-0.5), wait=[t1])
            B_("vt%d" % sb).wrd = {}
            B_("vt%d" % sb).rdd = {}
            B_("vt%d" % sb).did_w(t2)
            xnstT = xnst if sb == 0 else xnst2
            xnstN = "xnst" if sb == 0 else "xnst2"
            lofmT = lofm if sb == 0 else lofm2
            lofmN = "lofm" if sb == 0 else "lofm2"
            if own:
                tsa = V(lambda v, ct=ct, sb=sb: v.tensor_scalar(
                    stab[:, ct, 0:1], vt[:, sb, 2:3], 1.0, None, ALU.add), wait=[t2])
                tsb = V(lambda v, ct=ct, sb=sb: v.scalar_tensor_tensor(
                    stab[:, ct, 1:2], mv[:, sb, 0:1], -1.0, vt[:, sb, 2:3],
                    ALU.mult, ALU.mult), wait=[t2, ta])
                B_("stab").did_w(tsa)
                B_("stab").did_w(tsb)
                B_("vt%d" % sb).did_r(tsb)
                B_("mv%d" % sb).did_r(tsb)
                tai = A(lambda a, ct=ct, sb=sb: a.activation(
                    acc[:, ct, :], xstage[:, sb, :], AF.Identity,
                    scale=stab[:, ct, 0:1], bias=stab[:, ct, 1:2]),
                    wait=[tsa, tsb, td])
                B_("acc").did_w(tai)
                B_("stab").did_r(tai)
                xs.did_r(tai)
                tx = A(lambda a, ct=ct, sb=sb, xnstT=xnstT: a.activation(
                    xnstT[:], xstage[:, sb, :], AF.Identity,
                    scale=vt[:, sb, 2:3], bias=stab[:, ct, 1:2]),
                    wait=[t2, tsb, td] + B_(xnstN).w())
                B_(xnstN).wrd = {}
                B_(xnstN).rdd = {}
                B_(xnstN).did_w(tx)
                B_("mv%d" % sb).did_r(tx)
                B_("vt%d" % sb).did_r(tx)
                xs.did_r(tx)
                xb = B_("xnbf%d" % sb)
                tph = A(lambda a, sb=sb, xnstT=xnstT: a.activation(xnbf[:, sb, :], xnstT[:], AF.Copy),
                        wait=[tx] + xb.w())
                xb.wrd = {}
                xb.rdd = {}
                xb.did_w(tph)
                lb = B_("lobf%d" % sb)
                tpl = V(lambda v, sb=sb, xnstT=xnstT: v.tensor_tensor(
                    lobf[:, sb, :], xnstT[:], xnbf[:, sb, :], ALU.subtract),
                    wait=[tx, tph] + lb.w())
                lb.wrd = {}
                lb.rdd = {}
                lb.did_w(tpl)
                B_(xnstN).did_r(tpl)
                xb.did_r(tpl)
            else:
                xb = B_("xnbf%d" % sb)
                tss = V(lambda v, sb=sb: v.scalar_tensor_tensor(
                    vt[:, sb, 3:4], mv[:, sb, 0:1], -1.0, vt[:, sb, 2:3],
                    ALU.mult, ALU.mult), wait=[t2, ta])
                B_("vt%d" % sb).did_w(tss)
                tph = A(lambda a, sb=sb: a.activation(
                    xnbf[:, sb, :], xstage[:, sb, :], AF.Identity,
                    scale=vt[:, sb, 2:3], bias=vt[:, sb, 3:4]),
                    wait=[tss, td] + xb.w())
                xb.wrd = {}
                xb.rdd = {}
                xb.did_w(tph)
                B_("mv%d" % sb).did_r(tph)
                B_("vt%d" % sb).did_r(tph)
                xs.did_r(tph)

            dstt = xnf_own if own else xnf_sib
            dstn = "xnf_own" if own else "xnf_sib"
            col = (ct % 8) * 128
            strm = "dta" if sb == 0 else "dtb"
            blkn = dstn + str((ct % 8) // 4)
            te = DMA("sync", strm, lambda s, sb=sb, dstt=dstt, col=col: s.dma_start(
                out=dstt[:, :, col:col + 128],
                in_=xnbf[:, sb, :].rearrange("p (c x) -> p c x", c=NCH),
                transpose=True), wait=[tph] + B_(blkn).w())
            xb.did_r(te)
            B_(blkn).did_w(te)

            if own:
                tlo = DMA("sync", "dsb" if sb else "dsa",
                          lambda s, sb=sb, lofmT=lofmT: s.dma_start(
                    out=lofmT[:, :, :],
                    in_=lobf[:, sb, :].rearrange("p (c x) -> p c x", c=NCH),
                    transpose=True), wait=[tpl] + B_(lofmN).w())
                lb.did_r(tlo)
                B_(lofmN).wrd = {}
                B_(lofmN).rdd = {}
                B_(lofmN).did_w(tlo)
                lts = [tlo] * NCH
                mml = None
                for dc in range(NCH):
                    T(lambda t_, dc=dc, col=col: t_.matmul(
                        psF[0][:, 0:16], xnf_own[:, dc, col:col + 128],
                        chl[:, dc, 0:16], start=(dc == 0), stop=False),
                        wait=(B_("xnf_own%d" % ((ct % 8) // 4)).r() + B_("chl").r() +
                              (B_("ps0").w() if dc == 0 else [])),
                        grp_end=False)
                    T(lambda t_, dc=dc, col=col: t_.matmul(
                        psF[0][:, 0:16], xnf_own[:, dc, col:col + 128],
                        chl[:, dc, 16:32], start=False, stop=False), grp_end=False)
                for dc in range(NCH):
                    mml = T(lambda t_, dc=dc, lofmT=lofmT: t_.matmul(
                        psF[0][:, 0:16], lofmT[:, dc, :], chl[:, dc, 0:16],
                        start=False, stop=(dc == NCH - 1)),
                        wait=[lts[dc]], grp_end=(dc == NCH - 1))
                B_("ps0").wrd = {}
                B_("ps0").rdd = {}
                B_("ps0").did_w(mml)
                B_(lofmN).did_r(mml)
                t2g = V(lambda v, sb=sb: v.tensor_copy(sg[:, sb, :], psF[0][:, 0:16]),
                        wait=[mml] + B_("sg%d" % sb).w())
                B_("ps0").did_r(t2g)
                t3g = A(lambda a, sb=sb: a.activation(sg[:, sb, :], sg[:, sb, :], AF.Sigmoid), wait=[t2g])
                t4g = V(lambda v, sb=sb: v.tensor_tensor(sg[:, sb, :], sg[:, sb, :], bgr[:], ALU.add),
                        wait=[t3g] + B_("bgr").r())
                B_("sg%d" % sb).wrd = {}
                B_("sg%d" % sb).rdd = {}
                B_("sg%d" % sb).did_w(t4g)
                t5g = V(lambda v, sb=sb: v.max(out=m8[:, sb, :], in_=sg[:, sb, :]), wait=[t4g] + B_("m8%d" % sb).w())
                B_("m8%d" % sb).wrd = {}
                B_("m8%d" % sb).rdd = {}
                B_("m8%d" % sb).did_w(t5g)
                t6g = V(lambda v, ct=ct, sb=sb: v.scalar_tensor_tensor(
                    gat[:, ct, :], sg[:, sb, :], m8[:, sb, 3:4], sg[:, sb, :], ALU.is_ge, ALU.mult),
                    wait=[t5g])
                B_("gat").did_w(t6g)
                B_("m8%d" % sb).did_r(t6g)
                B_("sg%d" % sb).did_r(t6g)
                t7g = V(lambda v, ct=ct, sb=sb: v.reduce_sum(gsc[:, sb, 0:1], gat[:, ct, :], axis=AX),
                        wait=[t6g] + B_("gsc%d" % sb).w())
                t8g = V(lambda v, sb=sb: v.tensor_scalar(gsc[:, sb, 0:1], gsc[:, sb, 0:1], 1e-8, None,
                                                  ALU.max), wait=[t7g])
                t9g = V(lambda v, sb=sb: v.reciprocal(gsc[:, sb, 1:2], gsc[:, sb, 0:1]), wait=[t8g])
                B_("gsc%d" % sb).wrd = {}
                B_("gsc%d" % sb).rdd = {}
                B_("gsc%d" % sb).did_w(t9g)
                tag = V(lambda v, ct=ct, sb=sb: v.tensor_scalar(
                    gat[:, ct, :], gat[:, ct, :], gsc[:, sb, 1:2], None, ALU.mult),
                    wait=[t9g, t6g])
                B_("gat").did_w(tag)
                B_("gsc%d" % sb).did_r(tag)
                tk = T(lambda t_, ct=ct: t_.transpose(
                    psF[2][0:16, 0:128], gat[:, ct, :], identf[:]),
                    wait=[tag] + B_("ps2").w())
                B_("ps2").wrd = {}
                B_("ps2").rdd = {}
                B_("ps2").did_w(tk)
                B_("gat").did_r(tk)
                tg = V(lambda v, col=col: v.tensor_copy(
                    G2[0:16, col:col + 128], psF[2][0:16, 0:128]),
                    wait=[tk] + B_("G2").r())
                B_("ps2").did_r(tg)
                B_("G2").did_w(tg)

        if dbg:
            for ct in range(OWN):
                t_ = DMA("scalar", "ds", lambda s, ct=ct: s.dma_start(
                    out=dbg_outs["dacc0"][ct], in_=acc[:, ct, :]), wait=B_("acc").r())
                B_("acc").did_r(t_)
            for dc in range(NCH):
                t_ = DMA("scalar", "ds", lambda s, dc=dc: s.dma_start(
                    out=dbg_outs["dxnf"][:, dc, 0:1024], in_=xnf_own[:, dc, :]),
                    wait=B_("xnf_own0").r() + B_("xnf_own1").r())
                B_("xnf_own0").did_r(t_)
                B_("xnf_own1").did_r(t_)
                t_ = DMA("scalar", "ds", lambda s, dc=dc: s.dma_start(
                    out=dbg_outs["dxnf"][:, dc, 1024:2048], in_=xnf_sib[:, dc, :]),
                    wait=B_("xnf_sib0").r() + B_("xnf_sib1").r())
                B_("xnf_sib0").did_r(t_)
                B_("xnf_sib1").did_r(t_)
            t_ = DMA("scalar", "ds", lambda s: s.dma_start(out=dbg_outs["dgat"], in_=gat[:]),
                     wait=B_("gat").r())
            B_("gat").did_r(t_)
            for nm in ("acc", "xnf_own0", "xnf_own1", "xnf_sib0", "xnf_sib1", "gat"):
                B_(nm).rdd["ds"] = pl.cnt["ds"]

    def xtile(tt):
        return (xnf_own, tt * 512) if tt < 2 else (xnf_sib, (tt - 2) * 512)

    def xtbuf(tt):
        return ("xnf_own%d" % tt) if tt < 2 else ("xnf_sib%d" % (tt - 2))

    if "p1" in phases:
        bar = pl.barrier_toks()
        for nm in ("kbuf0", "kbuf1", "vbuf", "qbuf0", "qbuf1", "wrA", "wrB",
                   "wvpb", "scob0", "scob1", "scob2", "prob0", "prob1", "outg",
                   "msk8", "amx0", "amx1", "amx2"):
            B_(nm).seed(bar)
        tm = DMA("sync", "dcc", lambda s: s.dma_start(out=msk8[:], in_=d_msk),
                 wait=B_("msk8").w())
        B_("msk8").did_w(tm)

        def make_prep(h):
            ksl = h % 2
            th = []

            def t_issue():
                twk = DMA("gpsimd", "dwk", lambda s, h=h: s.dma_start(
                    out=wrA[:], in_=wsrc(("wk", h))), wait=B_("wrA").w())
                B_("wrA").wrd = {}
                B_("wrA").rdd = {}
                B_("wrA").did_w(twk)
                twq = DMA("gpsimd", "dwq", lambda s, h=h: s.dma_start(
                    out=wrB[:], in_=wsrc(("wq", h))), wait=B_("wrB").w())
                B_("wrB").wrd = {}
                B_("wrB").rdd = {}
                B_("wrB").did_w(twq)
                prep_tok[h] = (twk, twq)
            th.append(t_issue)

            def mk_ktile(tt):
                def t_k():
                    twk, _ = prep_tok[h]
                    kb = B_("kbuf%d" % ksl)
                    if h not in kprep_old:
                        kold = kb.w()
                        kb.wrd = {}
                        kb.rdd = {}
                        kprep_old[h] = kold
                    xt, xo = xtile(tt)
                    bk = 4 + tt % 2
                    ps, pb = psF[bk], B_("ps%d" % bk)
                    mm = None
                    for dc in range(NCH):
                        mm = T(lambda t_, dc=dc, xt=xt, xo=xo, ps=ps: t_.matmul(
                            ps[:], wrA[:, dc, :], xt[:, dc, xo:xo + 512],
                            start=(dc == 0), stop=(dc == NCH - 1)),
                            wait=([twk] + B_(xtbuf(tt)).r() +
                                  (pb.w() if dc == 0 else [])),
                            grp_end=(dc == NCH - 1))
                    pb.wrd = {}
                    pb.rdd = {}
                    pb.did_w(mm)
                    B_(xtbuf(tt)).did_r(mm)
                    B_("wrA").did_r(mm)
                    kold = kprep_old[h]
                    if tt % 2 == 0:
                        te = A(lambda a, tt=tt, h=h, ksl=ksl, ps=ps: a.activation(
                            kbuf[:, ksl, tt * 512:(tt + 1) * 512], ps[:], AF.Identity,
                            bias=bkt[:, h:h + 1]),
                            wait=[mm] + kold + B_("bkt").r())
                    else:
                        te = V(lambda v, tt=tt, h=h, ksl=ksl, ps=ps: v.tensor_scalar(
                            kbuf[:, ksl, tt * 512:(tt + 1) * 512], ps[:],
                            bkt[:, h:h + 1], None, ALU.add),
                            wait=[mm] + kold + B_("bkt").r())
                    pb.did_r(te)
                    B_("kbuf%d" % ksl).did_w(te)
                return t_k
            for tt in ((2, 3, 0, 1) if h == 0 else (0, 1, 2, 3)):
                th.append(mk_ktile(tt))

            def mk_qtile(tt):
                def t_q():
                    _, twq = prep_tok[h]
                    qb = B_("qbuf%d" % ksl)
                    if tt == 0:
                        qold = qb.w()
                        qb.wrd = {}
                        qb.rdd = {}
                        qprep_old[h] = qold
                    xt, xo = xtile(tt)
                    bk = 4 + tt % 2
                    ps, pb = psF[bk], B_("ps%d" % bk)
                    mm = None
                    for dc in range(NCH):
                        mm = T(lambda t_, dc=dc, xt=xt, xo=xo, ps=ps: t_.matmul(
                            ps[:], wrB[:, dc, :], xt[:, dc, xo:xo + 512],
                            start=(dc == 0), stop=(dc == NCH - 1)),
                            wait=[twq] + B_(xtbuf(tt)).r() +
                                 (pb.w() if dc == 0 else []),
                            grp_end=(dc == NCH - 1))
                    pb.wrd = {}
                    pb.rdd = {}
                    pb.did_w(mm)
                    B_(xtbuf(tt)).did_r(mm)
                    B_("wrB").did_r(mm)
                    qold = qprep_old[h]
                    te = A(lambda a, tt=tt, h=h, ksl=ksl, ps=ps: a.activation(
                        qbuf[:, ksl, tt * 512:(tt + 1) * 512], ps[:], AF.Identity,
                        bias=bqt[:, h:h + 1]),
                        wait=[mm] + (qold if tt == 0 else []) + B_("bqt").r())
                    pb.did_r(te)
                    B_("qbuf%d" % ksl).did_w(te)
                return t_q
            for tt in range(2):
                th.append(mk_qtile(tt))
            return th

        prep_tok = {}
        kprep_old = {}
        qprep_old = {}

        for h in range(H):
            hp = h % 2
            ksl = h % 2
            wv = B_("wvpb")
            kb = B_("kbuf%d" % ksl)
            qb = B_("qbuf%d" % ksl)
            vb = B_("vbuf")
            if h == 0:
                for th in make_prep(0):
                    th()
            nxt = make_prep(h + 1) if h + 1 < H else []

            tnm_tok = {}
            smx_tok = {}

            def plan_scores(tc, h=h, ksl=ksl):
                sl = tc % 3
                LA = (tc + 1) * 128
                L = LA + 1024
                kb = B_("kbuf%d" % ksl)
                qb = B_("qbuf%d" % ksl)
                sc = B_("scob%d" % sl)
                scold = sc.w()
                sc.wrd = {}
                sc.rdd = {}
                am = B_("amx%d" % sl)
                amold = am.w()
                am.wrd = {}
                am.rdd = {}
                for j in range(2):
                    ps, pb = psF[j % 2], B_("ps%d" % (j % 2))
                    mm = T(lambda t_, j=j, tc=tc, ksl=ksl, ps=ps: t_.matmul(
                        ps[:], qbuf[:, ksl, tc * 128:(tc + 1) * 128],
                        kbuf[:, ksl, 1024 + j * 512:1536 + j * 512],
                        start=True, stop=True),
                        wait=qb.r() + kb.r() + pb.w())
                    pb.wrd = {}
                    pb.rdd = {}
                    pb.did_w(mm)
                    kb.did_r(mm)
                    qb.did_r(mm)
                    te = A(lambda a, j=j, sl=sl, ps=ps: a.activation(
                        scob[:, sl, j * 512:(j + 1) * 512], ps[:],
                        AF.Identity, scale=SCALE, bias=msk8[:, 0:1]),
                        wait=[mm] + B_("msk8").r() + (scold if j == 0 else []))
                    pb.did_r(te)
                    B_("msk8").did_r(te)
                    sc.did_w(te)
                atiles = [(a0, min(a0 + 512, LA)) for a0 in range(0, LA, 512)]
                lastev = None
                for i, (a0, a1) in enumerate(atiles):
                    ps, pb = psF[i % 2], B_("ps%d" % (i % 2))
                    mm = T(lambda t_, a0=a0, a1=a1, tc=tc, ksl=ksl, ps=ps: t_.matmul(
                        ps[:, 0:a1 - a0], qbuf[:, ksl, tc * 128:(tc + 1) * 128],
                        kbuf[:, ksl, a0:a1], start=True, stop=True),
                        wait=pb.w())
                    pb.wrd = {}
                    pb.rdd = {}
                    pb.did_w(mm)
                    qb.did_r(mm)
                    kb.did_r(mm)
                    if i == len(atiles) - 1:
                        te = V(lambda v, a0=a0, a1=a1, sl=sl, ps=ps: v.tensor_scalar(
                            scob[:, sl, 1024 + a0:1024 + a1], ps[:, 0:a1 - a0],
                            SCALE, None, ALU.mult), wait=[mm])
                    else:
                        te = A(lambda a, a0=a0, a1=a1, sl=sl, ps=ps: a.activation(
                            scob[:, sl, 1024 + a0:1024 + a1], ps[:, 0:a1 - a0],
                            AF.Identity, scale=SCALE), wait=[mm])
                    pb.did_r(te)
                    sc.did_w(te)
                    lastev = te
                tdg = V(lambda v, tc=tc, sl=sl: v.tensor_tensor(
                    scob[:, sl, 1024 + tc * 128:1024 + (tc + 1) * 128],
                    scob[:, sl, 1024 + tc * 128:1024 + (tc + 1) * 128],
                    dgm[:], ALU.add), wait=[lastev] + B_("dgm").r())
                sc.did_w(tdg)
                B_("dgm").did_r(tdg)
                tmx = V(lambda v, L=L, sl=sl: v.reduce_max(
                    amx[:, sl, 0:1], scob[:, sl, 0:L], axis=AX, negate=True),
                    wait=sc.w() + amold)
                sc.did_r(tmx)
                am.did_w(tmx)
                smx_tok[tc] = tmx

            def plan_scores_b(tc, h=h):
                sl = tc % 3
                L = (tc + 9) * 128
                sc = B_("scob%d" % sl)
                am = B_("amx%d" % sl)
                pr = B_("prob%d" % (tc % 2))
                prold = pr.w()
                pr.wrd = {}
                pr.rdd = {}
                tex = A(lambda a, L=L, sl=sl, pp=tc % 2: a.activation(
                    prob[:, pp, 0:L], scob[:, sl, 0:L], AF.Exp, bias=amx[:, sl, 0:1],
                    scale=1.0, accum_out=amx[:, sl, 2:3]),
                    wait=[smx_tok[tc]] + prold)
                pr.did_w(tex)
                sc.did_r(tex)
                am.did_w(tex)
                trc = V(lambda v, sl=sl: v.reciprocal(amx[:, sl, 3:4], amx[:, sl, 2:3]),
                        wait=[tex])
                am.did_w(trc)
                tnm = V(lambda v, L=L, sl=sl, pp=tc % 2: v.tensor_scalar(
                    prob[:, pp, 0:L], prob[:, pp, 0:L], amx[:, sl, 3:4], None, ALU.mult),
                    wait=[trc, tex])
                pr.did_w(tnm)
                am.did_r(tnm)
                tnm_tok[tc] = tnm

            def plan_transposes(tc, h=h):
                sl = tc % 2
                pr = B_("prob%d" % sl)
                strm = "dta" if tc % 2 == 0 else "dtb"
                t1 = DMA("sync", strm, lambda s, tc=tc, sl=sl: s.dma_start(
                    out=atT[:, 0:tc + 9, 0:128],
                    in_=prob[:, sl, 0:(tc + 9) * 128].rearrange(
                        "p (c x) -> p c x", c=tc + 9),
                    transpose=True),
                    wait=[tnm_tok[tc]] + wv.w())
                wv.did_w(t1)
                pr.did_r(t1)

            def plan_av(tc, h=h, hp=hp):
                out_ps = psF[6 + tc % 2]
                opb = B_("ps%d" % (6 + tc % 2))
                opold = opb.w()
                opb.wrd = {}
                opb.rdd = {}
                order = [("sib", j) for j in range(8)] + \
                        [("own", i) for i in range(tc + 1)]
                mmf = None
                for oi, (reg, i) in enumerate(order):
                    slot = i if reg == "sib" else 8 + i
                    lci = 8 + i if reg == "sib" else i
                    mmf = T(lambda t_, lci=lci, hp=hp, slot=slot, oi=oi,
                            out_ps=out_ps, n=len(order): t_.matmul(
                        out_ps[:, 0:128], vbuf[:, lci, hp * 128:(hp + 1) * 128],
                        atT[:, slot, 0:128], start=(oi == 0), stop=(oi == n - 1)),
                        wait=wv.r() + vb.r() + (opold if oi == 0 else []),
                        grp_end=(oi == len(order) - 1))
                    wv.did_r(mmf)
                    vb.did_r(mmf)
                opb.did_w(mmf)
                toe = A(lambda a, hp=hp, tc=tc, h=h, out_ps=out_ps: a.activation(
                    outg[:, hp, tc * 128:(tc + 1) * 128], out_ps[:, 0:128],
                    AF.Identity, bias=bvt[:, h:h + 1]),
                    wait=[mmf] + (B_("outg").w() if (hp == 0 and tc == 0) else []) +
                         B_("bvt").r())
                opb.did_r(toe)
                if hp == 0 and tc == 0:
                    B_("outg").wrd = {}
                    B_("outg").rdd = {}
                B_("outg").did_w(toe)

            plan_scores(0)
            plan_scores(1)
            plan_scores_b(0)
            plan_scores(2)

            if hp == 0:
                twv = DMA("gpsimd", "dwv", lambda s, h=h: s.dma_start(
                    out=wvb[:], in_=wsrc(("wv", h))), wait=wv.w())
                wv.wrd = {}
                wv.rdd = {}
                wv.did_w(twv)
                vold = vb.w()
                vb.wrd = {}
                vb.rdd = {}
                for lc in range(LCH):
                    xt = xnf_own if lc < 8 else xnf_sib
                    xo = (lc % 8) * 128
                    bk = 4 + lc % 2
                    ps, pb = psF[bk], B_("ps%d" % bk)
                    mm = None
                    for dc in range(NCH):
                        mm = T(lambda t_, dc=dc, xt=xt, xo=xo, ps=ps: t_.matmul(
                            ps[:, 0:256], xt[:, dc, xo:xo + 128], wvb[:, dc, :],
                            start=(dc == 0), stop=(dc == NCH - 1)),
                            wait=[twv] + B_(xtbuf(lc // 4)).r() +
                                 (pb.w() if dc == 0 else []),
                            grp_end=(dc == NCH - 1))
                    pb.wrd = {}
                    pb.rdd = {}
                    pb.did_w(mm)
                    B_(xtbuf(lc // 4)).did_r(mm)
                    wv.did_r(mm)
                    if lc % 2 == 0:
                        te = A(lambda a, lc=lc, ps=ps: a.activation(
                            vbuf[:, lc, :], ps[:, 0:256], AF.Copy),
                            wait=[mm] + (vold if lc == 0 else []))
                    else:
                        te = V(lambda v, lc=lc, ps=ps: v.tensor_copy(
                            vbuf[:, lc, :], ps[:, 0:256]),
                            wait=[mm] + (vold if lc == 0 else []))
                    pb.did_r(te)
                    vb.did_w(te)

            for tc in range(OWN):
                if tc + 1 < OWN:
                    plan_scores_b(tc + 1)
                if tc + 3 < OWN:
                    plan_scores(tc + 3)
                if nxt:
                    nxt.pop(0)()
                plan_transposes(tc)
                plan_av(tc)
            while nxt:
                nxt.pop(0)()

            if hp == 1:
                g2i = h // 2
                wps = [kbuf[:, 1, :], atT[:].rearrange("p c x -> p (c x)")]
                wpB = [B_("kbuf1"), wv]
                tws = []
                for hc in range(2):
                    wpo = wpB[hc].w()
                    wpB[hc].wrd = {}
                    wpB[hc].rdd = {}
                    tw = DMA("gpsimd", "dwp", lambda s, g2i=g2i, hc=hc: s.dma_start(
                        out=wps[hc], in_=wsrc(("wp", g2i * 2 + hc))),
                        wait=wpo)
                    wpB[hc].did_w(tw)
                    tws.append(tw)
                wptok = tws[1]
                for hc in range(2):
                    wpB[hc].wrd["dwp"] = pl.cnt["dwp"]
                for tcq in range(OWN):
                    mms = []
                    for hc in range(2):
                        for dt in range(4):
                            bk = (tcq % 2) * 4 + dt
                            ps, pb = psF[bk], B_("ps%d" % bk)
                            mm = T(lambda t_, hc=hc, tcq=tcq, dt=dt, ps=ps: t_.matmul(
                                ps[:], outg[:, hc, tcq * 128:(tcq + 1) * 128],
                                wps[hc][:, dt * 512:(dt + 1) * 512],
                                start=(hc == 0), stop=(hc == 1)),
                                wait=[wptok] + B_("outg").r() +
                                     (pb.w() if hc == 0 else []),
                                grp_end=(hc == 1))
                            if hc == 0:
                                pb.wrd = {}
                                pb.rdd = {}
                            if mm is not None:
                                pb.did_w(mm)
                                wpB[0].did_r(mm)
                                wpB[1].did_r(mm)
                                B_("outg").did_r(mm)
                                mms.append((dt, ps, pb, mm))
                    for dt, ps, pb, mm in mms:
                        tev = V(lambda v, tcq=tcq, dt=dt, ps=ps: v.tensor_tensor(
                            acc[:, tcq, dt * 512:(dt + 1) * 512],
                            acc[:, tcq, dt * 512:(dt + 1) * 512], ps[:], ALU.add),
                            wait=[mm] + B_("acc").w())
                        pb.did_r(tev)
                        B_("acc").did_w(tev)

        if dbg:
            for ct in range(OWN):
                t_ = DMA("scalar", "ds", lambda s, ct=ct: s.dma_start(
                    out=dbg_outs["dacc1"][ct], in_=acc[:, ct, :]), wait=B_("acc").r())
                B_("acc").did_r(t_)
            B_("acc").rdd["ds"] = pl.cnt["ds"]

    if "p2" in phases:
        bar = pl.barrier_toks()
        for nm in ("hbuf", "w1b0", "w1b1", "w2b0", "w2b1", "b2sB"):
            B_(nm).seed(bar)
        tb2 = DMA("sync", "dcc", lambda s: s.dma_start(out=b2s[:], in_=d_b2s),
                  wait=B_("b2sB").w() + B_("xnf_sib0").w() + B_("xnf_sib1").w())
        B_("b2sB").did_w(tb2)
        for tcq in range(OWN):
            for dt in range(4):
                ps, pb = psF[(tcq * 4 + dt) % 4], B_("ps%d" % ((tcq * 4 + dt) % 4))
                mm = T(lambda t_, tcq=tcq, dt=dt, ps=ps: t_.matmul(
                    ps[:], G2[0:19, tcq * 128:(tcq + 1) * 128],
                    b2s[0:19, dt * 512:(dt + 1) * 512], start=True, stop=True),
                    wait=[tb2] + B_("G2").r() + pb.w())
                pb.wrd = {}
                pb.rdd = {}
                pb.did_w(mm)
                B_("G2").did_r(mm)
                te = V(lambda v, tcq=tcq, dt=dt, ps=ps: v.tensor_tensor(
                    acc[:, tcq, dt * 512:(dt + 1) * 512],
                    acc[:, tcq, dt * 512:(dt + 1) * 512], ps[:], ALU.add),
                    wait=[mm] + B_("acc").w())
                pb.did_r(te)
                B_("acc").did_w(te)

        for e in range(NEXP):
            hb = B_("hbuf")
            hold = hb.w()
            hb.wrd = {}
            hb.rdd = {}
            first_h = True
            for ft, (f0, f1) in enumerate(F_TILES):
                wslot = ft % 2
                wb = B_("w1b%d" % wslot)
                tw = DMA("gpsimd", "dw1a" if wslot == 0 else "dw1b",
                         lambda s, e=e, ft=ft, wslot=wslot, w=f1 - f0:
                         s.dma_start(out=w1b[:, wslot, :, 0:w], in_=wsrc(("w1", e, ft))),
                         wait=wb.w())
                wb.wrd = {}
                wb.rdd = {}
                wb.did_w(tw)
                nfc = (f1 - f0) // 128
                for fci in range(nfc):
                    Fg = (f0 // 128) + fci
                    bk0, bk1 = (6, 7) if Fg % 2 else (0, 1)
                    ps0, ps1 = psF[bk0], psF[bk1]
                    pb0, pb1 = B_("ps%d" % bk0), B_("ps%d" % bk1)
                    mm0 = mm1 = None
                    for dc in range(NCH):
                        mm0 = T(lambda t_, dc=dc, fci=fci, wslot=wslot, ps=ps0: t_.matmul(
                            ps[:], w1b[:, wslot, dc, fci * 128:(fci + 1) * 128],
                            xnf_own[:, dc, 0:512],
                            start=(dc == 0), stop=(dc == NCH - 1)),
                            wait=[tw] + B_("xnf_own0").r() +
                                 (pb0.w() if dc == 0 else []),
                            grp_end=(dc == NCH - 1))
                        mm1 = T(lambda t_, dc=dc, fci=fci, wslot=wslot, ps=ps1: t_.matmul(
                            ps[:], w1b[:, wslot, dc, fci * 128:(fci + 1) * 128],
                            xnf_own[:, dc, 512:1024],
                            start=(dc == 0), stop=(dc == NCH - 1)),
                            wait=B_("xnf_own1").r() + (pb1.w() if dc == 0 else []),
                            grp_end=(dc == NCH - 1))
                    pb0.wrd = {}
                    pb0.rdd = {}
                    pb0.did_w(mm0)
                    pb1.wrd = {}
                    pb1.rdd = {}
                    pb1.did_w(mm1)
                    wb.did_r(mm1)
                    for tt, (mm, ps, pb) in enumerate(((mm0, ps0, pb0), (mm1, ps1, pb1))):
                        te = A(lambda a, Fg=Fg, tt=tt, e=e, ps=ps: a.activation(
                            hbuf[:, Fg, tt * 512:(tt + 1) * 512], ps[:], AF.Relu,
                            bias=b1t[:, e, Fg:Fg + 1]),
                            wait=[mm] + (hold if first_h else []) + B_("b1t").r())
                        first_h = False
                        pb.did_r(te)
                        hb.did_w(te)
            for dt in range(4):
                wslot = dt % 2
                wb = B_("w2b%d" % wslot)
                tw = DMA("gpsimd", "dw2a" if wslot == 0 else "dw2b",
                         lambda s, e=e, dt=dt, wslot=wslot: s.dma_start(
                    out=w2b[:, wslot, :, :], in_=wsrc(("w2", e, dt))), wait=wb.w())
                wb.wrd = {}
                wb.rdd = {}
                wb.did_w(tw)
                for tcq in range(OWN):
                    bk = 2 + (tcq % 4)
                    ps, pb = psF[bk], B_("ps%d" % bk)
                    mm = None
                    for fc in range(FCH):
                        mm = T(lambda t_, fc=fc, tcq=tcq, wslot=wslot, ps=ps: t_.matmul(
                            ps[:], hbuf[:, fc, tcq * 128:(tcq + 1) * 128],
                            w2b[:, wslot, fc, :], start=(fc == 0), stop=(fc == FCH - 1)),
                            wait=[tw] + hb.r() + (pb.w() if fc == 0 else []),
                            grp_end=(fc == FCH - 1))
                    pb.wrd = {}
                    pb.rdd = {}
                    pb.did_w(mm)
                    wb.did_r(mm)
                    hb.did_r(mm)
                    if e < 2:
                        te = V(lambda v, tcq=tcq, dt=dt, ps=ps: v.tensor_tensor(
                            acc[:, tcq, dt * 512:(dt + 1) * 512],
                            acc[:, tcq, dt * 512:(dt + 1) * 512], ps[:], ALU.add),
                            wait=[mm] + B_("acc").w())
                    else:
                        te = V(lambda v, tcq=tcq, dt=dt, e=e, ps=ps:
                               v.scalar_tensor_tensor(
                            acc[:, tcq, dt * 512:(dt + 1) * 512], ps[:],
                            gat[:, tcq, e - 2:e - 1],
                            acc[:, tcq, dt * 512:(dt + 1) * 512], ALU.mult, ALU.add),
                            wait=[mm] + B_("acc").w() + B_("gat").r())
                        B_("gat").did_r(te)
                    pb.did_r(te)
                    B_("acc").did_w(te)

    last = None
    for tcq in range(OWN):
        last = DMA("scalar", "ds", lambda s, tcq=tcq: s.dma_start(
            out=d_y[tcq], in_=acc[:, tcq, :]), wait=B_("acc").r())
        B_("acc").did_r(last)
    pl.on("scalar", None, [last])

    sstack = contextlib.ExitStack()
    semh = {s: sstack.enter_context(nc.semaphore(s)) for s in SEMS}

    with nc.Block() as blk:
        def emit(ops):
            def _f(eng):
                for ws, fn, inc, n in ops:
                    for s, v in ws:
                        eng.wait_ge(semh[s], v)
                    if fn is not None:
                        instr = fn(eng)
                        if inc is not None:
                            instr.then_inc(semh[inc], n)
            return _f

        blk.tensor(emit(pl.ops["tensor"]))
        blk.scalar(emit(pl.ops["scalar"]))
        blk.vector(emit(pl.ops["vector"]))
        blk.gpsimd(emit(pl.ops["gpsimd"]))
        blk.sync(emit(pl.ops["sync"]))

    sstack.close()
    pstack.close()
    return nc, dbg_outs


def make_inputs(inp):
    x = np.asarray(inp["L0_attn_x"], np.float32)
    wstream = pack_wstream(inp)
    bqt = np.ascontiguousarray(np.asarray(inp["L0_attn_bq"], np.float32).reshape(H, 128).T)
    bkt = np.ascontiguousarray(np.asarray(inp["L0_attn_bk"], np.float32).reshape(H, 128).T)
    bvt = np.ascontiguousarray(np.asarray(inp["L0_attn_bv"], np.float32).reshape(H, 128).T)
    b1_all = np.concatenate([np.asarray(inp["L0_ffn_shared_b1"], np.float32),
                             np.asarray(inp["L0_ffn_routed_b1"], np.float32)], 0)
    b1t = np.ascontiguousarray(b1_all.reshape(NEXP, FCH, 128).transpose(2, 0, 1))
    bgrep = np.ascontiguousarray(
        np.broadcast_to(np.asarray(inp["L0_ffn_bias"], np.float32)[None, :], (128, NRT)))
    b2s = np.ascontiguousarray(np.concatenate([
        np.asarray(inp["L0_ffn_routed_b2"], np.float32),
        np.asarray(inp["L0_attn_bp"], np.float32)[None, :],
        np.asarray(inp["L0_ffn_shared_b2"], np.float32)], 0))
    cen = np.ascontiguousarray(np.asarray(inp["L0_ffn_centroids"], np.float32))
    identf = np.eye(128, dtype=np.float32)
    jj = np.arange(128)
    diagm = np.where(jj[None, :] <= jj[:, None], 0.0, NEG).astype(np.float32)

    in_maps = []
    for c in range(8):
        b, hh = c // 2, c % 2
        xb = x[b]
        xl = xb if hh == 0 else np.concatenate([xb[1024:], xb[:1024]], 0)
        xl = np.ascontiguousarray(xl.reshape(LCH, 128, DM))
        msk = np.full((128, 32), NEG if hh == 0 else 0.0, dtype=np.float32)
        in_maps.append({
            "xin": xl, "wstream": wstream, "bqt": bqt, "bkt": bkt, "bvt": bvt,
            "b1t": b1t, "bgrep": bgrep, "b2s": b2s, "cen": cen,
            "identf": identf, "msk8": msk, "diagm": diagm,
        })
    return in_maps


def kernel(**inputs):
    nc, _ = build_nc(dbg=False)
    in_maps = make_inputs(inputs)
    res = run_bass_kernel_spmd(nc, in_maps, core_ids=list(range(8)))
    out = np.empty((B, S, DM), np.float32)
    for c in range(8):
        b, hh = c // 2, c % 2
        y = res.results[c]["y"].reshape(1024, DM)
        out[b, hh * 1024:(hh + 1) * 1024, :] = y
    return out
